# version 02
# baseline (speedup 1.0000x reference)
import contextlib
from fractions import Fraction

import numpy as np
import concourse.bass as bass
import concourse.mybir as mybir
from concourse.bass_utils import run_bass_kernel_spmd

F32 = mybir.dt.float32
BF16 = mybir.dt.bfloat16
ALU = mybir.AluOpType
ACT = mybir.ActivationFunctionType

D = 1024
S = 4096
H = 16
DH = 64
MJ = 11
NT1 = 8
T1 = 512
SUB = 128
NT3 = 16
T3 = 256
FD = 4096
EPS = 1e-5

ENGINES = ("tensor", "vector", "scalar", "gpsimd", "sync")


def _basis_matrices():
    def cmul(a, b):
        out = {}
        for i, ai in a.items():
            for j, bj in b.items():
                for k in (i + j, abs(i - j)):
                    out[k] = out.get(k, Fraction(0)) + ai * bj / 2
        return {k: v for k, v in out.items() if v != 0}

    def cadd(a, b, sa=Fraction(1), sb=Fraction(1)):
        out = {}
        for k, v in a.items():
            out[k] = out.get(k, Fraction(0)) + sa * v
        for k, v in b.items():
            out[k] = out.get(k, Fraction(0)) + sb * v
        return {k: v for k, v in out.items() if v != 0}

    T0, T1_ = {0: Fraction(1)}, {1: Fraction(1)}
    P = [None] * MJ
    P[0] = T0
    P[1] = cadd(T1_, {}, Fraction(2))
    P[2] = cmul(cadd(P[1], {}, Fraction(1, 2)), cadd(P[1], {}, Fraction(1, 2)))
    P[3] = cmul(cadd(P[2], T0, Fraction(1), Fraction(-3, 4)), P[1])
    P[4] = cmul(cadd(P[2], T0, Fraction(1), Fraction(-1, 2)),
                cadd(P[2], T0, Fraction(1), Fraction(-1, 2)))
    P[5] = cmul(P[2], P[3])
    P[6] = cmul(P[3], P[3])
    P[7] = cmul(P[3], P[4])
    P[8] = cmul(P[4], P[4])
    P[9] = cmul(P[4], P[5])
    P[10] = cmul(P[5], P[5])
    Mq = [[P[j].get(m, Fraction(0)) for m in range(MJ)] for j in range(MJ)]
    n = MJ
    Aug = [row[:] + [Fraction(int(i == j)) for j in range(n)]
           for i, row in enumerate(Mq)]
    for c in range(n):
        p = next(r for r in range(c, n) if Aug[r][c] != 0)
        Aug[c], Aug[p] = Aug[p], Aug[c]
        piv = Aug[c][c]
        Aug[c] = [v / piv for v in Aug[c]]
        for r in range(n):
            if r != c and Aug[r][c] != 0:
                f = Aug[r][c]
                Aug[r] = [vr - f * vc for vr, vc in zip(Aug[r], Aug[c])]
    Minv = [row[n:] for row in Aug]
    E_A = np.array([[float(Minv[j][m]) for m in range(MJ)] for j in range(MJ)],
                   dtype=np.float64)
    return E_A, E_A.T.copy()


E_A64, E_B64 = _basis_matrices()


class Prog:

    def __init__(self, nc, stk):
        self.nc = nc
        self.steps = {e: [] for e in ENGINES}
        self._sems = {}
        self._stk = stk
        self.counts = {}

    def sem(self, name):
        if name not in self._sems:
            self._sems[name] = self._stk.enter_context(self.nc.semaphore(name))
            self.counts[name] = 0
        return self._sems[name]

    def step(self, eng, fn, waits=(), inc=None, inc_k=1):
        w = []
        seen = {}
        for (s, v) in waits:
            if v is None:
                continue
            self.sem(s)
            if s not in seen or seen[s] < v:
                seen[s] = v
        w = [(self._sems[s], v) for s, v in seen.items()]
        if inc is not None:
            self.sem(inc)
            self.counts[inc] += inc_k
        self.steps[eng].append((fn, w, (self._sems[inc] if inc else None), inc_k))
        return self.counts[inc] if inc else None

    def now(self, sem_name):
        self.sem(sem_name)
        return self.counts[sem_name]

    def emit(self, block):
        for ename in ENGINES:
            steps = self.steps[ename]
            if not steps:
                continue

            def body(eng, steps=steps):
                for fn, waits, inc_sem, inc_k in steps:
                    for (sem, val) in waits:
                        eng.wait_ge(sem, val)
                    ins = fn(eng)
                    if inc_sem is not None:
                        assert ins is not None
                        ins.then_inc(inc_sem, inc_k)
            getattr(block, ename)(body)


def ceil32(x):
    return (x + 31) & ~31


def build_nc():
    nc = bass.Bass("TRN2", target_bir_lowering=False,
                   detect_race_conditions=False)

    xT = nc.dram_tensor("xT", [D, S], F32, kind="ExternalInput")
    w_inT = nc.dram_tensor("w_inT", [D, 3 * D], F32, kind="ExternalInput")
    w_outT = nc.dram_tensor("w_outT", [D, D], F32, kind="ExternalInput")
    betaT = nc.dram_tensor("betaT", [MJ, H], F32, kind="ExternalInput")
    W_fcT = nc.dram_tensor("W_fcT", [D, FD], F32, kind="ExternalInput")
    W_projT = nc.dram_tensor("W_projT", [FD, D], F32, kind="ExternalInput")
    outT = nc.dram_tensor("outT", [D, S], F32, kind="ExternalOutput")
    y1d = nc.dram_tensor("y1scratch", [D, S], F32, kind="Internal")

    EA32 = E_A64.astype(np.float32)
    assert np.array_equal(EA32.astype(np.float64), E_A64), "E_A not fp32 exact"

    with contextlib.ExitStack() as stk:
        P = Prog(nc, stk)
        BASE = 16512
        LIMIT = 212832
        assert BASE >= nc.sbuf_base and BASE + LIMIT <= nc.sbuf_top
        placed = []

        def at(name, off, shape, dtype):
            nbytes = int(np.prod(shape[1:])) * mybir.dt.size(dtype)
            assert off + nbytes <= LIMIT, f"{name}: {off}+{nbytes} > {LIMIT}"
            placed.append((name, off, off + nbytes))
            return nc.alloc_sbuf_tensor_at(name, list(shape), dtype,
                                           offset=BASE + off)

        w_inkv_bf = at("w_inkv_bf", 0, [128, 8, 2 * D], BF16)
        w_inq_bf = at("w_inq_bf", 0, [128, 8, D], BF16)
        w_outT_bf = at("w_outT_bf", 16384, [128, 8, D], BF16)
        W_fcT_bf = at("W_fcT_bf", 0, [128, 8, FD], BF16)
        o = 201472
        beta_exp = at("beta_exp", o, [128, 6 * H], F32); o += ceil32(6 * H * 4)
        iden_f32 = at("iden_f32", o, [128, 128], F32); o += 512
        iota_f32 = at("iota_f32", o, [128, 128], F32); o += 512
        ones_lhs = at("ones_lhs", o, [128, 128], BF16); o += 256
        ones_rhs = at("ones_rhs", o, [128, 512], BF16); o += 1024
        ones_row = at("ones_row", o, [1, 64], F32); o += 256
        beta_row = at("beta_row", o, [1, MJ * H], F32); o += ceil32(MJ * H * 4)
        wsum_in = at("wsum_in", o, [1, 3 * D], BF16); o += ceil32(3 * D * 2)
        negmu_row = at("negmu_row", o, [1, T1], BF16); o += ceil32(T1 * 2)
        statc = at("statc", o, [128, 8, 4], F32); o += 4 * 8 * 4
        cpool = at("cpool", o, [128, 2], F32); o += 32
        C_END = o
        assert C_END <= LIMIT, f"C region: {C_END}"
        C0 = 201472

        E0 = 32768
        p = E0
        x_roll = at("x_roll", p, [128, 8, SUB], F32); p += 8 * SUB * 4
        x_bf = at("x_bf", p, [128, 8, T1], BF16); p += 8 * T1 * 2
        x2_bf = at("x2_bf", p, [128, 8, T1], BF16); p += 8 * T1 * 2
        k2_t = at("k2_t", p, [128, 2, D], F32); p += 2 * D * 4
        f_p2 = at("f_p2", p, [128, D], F32); p += D * 4
        f_p3 = at("f_p3", p, [128, D], F32); p += D * 4
        f_p4 = at("f_p4", p, [128, D], F32); p += D * 4
        f_p5 = at("f_p5", p, [128, D], F32); p += D * 4
        v_bf = at("v_bf", p, [128, 4, D], BF16); p += 4 * D * 2
        tk_bf = at("tk_bf", p, [128, 4, H, 10, DH], BF16); p += 4 * H * 10 * DH * 2
        kvt_acc = at("kvt_acc", p, [128, 8, 10 * DH], F32); p += 8 * 10 * DH * 4
        vsum_acc = at("vsum_acc", p, [128, H * DH], F32); p += H * DH * 4
        P1_END = p
        assert P1_END <= C0, f"P1 over: {P1_END}"

        p = E0
        kvp_f = at("kvp_f", p, [128, 6, H, DH], F32); p += 6 * H * DH * 4
        t1_f = at("t1_f", p, [128, 6, H, DH], F32); p += 6 * H * DH * 4
        et32 = at("et32", p, [128, 6, 128], F32); p += 6 * 128 * 4
        TR_END = p
        tk_off = [x for x in placed if x[0] == "tk_bf"][0][1]
        assert TR_END <= tk_off, "TR scratch hits tk"

        p = E0
        x2_t = at("x2_t", p, [128, 2, 8, T1], F32); p += 2 * 8 * T1 * 4
        xb2 = at("xb2", p, [128, 8, T1], BF16); p += 8 * T1 * 2
        xsq2 = at("xsq2", p, [128, 8, T1], BF16); p += 8 * T1 * 2
        q2h = at("q2h", p, [128, T1], F32); p += T1 * 4
        g_p2 = at("g_p2", p, [128, T1], F32); p += T1 * 4
        g_p3 = at("g_p3", p, [128, T1], F32); p += T1 * 4
        g_p4 = at("g_p4", p, [128, T1], F32); p += T1 * 4
        g_p5 = at("g_p5", p, [128, T1], F32); p += T1 * 4
        pf_bf = at("pf_bf", p, [128, 10, T1], BF16); p += 10 * T1 * 2
        attn_bf = at("attn_bf", p, [128, 8, T1], BF16); p += 8 * T1 * 2
        y1_t = at("y1_t", p, [128, 8, T1], F32); p += 8 * T1 * 4
        rstd_rep = at("rstd_rep", p, [128, T1], F32); p += T1 * 4
        mu_rep = at("mu_rep", p, [128, T1], F32); p += T1 * 4
        msq_rep = at("msq_rep", p, [128, T1], F32); p += T1 * 4
        kvp_bf = at("kvp_bf", p, [128, MJ, 8, DH], BF16); p += MJ * 8 * DH * 2
        P2_END = p
        assert P2_END <= C0, f"P2 over: {P2_END}"
        kvpbf_off = [x for x in placed if x[0] == "kvp_bf"][0][1]
        assert kvpbf_off >= TR_END, "kvp_bf overlaps TR scratch"

        p = 65536
        W_projT_bf = at("W_projT_bf", p, [128, 32, D], BF16); p += 32 * D * 2
        y1_in = at("y1_in", p, [128, 2, 8, T3], F32); p += 2 * 8 * T3 * 4
        yb3 = at("yb3", p, [128, 8, T3], BF16); p += 8 * T3 * 2
        h_sb = at("h_sb", p, [128, 16, T3], BF16)
        ysq3 = at("ysq3", p, [128, 8, T3], BF16); p += 16 * T3 * 2
        y2_sb = at("y2_sb", p, [128, 8, T3], F32); p += 8 * T3 * 4
        m_sb = at("m_sb", p, [128, 8, T3], F32); p += 8 * T3 * 4
        spk_sb = at("spk_sb", p, [128, 8, T3], F32); p += 8 * T3 * 4
        a_sb = at("a_sb", p, [128, 2, T3], F32); p += 2 * T3 * 4
        rsq_rep = at("rsq_rep", p, [128, T3], F32); p += T3 * 4
        mu3_rep = at("mu3_rep", p, [128, T3], F32); p += T3 * 4
        msq3_rep = at("msq3_rep", p, [128, T3], F32); p += T3 * 4
        nmu3_row = at("nmu3_row", p, [1, T3], BF16); p += ceil32(T3 * 2)
        wsum_fc = at("wsum_fc", p, [1, FD], BF16); p += ceil32(FD * 2)
        P3_END = p
        assert P3_END <= C0, f"P3 over: {P3_END}"

        PSB = [nc.alloc_psum_tensor(f"psb{i}", [128, 512], F32) for i in range(8)]

        SD = "sdma"
        GP = ["gdma0", "gdma1", "gdma2", "gdma3"]
        PT, PV, PS, PG = "pt", "pv", "ps", "pg"
        gp_i = [0]
        gp_pending = {}
        bank_guard = {}

        def gdma(fn, waits=()):
            s = GP[gp_i[0] % len(GP)]
            gp_i[0] += 1
            w = list(waits)
            if s in gp_pending:
                w.append((s, gp_pending[s]))
            P.step("gpsimd", fn, waits=w, inc=s, inc_k=16)
            gp_pending[s] = P.now(s)
            return (s, P.now(s))

        def bank_w(*banks):
            return [bank_guard[b] for b in banks if b in bank_guard]

        def bank_r(b, sem_name):
            bank_guard[b] = (sem_name, P.now(sem_name))

        w_kv_done = gdma(lambda e: e.dma_start(
            out=w_inkv_bf[:, :, :],
            in_=w_inT.ap()[:, D:3 * D].rearrange("(c p) f -> p c f", p=128)))
        beta_done = gdma(lambda e: e.dma_start(
            out=beta_row[0:1, :],
            in_=betaT.ap().rearrange("m h -> (m h)")))

        P.step("gpsimd", lambda e: e.iota(
            iota_f32[:, :], [[1, 128]], channel_multiplier=-1,
            allow_small_or_imprecise_dtypes=True), inc=PG)
        g_iota = P.now(PG)
        P.step("vector", lambda e: e.tensor_single_scalar(
            iden_f32[:, :], iota_f32[:, :], 0.0, ALU.is_equal),
            waits=[(PG, g_iota)], inc=PV)
        P.step("vector", lambda e: e.memset(ones_lhs[:, :], 1.0), inc=PV)
        P.step("vector", lambda e: e.memset(ones_rhs[:, :], 1.0), inc=PV)
        P.step("vector", lambda e: e.memset(ones_row[0:1, :], 1.0), inc=PV)
        P.step("vector", lambda e: e.memset(cpool[:, 0:1], float(EPS)), inc=PV)
        P.step("vector", lambda e: e.memset(cpool[:, 1:2], -0.5), inc=PV)
        nc.const_aps.register(F32, float(EPS), cpool.ap()[:, 0:1])
        nc.const_aps.register(F32, -0.5, cpool.ap()[:, 1:2])
        consts_v = P.now(PV)

        for c in range(6):
            for half in range(2):
                m = 2 * c + half
                dst = (c * H, (c + 1) * H)
                if m >= MJ:
                    P.step("vector", lambda e, half=half, dst=dst: e.memset(
                        beta_exp[half * 64:(half + 1) * 64, dst[0]:dst[1]], 0.0),
                        inc=PV)
                    continue
                P.step("tensor", lambda e, m=m: e.matmul(
                    PSB[0][0:64, 0:H], ones_row[0:1, :],
                    beta_row.ap()[0:1, m * H:(m + 1) * H],
                    start=True, stop=True),
                    waits=[(PV, consts_v), (beta_done[0], beta_done[1])]
                    + bank_w(0), inc=PT)
                tb = P.now(PT)
                P.step("vector", lambda e, half=half, dst=dst: e.tensor_copy(
                    beta_exp[half * 64:(half + 1) * 64, dst[0]:dst[1]],
                    PSB[0][0:64, 0:H]), waits=[(PT, tb)], inc=PV)
                bank_r(0, PV)
        bexp_v = P.now(PV)

        def wsum_kv(e):
            ins = None
            for ci in range(8):
                for nb in range(4):
                    ins = e.matmul(PSB[nb][:, :], ones_lhs[:, :],
                                   w_inkv_bf[:, ci, nb * 512:(nb + 1) * 512],
                                   start=(ci == 0), stop=(ci == 7))
            return ins
        P.step("tensor", wsum_kv,
               waits=[(w_kv_done[0], w_kv_done[1]), (PV, consts_v)]
               + bank_w(0, 1, 2, 3), inc=PT)
        t_wkv = P.now(PT)
        for nb in range(4):
            P.step("vector", lambda e, nb=nb: e.tensor_copy(
                wsum_in.ap()[0:1, D + nb * 512:D + (nb + 1) * 512],
                PSB[nb][0:1, :]), waits=[(PT, t_wkv)], inc=PV)
            bank_r(nb, PV)
        v_wkv = P.now(PV)

        x_guard = None
        xbf_guard = None
        feat_guard = None
        tkv_guard = None

        for t in range(NT1):
            cast_marks = []
            for u in range(4):
                waits = []
                if x_guard is not None:
                    waits.append(x_guard)
                c0 = t * T1 + u * SUB
                P.step("sync", lambda e, c0=c0: e.dma_start(
                    out=x_roll[:, :, :],
                    in_=xT.ap()[:, c0:c0 + SUB].rearrange("(c p) s -> p c s", p=128)),
                    waits=waits, inc=SD, inc_k=16)
                dv = P.now(SD)
                w2 = [(SD, dv)]
                if xbf_guard is not None:
                    w2.append(xbf_guard)
                P.step("scalar", lambda e, u=u: e.activation(
                    x_bf.ap()[:, :, u * SUB:(u + 1) * SUB],
                    x_roll[:, :, :], ACT.Copy), waits=w2, inc=PS)
                P.step("scalar", lambda e, u=u: e.activation(
                    x2_bf.ap()[:, :, u * SUB:(u + 1) * SUB],
                    x_roll[:, :, :], ACT.Square), inc=PS)
                x_guard = (PS, P.now(PS))
                cast_marks.append(P.now(PS))
            casts_done = cast_marks[-1]

            def fsum(e):
                ins = None
                for ci in range(8):
                    ins = e.matmul(PSB[4][:, :], ones_lhs[:, :], x_bf[:, ci, :],
                                   start=(ci == 0), stop=(ci == 7))
                return ins
            P.step("tensor", fsum, waits=[(PS, casts_done)] + bank_w(4), inc=PT)
            t_sum = P.now(PT)
            P.step("vector", lambda e: e.tensor_scalar(
                negmu_row[0:1, :], PSB[4][0:1, :], -1.0 / 1024.0, None, ALU.mult),
                waits=[(PT, t_sum)], inc=PV)
            bank_r(4, PV)
            v_nmu = P.now(PV)

            for u in range(4):
                def tsum(e, u=u):
                    ins = None
                    for ci in range(8):
                        ins = e.matmul(PSB[5][:, 0:1],
                                       x_bf[:, ci, u * SUB:(u + 1) * SUB],
                                       ones_rhs[:, 0:1],
                                       start=(ci == 0), stop=(ci == 7))
                    return ins
                P.step("tensor", tsum,
                       waits=[(PS, cast_marks[u])] + bank_w(5), inc=PT)
                tv = P.now(PT)
                P.step("vector", lambda e, u=u: e.tensor_scalar(
                    statc[:, 0, u:u + 1], PSB[5][:, 0:1], 1.0 / 1024.0, None,
                    ALU.mult), waits=[(PT, tv)], inc=PV)
                bank_r(5, PV)
                def tsq(e, u=u):
                    ins = None
                    for ci in range(8):
                        ins = e.matmul(PSB[5][:, 0:1],
                                       x2_bf[:, ci, u * SUB:(u + 1) * SUB],
                                       ones_rhs[:, 0:1],
                                       start=(ci == 0), stop=(ci == 7))
                    return ins
                P.step("tensor", tsq, waits=bank_w(5), inc=PT)
                tv2 = P.now(PT)
                P.step("vector", lambda e, u=u: e.tensor_scalar(
                    statc[:, 1, u:u + 1], PSB[5][:, 0:1], 1.0 / 1024.0, None,
                    ALU.mult), waits=[(PT, tv2)], inc=PV)
                bank_r(5, PV)
            P.step("vector", lambda e: e.scalar_tensor_tensor(
                statc[:, 2, :], statc[:, 0, :], -1.0, statc[:, 0, :],
                ALU.mult, ALU.mult), waits=[(PV, P.now(PV))], inc=PV)
            P.step("vector", lambda e: e.tensor_add(
                statc[:, 2, :], statc[:, 2, :], statc[:, 1, :]),
                waits=[(PV, P.now(PV))], inc=PV)
            P.step("vector", lambda e: e.tensor_scalar(
                statc[:, 6, :], statc[:, 2, :], float(EPS), None, ALU.add),
                waits=[(PV, P.now(PV))], inc=PV)
            vva = P.now(PV)
            P.step("scalar", lambda e: e.activation(
                statc[:, 3, :], statc[:, 6, :], ACT.Sqrt),
                waits=[(PV, vva)], inc=PS)
            sva = P.now(PS)
            def recip_d(e):
                e.reciprocal(statc[:, 4, :], statc[:, 3, :])
                return e.drain()
            P.step("vector", recip_d, waits=[(PS, sva)], inc=PV)
            P.step("vector", lambda e: e.tensor_mul(
                statc[:, 7, :], statc[:, 4, :], statc[:, 4, :]),
                waits=[(PV, P.now(PV))], inc=PV)
            P.step("vector", lambda e: e.tensor_mul(
                statc[:, 7, :], statc[:, 7, :], statc[:, 6, :]),
                waits=[(PV, P.now(PV))], inc=PV)
            P.step("vector", lambda e: e.tensor_scalar(
                statc[:, 7, :], statc[:, 7, :], -0.5, 1.5, ALU.mult,
                ALU.add), waits=[(PV, P.now(PV))], inc=PV)
            P.step("vector", lambda e: e.tensor_mul(
                statc[:, 4, :], statc[:, 4, :], statc[:, 7, :]),
                waits=[(PV, P.now(PV))], inc=PV)
            P.step("vector", lambda e: e.tensor_scalar(
                statc[:, 5, :], statc[:, 4, :], 0.25, None, ALU.mult),
                waits=[(PV, P.now(PV))], inc=PV)
            rstd_all = P.now(PV)
            rstd_marks = [rstd_all] * 4

            featm = []
            for u in range(4):
                waits = ([(PS, casts_done), (PV, rstd_marks[u]), (PV, v_wkv),
                          (PV, v_nmu)] + bank_w(0, 1, 2, 3))
                def kvproj(e, u=u):
                    ins = None
                    for ci in range(8):
                        for ob in range(4):
                            ins = e.matmul(
                                PSB[ob][:, :],
                                x_bf[:, ci, u * SUB:(u + 1) * SUB],
                                w_inkv_bf[:, ci, ob * 512:(ob + 1) * 512],
                                start=(ci == 0), stop=False)
                    for ob in range(4):
                        ins = e.matmul(
                            PSB[ob][:, :],
                            negmu_row.ap()[0:1, u * SUB:(u + 1) * SUB],
                            wsum_in.ap()[0:1, D + ob * 512:D + (ob + 1) * 512],
                            start=False, stop=True)
                    return ins
                P.step("tensor", kvproj, waits=waits, inc=PT)
                t_proj = P.now(PT)

                ew = [(PT, t_proj)]
                if u >= 2:
                    ew += [(PV, featm[u - 2][0]), (PS, featm[u - 2][1]),
                           (PG, featm[u - 2][2])]
                if tkv_guard is not None:
                    ew.append(tkv_guard)
                for ob in range(2):
                    P.step("vector", lambda e, ob=ob, u=u: e.tensor_scalar(
                        k2_t[:, u % 2, ob * 512:(ob + 1) * 512], PSB[ob][:, :],
                        statc.ap()[:, 5, u:u + 1], None, ALU.mult),
                        waits=ew, inc=PV)
                    P.step("vector", lambda e, ob=ob, u=u: e.tensor_scalar(
                        k2_t[:, u % 2, ob * 512:(ob + 1) * 512],
                        k2_t[:, u % 2, ob * 512:(ob + 1) * 512],
                        2.0, -2.0, ALU.min, ALU.max), inc=PV)
                for ob in range(2, 4):
                    P.step("vector", lambda e, ob=ob, u=u: e.tensor_scalar(
                        v_bf[:, u, (ob - 2) * 512:(ob - 1) * 512],
                        PSB[ob][:, :], statc.ap()[:, 4, u:u + 1], None, ALU.mult),
                        waits=ew, inc=PV)
                evac_v = P.now(PV)
                for b in range(4):
                    bank_r(b, PV)

                k2s = k2_t[:, u % 2, :]
                tkv = tk_bf.ap()[:, u, :, :, :]
                k2r = k2s.rearrange("p (h d) -> p h d", h=H)
                f2r = f_p2.ap().rearrange("p (h d) -> p h d", h=H)
                f3r = f_p3.ap().rearrange("p (h d) -> p h d", h=H)
                f4r = f_p4.ap().rearrange("p (h d) -> p h d", h=H)
                f5r = f_p5.ap().rearrange("p (h d) -> p h d", h=H)
                fw = [(PV, evac_v)]
                if u >= 1:
                    fw += [(PV, featm[u - 1][0]), (PS, featm[u - 1][1]),
                           (PG, featm[u - 1][2])]
                elif feat_guard is not None:
                    fw += list(feat_guard)
                P.step("scalar", lambda e, k2s=k2s: e.activation(
                    f_p2[:, :], k2s, ACT.Square, scale=0.5), waits=fw, inc=PS)
                s2 = P.now(PS)
                P.step("vector", lambda e, k2s=k2s: e.scalar_tensor_tensor(
                    f_p3[:, :], f_p2[:, :], -0.75, k2s, ALU.add, ALU.mult),
                    waits=[(PS, s2)] + fw, inc=PV)
                v3 = P.now(PV)
                P.step("scalar", lambda e: e.activation(
                    f_p4[:, :], f_p2[:, :], ACT.Square, bias=-0.5),
                    waits=[(PS, s2)] + fw, inc=PS)
                s4 = P.now(PS)
                P.step("vector", lambda e: e.tensor_mul(
                    f_p5[:, :], f_p2[:, :], f_p3[:, :]),
                    waits=[(PV, v3)] + fw, inc=PV)
                v5 = P.now(PV)

                tw = [] if tkv_guard is None else [tkv_guard]
                P.step("gpsimd", lambda e, tkv=tkv, k2r=k2r: e.tensor_copy(
                    tkv[:, :, 0, :], k2r), waits=[(PV, evac_v)] + tw, inc=PG)
                P.step("gpsimd", lambda e, tkv=tkv, f2r=f2r: e.tensor_copy(
                    tkv[:, :, 1, :], f2r), waits=[(PS, s2)] + tw, inc=PG)
                P.step("gpsimd", lambda e, tkv=tkv, f3r=f3r: e.tensor_copy(
                    tkv[:, :, 2, :], f3r), waits=[(PV, v3)] + tw, inc=PG)
                P.step("vector", lambda e, tkv=tkv, f4r=f4r: e.tensor_copy(
                    tkv[:, :, 3, :], f4r), waits=[(PS, s4)] + tw, inc=PV)
                P.step("vector", lambda e, tkv=tkv, f5r=f5r: e.tensor_copy(
                    tkv[:, :, 4, :], f5r), waits=[(PV, v5)] + tw, inc=PV)
                P.step("scalar", lambda e, tkv=tkv, f3r=f3r: e.activation(
                    tkv[:, :, 5, :], f3r, ACT.Square),
                    waits=[(PV, v3)] + tw, inc=PS)
                P.step("scalar", lambda e, tkv=tkv, f4r=f4r: e.activation(
                    tkv[:, :, 7, :], f4r, ACT.Square),
                    waits=[(PS, s4)] + tw, inc=PS)
                P.step("scalar", lambda e, tkv=tkv, f5r=f5r: e.activation(
                    tkv[:, :, 9, :], f5r, ACT.Square),
                    waits=[(PV, v5)] + tw, inc=PS)
                P.step("vector", lambda e, tkv=tkv, f3r=f3r, f4r=f4r:
                       e.tensor_tensor(tkv[:, :, 6, :], f3r, f4r, ALU.mult),
                       waits=[(PS, s4)] + tw, inc=PV)
                P.step("vector", lambda e, tkv=tkv, f4r=f4r, f5r=f5r:
                       e.tensor_tensor(tkv[:, :, 8, :], f4r, f5r, ALU.mult),
                       inc=PV)
                featm.append((P.now(PV), P.now(PS), P.now(PG)))
                if u == 3:
                    feat_guard = [(PV, P.now(PV)), (PS, P.now(PS)),
                                  (PG, P.now(PG))]

            def vsmm(e):
                ins = None
                for u in range(4):
                    for nb in range(2):
                        ins = e.matmul(PSB[6 + nb][:, :], ones_lhs[:, :],
                                       v_bf[:, u, nb * 512:(nb + 1) * 512],
                                       start=(u == 0), stop=(u == 3))
                return ins
            P.step("tensor", vsmm,
                   waits=[(PV, featm[3][0])] + bank_w(6, 7), inc=PT)
            t_vs = P.now(PT)
            for nb in range(2):
                if t == 0:
                    fnv = (lambda e, nb=nb: e.tensor_copy(
                        vsum_acc[:, nb * 512:(nb + 1) * 512], PSB[6 + nb][:, :]))
                else:
                    fnv = (lambda e, nb=nb: e.tensor_tensor(
                        vsum_acc[:, nb * 512:(nb + 1) * 512], PSB[6 + nb][:, :],
                        vsum_acc[:, nb * 512:(nb + 1) * 512], ALU.add))
                P.step("vector", fnv, waits=[(PT, t_vs)], inc=PV)
            bank_r(6, PV)
            bank_r(7, PV)

            fw = [(PV, featm[3][0]), (PS, featm[3][1]), (PG, featm[3][2])]
            for h in range(H):
                b0 = 2 * (h % 3)
                b1 = b0 + 1
                half = h % 2
                row0 = half * 64
                def kvmm(e, h=h, b0=b0, b1=b1, row0=row0):
                    ins = None
                    for u in range(4):
                        lhs = v_bf[:, u, h * DH:(h + 1) * DH]
                        rhs = tk_bf.ap()[:, u, h, :, :].rearrange(
                            "p j d -> p (j d)")
                        ins = e.matmul(PSB[b0][row0:row0 + 64, :], lhs,
                                       rhs[:, 0:512], start=(u == 0),
                                       stop=(u == 3), tile_position=(0, row0))
                        ins = e.matmul(PSB[b1][row0:row0 + 64, 0:128], lhs,
                                       rhs[:, 512:640], start=(u == 0),
                                       stop=(u == 3), tile_position=(0, row0))
                    return ins
                P.step("tensor", kvmm, waits=fw + bank_w(b0, b1), inc=PT)
                t_kv = P.now(PT)
                accs = kvt_acc.ap()[row0:row0 + 64, h // 2, :]
                if t == 0:
                    f1 = (lambda e, b0=b0, row0=row0, accs=accs: e.tensor_copy(
                        accs[:, 0:512], PSB[b0][row0:row0 + 64, :]))
                    f2 = (lambda e, b1=b1, row0=row0, accs=accs: e.tensor_copy(
                        accs[:, 512:640], PSB[b1][row0:row0 + 64, 0:128]))
                else:
                    f1 = (lambda e, b0=b0, row0=row0, accs=accs: e.tensor_tensor(
                        accs[:, 0:512], PSB[b0][row0:row0 + 64, :],
                        accs[:, 0:512], ALU.add))
                    f2 = (lambda e, b1=b1, row0=row0, accs=accs: e.tensor_tensor(
                        accs[:, 512:640], PSB[b1][row0:row0 + 64, 0:128],
                        accs[:, 512:640], ALU.add))
                P.step("vector", f1, waits=[(PT, t_kv)], inc=PV)
                P.step("vector", f2, inc=PV)
                bank_r(b0, PV)
                bank_r(b1, PV)
            tkv_guard = (PT, P.now(PT))
            xbf_guard = (PT, P.now(PT))

        p1_t, p1_v, p1_s, p1_g = P.now(PT), P.now(PV), P.now(PS), P.now(PG)

        w_q_done = gdma(lambda e: e.dma_start(
            out=w_inq_bf[:, :, :],
            in_=w_inT.ap()[:, 0:D].rearrange("(c p) f -> p c f", p=128)),
            waits=[(PT, p1_t)])
        w_o_done = gdma(lambda e: e.dma_start(
            out=w_outT_bf[:, :, :],
            in_=w_outT.ap().rearrange("(c p) f -> p c f", p=128)),
            waits=[(PT, p1_t)])

        def wsum_q(e):
            ins = None
            for ci in range(8):
                for nb in range(2):
                    ins = e.matmul(PSB[nb][:, :], ones_lhs[:, :],
                                   w_inq_bf[:, ci, nb * 512:(nb + 1) * 512],
                                   start=(ci == 0), stop=(ci == 7))
            return ins
        P.step("tensor", wsum_q,
               waits=[(w_q_done[0], w_q_done[1])] + bank_w(0, 1), inc=PT)
        t_wq = P.now(PT)
        for nb in range(2):
            P.step("vector", lambda e, nb=nb: e.tensor_copy(
                wsum_in.ap()[0:1, nb * 512:(nb + 1) * 512], PSB[nb][0:1, :]),
                waits=[(PT, t_wq)], inc=PV)
            bank_r(nb, PV)
        v_wq = P.now(PV)

        P.step("vector", lambda e: e.memset(kvp_f[64:128, 5, :, :], 0.0),
               waits=[(PV, p1_v)], inc=PV)
        tr_b = 2
        for h in range(H):
            half = h % 2
            row0 = half * 64
            idn = iden_f32[row0:row0 + 64, row0:row0 + 64]
            P.step("vector", lambda e, h=h: e.tensor_copy(
                kvp_f[0:64, 0, h, :], vsum_acc.ap()[0:64, h * DH:(h + 1) * DH]),
                waits=[(PV, p1_v)], inc=PV)
            def tr0(e, h=h, row0=row0, idn=idn):
                return e.transpose(
                    PSB[tr_b][0:64, 0:64],
                    kvt_acc.ap()[row0:row0 + 64, h // 2, 0:64], idn)
            P.step("tensor", tr0, waits=[(PV, p1_v)] + bank_w(tr_b), inc=PT)
            ttr = P.now(PT)
            P.step("vector", lambda e, h=h: e.tensor_copy(
                kvp_f[64:128, 0, h, :], PSB[tr_b][0:64, 0:64]),
                waits=[(PT, ttr)], inc=PV)
            bank_r(tr_b, PV)
            for c in range(1, 6):
                jm0 = 2 * c - 1
                if c < 5:
                    def trc(e, h=h, row0=row0, jm0=jm0, idn=idn):
                        return e.transpose(
                            PSB[tr_b][:, 0:64],
                            kvt_acc.ap()[row0:row0 + 64, h // 2,
                                         jm0 * 64:(jm0 + 2) * 64], idn)
                    P.step("tensor", trc, waits=bank_w(tr_b), inc=PT)
                    ttr = P.now(PT)
                    P.step("vector", lambda e, h=h, c=c: e.tensor_copy(
                        kvp_f[:, c, h, :], PSB[tr_b][:, 0:64]),
                        waits=[(PT, ttr)], inc=PV)
                else:
                    def trc(e, h=h, row0=row0, jm0=jm0, idn=idn):
                        return e.transpose(
                            PSB[tr_b][0:64, 0:64],
                            kvt_acc.ap()[row0:row0 + 64, h // 2,
                                         jm0 * 64:(jm0 + 1) * 64], idn)
                    P.step("tensor", trc, waits=bank_w(tr_b), inc=PT)
                    ttr = P.now(PT)
                    P.step("vector", lambda e, h=h, c=c: e.tensor_copy(
                        kvp_f[0:64, c, h, :], PSB[tr_b][0:64, 0:64]),
                        waits=[(PT, ttr)], inc=PV)
                bank_r(tr_b, PV)
        v_kvp = P.now(PV)

        def emit_stage(EM, src_ap, write_evac, base_wait):
            for ci in range(6):
                cjs = [cj for cj in range(6)
                       if any(EM[2 * ci + a][2 * cj + b] != 0
                              for a in range(2) for b in range(2)
                              if 2 * ci + a < MJ and 2 * cj + b < MJ)]
                def btile(e, ci=ci, cjs=tuple(cjs)):
                    ins = e.memset(et32[:, :, :], 0.0)
                    e.drain()
                    for k, cj in enumerate(cjs):
                        for a in range(2):
                            for b in range(2):
                                mi, mj = 2 * ci + a, 2 * cj + b
                                if mi >= MJ or mj >= MJ:
                                    continue
                                val = float(EM[mi][mj])
                                if val == 0.0:
                                    continue
                                ins = e.tensor_scalar_mul(
                                    et32[b * 64:(b + 1) * 64, k,
                                         a * 64:(a + 1) * 64],
                                    iden_f32[b * 64:(b + 1) * 64,
                                             b * 64:(b + 1) * 64], val)
                    return ins
                P.step("vector", btile,
                       waits=[(PT, P.now(PT)), (PV, base_wait)], inc=PV)
                vb = P.now(PV)
                for h in range(H):
                    bk = h % 2
                    def emm(e, cjs=tuple(cjs), h=h, bk=bk):
                        ins = None
                        for k, cj in enumerate(cjs):
                            ins = e.matmul(
                                PSB[bk][:, 0:64], et32[:, k, :],
                                src_ap[:, cj, h, :],
                                start=(k == 0), stop=(k == len(cjs) - 1))
                        return ins
                    P.step("tensor", emm, waits=[(PV, vb)] + bank_w(bk), inc=PT)
                    write_evac(ci, h, P.now(PT), bk)
            return

        def evac_t1(ci, h, tmm, bk):
            P.step("vector", lambda e, ci=ci, h=h, bk=bk: e.tensor_scalar(
                t1_f[:, ci, h, :], PSB[bk][:, 0:64],
                beta_exp.ap()[:, ci * H + h:ci * H + h + 1], None, ALU.mult),
                waits=[(PT, tmm)], inc=PV)
            bank_r(bk, PV)

        emit_stage([[float(x) for x in row] for row in E_A64],
                   kvp_f.ap(), evac_t1, v_kvp)
        v_t1 = P.now(PV)

        def evac_kvp(ci, h, tmm, bk):
            half = h % 2
            for a in range(2):
                j = 2 * ci + a
                if j >= MJ:
                    continue
                P.step("vector",
                       lambda e, a=a, j=j, h=h, bk=bk, half=half: e.tensor_copy(
                           kvp_bf[half * 64:(half + 1) * 64, j, h // 2, :],
                           PSB[bk][a * 64:(a + 1) * 64, 0:64]),
                       waits=[(PT, tmm)], inc=PV)
            bank_r(bk, PV)

        emit_stage([[float(x) for x in row] for row in E_B64],
                   t1_f.ap(), evac_kvp, v_t1)
        v_kvpb = P.now(PV)

        x2_guard = [None, None]
        attn_guard = None
        featg = None
        y1_dma = []
        for t in range(NT1):
            slot = t % 2
            waits = [(PV, v_kvpb), (PV, v_wq),
                     (w_o_done[0], w_o_done[1])]
            if x2_guard[slot] is not None:
                waits.append(x2_guard[slot])
            c0 = t * T1
            P.step("sync", lambda e, c0=c0, slot=slot: e.dma_start(
                out=x2_t[:, slot, :, :],
                in_=xT.ap()[:, c0:c0 + T1].rearrange("(c p) s -> p c s", p=128)),
                waits=waits, inc=SD, inc_k=16)
            dv = P.now(SD)
            cw = [(SD, dv), (PT, P.now(PT))]
            P.step("scalar", lambda e, slot=slot: e.activation(
                xb2[:, :, :], x2_t[:, slot, :, :], ACT.Copy), waits=cw, inc=PS)
            P.step("scalar", lambda e, slot=slot: e.activation(
                xsq2[:, :, :], x2_t[:, slot, :, :], ACT.Square), inc=PS)
            s_cast = P.now(PS)

            def p2sum(e):
                ins = None
                for ci in range(8):
                    ins = e.matmul(PSB[6][:, :], ones_lhs[:, :], xb2[:, ci, :],
                                   start=(ci == 0), stop=(ci == 7))
                return ins
            def p2sq(e):
                ins = None
                for ci in range(8):
                    ins = e.matmul(PSB[7][:, :], ones_lhs[:, :], xsq2[:, ci, :],
                                   start=(ci == 0), stop=(ci == 7))
                return ins
            P.step("tensor", p2sum, waits=[(PS, s_cast)] + bank_w(6), inc=PT)
            P.step("tensor", p2sq, waits=bank_w(7), inc=PT)
            t_st = P.now(PT)
            P.step("vector", lambda e: e.tensor_scalar(
                mu_rep[:, :], PSB[6][:, :], 1.0 / 1024.0, None, ALU.mult),
                waits=[(PT, t_st), (PS, P.now(PS))], inc=PV)
            P.step("vector", lambda e: e.tensor_scalar(
                negmu_row[0:1, :], PSB[6][0:1, :], -1.0 / 1024.0, None,
                ALU.mult), inc=PV)
            bank_r(6, PV)
            P.step("vector", lambda e: e.tensor_scalar(
                msq_rep[:, :], PSB[7][:, :], 1.0 / 1024.0, None, ALU.mult),
                inc=PV)
            bank_r(7, PV)
            v_nmu2 = P.now(PV)
            P.step("vector", lambda e: e.scalar_tensor_tensor(
                rstd_rep[:, :], mu_rep[:, :], -1.0, mu_rep[:, :],
                ALU.mult, ALU.mult), waits=[(PV, P.now(PV))], inc=PV)
            P.step("vector", lambda e: e.tensor_add(
                rstd_rep[:, :], rstd_rep[:, :], msq_rep[:, :]),
                waits=[(PV, P.now(PV))], inc=PV)
            P.step("vector", lambda e: e.tensor_scalar(
                msq_rep[:, :], rstd_rep[:, :], float(EPS), None, ALU.add),
                waits=[(PV, P.now(PV))], inc=PV)
            vv = P.now(PV)
            P.step("scalar", lambda e: e.activation(
                mu_rep[:, :], msq_rep[:, :], ACT.Sqrt), waits=[(PV, vv)],
                inc=PS)
            sv = P.now(PS)
            def recip2_d(e):
                e.reciprocal(rstd_rep[:, :], mu_rep[:, :])
                return e.drain()
            P.step("vector", recip2_d, waits=[(PS, sv)], inc=PV)
            P.step("vector", lambda e: e.tensor_mul(
                mu_rep[:, :], rstd_rep[:, :], rstd_rep[:, :]),
                waits=[(PV, P.now(PV))], inc=PV)
            P.step("vector", lambda e: e.tensor_mul(
                mu_rep[:, :], mu_rep[:, :], msq_rep[:, :]),
                waits=[(PV, P.now(PV))], inc=PV)
            P.step("vector", lambda e: e.tensor_scalar(
                mu_rep[:, :], mu_rep[:, :], -0.5, 1.5, ALU.mult, ALU.add),
                waits=[(PV, P.now(PV))], inc=PV)
            P.step("vector", lambda e: e.tensor_mul(
                rstd_rep[:, :], rstd_rep[:, :], mu_rep[:, :]),
                waits=[(PV, P.now(PV))], inc=PV)
            v_rstd = P.now(PV)

            for hp in range(8):
                bq = hp % 2
                def qproj(e, hp=hp, bq=bq):
                    ins = None
                    for ci in range(8):
                        ins = e.matmul(
                            PSB[bq][:, :],
                            w_inq_bf[:, ci, hp * 128:(hp + 1) * 128],
                            xb2[:, ci, :], start=(ci == 0), stop=False)
                    ins = e.matmul(
                        PSB[bq][:, :],
                        wsum_in.ap()[0:1, hp * 128:(hp + 1) * 128],
                        negmu_row.ap()[0:1, :], start=False, stop=True)
                    return ins
                P.step("tensor", qproj,
                       waits=[(PS, s_cast), (PV, v_nmu2)] + bank_w(bq), inc=PT)
                t_q = P.now(PT)
                qw = [(PT, t_q), (PV, v_rstd)]
                if featg is not None:
                    qw += [featg]
                P.step("vector", lambda e, bq=bq: e.scalar_tensor_tensor(
                    q2h[:, :], PSB[bq][:, :], 0.25, rstd_rep[:, :],
                    ALU.mult, ALU.mult), waits=qw, inc=PV)
                P.step("vector", lambda e: e.tensor_scalar(
                    q2h[:, :], q2h[:, :], 2.0, -2.0, ALU.min, ALU.max), inc=PV)
                v_q2 = P.now(PV)
                bank_r(bq, PV)

                P.step("scalar", lambda e: e.activation(
                    g_p2[:, :], q2h[:, :], ACT.Square, scale=0.5),
                    waits=[(PV, v_q2)], inc=PS)
                s2 = P.now(PS)
                P.step("vector", lambda e: e.scalar_tensor_tensor(
                    g_p3[:, :], g_p2[:, :], -0.75, q2h[:, :], ALU.add,
                    ALU.mult), waits=[(PS, s2)], inc=PV)
                v3 = P.now(PV)
                P.step("scalar", lambda e: e.activation(
                    g_p4[:, :], g_p2[:, :], ACT.Square, bias=-0.5),
                    waits=[(PS, s2)], inc=PS)
                s4 = P.now(PS)
                P.step("vector", lambda e: e.tensor_mul(
                    g_p5[:, :], g_p2[:, :], g_p3[:, :]),
                    waits=[(PV, v3)], inc=PV)
                v5 = P.now(PV)
                P.step("gpsimd", lambda e: e.tensor_copy(
                    pf_bf[:, 0, :], q2h[:, :]), waits=[(PV, v_q2)], inc=PG)
                P.step("gpsimd", lambda e: e.tensor_copy(
                    pf_bf[:, 1, :], g_p2[:, :]), waits=[(PS, s2)], inc=PG)
                P.step("gpsimd", lambda e: e.tensor_copy(
                    pf_bf[:, 2, :], g_p3[:, :]), waits=[(PV, v3)], inc=PG)
                P.step("vector", lambda e: e.tensor_copy(
                    pf_bf[:, 3, :], g_p4[:, :]), waits=[(PS, s4)], inc=PV)
                P.step("vector", lambda e: e.tensor_copy(
                    pf_bf[:, 4, :], g_p5[:, :]), waits=[(PV, v5)], inc=PV)
                P.step("scalar", lambda e: e.activation(
                    pf_bf[:, 5, :], g_p3[:, :], ACT.Square),
                    waits=[(PV, v3)], inc=PS)
                P.step("scalar", lambda e: e.activation(
                    pf_bf[:, 7, :], g_p4[:, :], ACT.Square),
                    waits=[(PS, s4)], inc=PS)
                P.step("scalar", lambda e: e.activation(
                    pf_bf[:, 9, :], g_p5[:, :], ACT.Square),
                    waits=[(PV, v5)], inc=PS)
                P.step("vector", lambda e: e.tensor_tensor(
                    pf_bf[:, 6, :], g_p3[:, :], g_p4[:, :], ALU.mult),
                    waits=[(PS, s4)], inc=PV)
                P.step("vector", lambda e: e.tensor_tensor(
                    pf_bf[:, 8, :], g_p4[:, :], g_p5[:, :], ALU.mult), inc=PV)
                fv, fs, fg = P.now(PV), P.now(PS), P.now(PG)

                be = 2 + 2 * (hp % 2)
                def einsum(e, hp=hp, be=be):
                    ins = None
                    for j in range(MJ):
                        for half in range(2):
                            h = 2 * hp + half
                            lhs = kvp_bf.ap()[half * 64:(half + 1) * 64,
                                              j, h // 2, :]
                            rhs = (ones_rhs[half * 64:(half + 1) * 64, :]
                                   if j == 0 else
                                   pf_bf.ap()[half * 64:(half + 1) * 64,
                                              j - 1, :])
                            ins = e.matmul(
                                PSB[be + half][half * 64:half * 64 + 64, :],
                                lhs, rhs, start=(j == 0), stop=(j == MJ - 1),
                                tile_position=(half * 64, half * 64))
                    return ins
                P.step("tensor", einsum,
                       waits=[(PV, fv), (PS, fs), (PG, fg)]
                       + bank_w(be, be + 1), inc=PT)
                t_e = P.now(PT)
                featg = (PT, t_e)
                aw = [(PT, t_e)]
                if attn_guard is not None:
                    aw.append(attn_guard)
                P.step("scalar", lambda e, hp=hp, be=be: e.activation(
                    attn_bf.ap()[0:64, hp, :], PSB[be][0:64, :], ACT.Copy),
                    waits=aw, inc=PS)
                P.step("scalar", lambda e, hp=hp, be=be: e.activation(
                    attn_bf.ap()[64:128, hp, :], PSB[be + 1][64:128, :],
                    ACT.Copy), inc=PS)
                bank_r(be, PS)
                bank_r(be + 1, PS)
            s_attn = P.now(PS)

            for oc in range(8):
                bw_ = 6 + (oc % 2)
                def womm(e, oc=oc, bw_=bw_):
                    ins = None
                    for ac in range(8):
                        ins = e.matmul(
                            PSB[bw_][:, :],
                            w_outT_bf[:, ac, oc * 128:(oc + 1) * 128],
                            attn_bf[:, ac, :], start=(ac == 0), stop=(ac == 7))
                    return ins
                P.step("tensor", womm,
                       waits=[(PS, s_attn)] + bank_w(bw_), inc=PT)
                t_w = P.now(PT)
                yw = [(PT, t_w)]
                if t > 0:
                    yw.append((SD, y1_dma[-1]))
                P.step("vector",
                       lambda e, oc=oc, bw_=bw_, slot=slot: e.tensor_tensor(
                           y1_t[:, oc, :], PSB[bw_][:, :],
                           x2_t[:, slot, oc, :], ALU.add),
                       waits=yw, inc=PV)
                bank_r(bw_, PV)
            v_y1 = P.now(PV)
            x2_guard[slot] = (PV, v_y1)
            attn_guard = (PT, P.now(PT))
            P.step("sync", lambda e, t=t: e.dma_start(
                out=y1d.ap()[:, t * T1:(t + 1) * T1]
                    .rearrange("(c p) s -> p c s", p=128),
                in_=y1_t[:, :, :]), waits=[(PV, v_y1)], inc=SD, inc_k=16)
            y1_dma.append(P.now(SD))

        p2_t, p2_v, p2_s, p2_g = P.now(PT), P.now(PV), P.now(PS), P.now(PG)

        wfc_done = gdma(lambda e: e.dma_start(
            out=W_fcT_bf[:, :, :],
            in_=W_fcT.ap().rearrange("(c p) f -> p c f", p=128)),
            waits=[(PT, p2_t), (PV, p2_v), (PS, p2_s), (PG, p2_g)])
        wpj_done = gdma(lambda e: e.dma_start(
            out=W_projT_bf[:, :, :],
            in_=W_projT.ap().rearrange("(c p) f -> p c f", p=128)),
            waits=[(PT, p2_t), (PV, p2_v)])

        def wsumfc(e):
            ins = None
            for nb in range(8):
                for ci in range(8):
                    ins = e.matmul(PSB[nb][:, :], ones_lhs[:, :],
                                   W_fcT_bf[:, ci, nb * 512:(nb + 1) * 512],
                                   start=(ci == 0), stop=(ci == 7))
            return ins
        P.step("tensor", wsumfc,
               waits=[(wfc_done[0], wfc_done[1])] + bank_w(*range(8)), inc=PT)
        t_wfc = P.now(PT)
        for nb in range(8):
            P.step("vector", lambda e, nb=nb: e.tensor_copy(
                wsum_fc.ap()[0:1, nb * 512:(nb + 1) * 512], PSB[nb][0:1, :]),
                waits=[(PT, t_wfc)], inc=PV)
            bank_r(nb, PV)
        v_wfc = P.now(PV)

        y3_guard = [None, None]
        hsb_guard = None
        for t in range(NT3):
            slot = t % 2
            waits = [(PV, v_wfc), (wpj_done[0], wpj_done[1]),
                     (SD, y1_dma[t // 2])]
            if y3_guard[slot] is not None:
                waits.append(y3_guard[slot])
            c0 = t * T3
            P.step("sync", lambda e, c0=c0, slot=slot: e.dma_start(
                out=y1_in[:, slot, :, :],
                in_=y1d.ap()[:, c0:c0 + T3].rearrange("(c p) s -> p c s", p=128)),
                waits=waits, inc=SD, inc_k=16)
            dv = P.now(SD)
            cw = [(SD, dv), (PT, P.now(PT))]
            P.step("scalar", lambda e, slot=slot: e.activation(
                yb3[:, :, :], y1_in[:, slot, :, :], ACT.Copy), waits=cw, inc=PS)
            P.step("scalar", lambda e, slot=slot: e.activation(
                ysq3[:, :, :], y1_in[:, slot, :, :], ACT.Square), inc=PS)
            s_cast = P.now(PS)
            def p3sum(e):
                ins = None
                for ci in range(8):
                    ins = e.matmul(PSB[6][:, 0:T3], ones_lhs[:, :],
                                   yb3[:, ci, :], start=(ci == 0), stop=(ci == 7))
                return ins
            def p3sq(e):
                ins = None
                for ci in range(8):
                    ins = e.matmul(PSB[7][:, 0:T3], ones_lhs[:, :],
                                   ysq3[:, ci, :], start=(ci == 0), stop=(ci == 7))
                return ins
            P.step("tensor", p3sum, waits=[(PS, s_cast)] + bank_w(6), inc=PT)
            P.step("tensor", p3sq, waits=bank_w(7), inc=PT)
            t_st = P.now(PT)
            P.step("vector", lambda e: e.tensor_scalar(
                mu3_rep[:, :], PSB[6][:, 0:T3], 1.0 / 1024.0, None, ALU.mult),
                waits=[(PT, t_st)], inc=PV)
            P.step("vector", lambda e: e.tensor_scalar(
                nmu3_row[0:1, :], PSB[6][0:1, 0:T3], -1.0 / 1024.0, None,
                ALU.mult), inc=PV)
            bank_r(6, PV)
            P.step("vector", lambda e: e.tensor_scalar(
                msq3_rep[:, :], PSB[7][:, 0:T3], 1.0 / 1024.0, None, ALU.mult),
                inc=PV)
            bank_r(7, PV)
            v_nmu3 = P.now(PV)
            P.step("vector", lambda e: e.scalar_tensor_tensor(
                rsq_rep[:, :], mu3_rep[:, :], -1.0, mu3_rep[:, :],
                ALU.mult, ALU.mult), waits=[(PV, P.now(PV))], inc=PV)
            P.step("vector", lambda e: e.tensor_add(
                rsq_rep[:, :], rsq_rep[:, :], msq3_rep[:, :]),
                waits=[(PV, P.now(PV))], inc=PV)
            P.step("vector", lambda e: e.tensor_scalar(
                msq3_rep[:, :], rsq_rep[:, :], float(EPS), None, ALU.add),
                waits=[(PV, P.now(PV))], inc=PV)
            def recip3_d(e):
                e.reciprocal(rsq_rep[:, :], msq3_rep[:, :])
                return e.drain()
            P.step("vector", recip3_d, waits=[(PV, P.now(PV))], inc=PV)
            v_rsq = P.now(PV)

            for half in range(2):
                h_marks = []
                for fc in range(16):
                    fci = half * 16 + fc
                    bk = fc % 2
                    def fcmm(e, fci=fci, bk=bk):
                        ins = None
                        for ci in range(8):
                            ins = e.matmul(
                                PSB[bk][:, 0:T3],
                                W_fcT_bf[:, ci, fci * 128:(fci + 1) * 128],
                                yb3[:, ci, :], start=(ci == 0), stop=False)
                        ins = e.matmul(
                            PSB[bk][:, 0:T3],
                            wsum_fc.ap()[0:1, fci * 128:(fci + 1) * 128],
                            nmu3_row.ap()[0:1, :], start=False, stop=True)
                        return ins
                    P.step("tensor", fcmm,
                           waits=[(PS, s_cast), (PV, v_nmu3)] + bank_w(bk),
                           inc=PT)
                    t_fc = P.now(PT)
                    P.step("scalar", lambda e, bk=bk: e.activation(
                        a_sb[:, bk, :], PSB[bk][:, 0:T3], ACT.Relu),
                        waits=[(PT, t_fc)], inc=PS)
                    sa = P.now(PS)
                    bank_r(bk, PS)
                    hw = [(PS, sa)]
                    if hsb_guard is not None:
                        hw.append(hsb_guard)
                    P.step("vector", lambda e, bk=bk, fc=fc: e.tensor_tensor(
                        h_sb[:, fc, :], a_sb[:, bk, :], a_sb[:, bk, :],
                        ALU.mult), waits=hw, inc=PV)
                    h_marks.append(P.now(PV))
                v_h = h_marks[-1]
                for oc in range(8):
                    bk = 2 + (oc % 4)
                    def pjmm(e, oc=oc, bk=bk, half=half):
                        ins = None
                        for fc in range(16):
                            fci = half * 16 + fc
                            ins = e.matmul(
                                PSB[bk][:, 0:T3],
                                W_projT_bf[:, fci, oc * 128:(oc + 1) * 128],
                                h_sb[:, fc, :], start=(fc == 0), stop=(fc == 15))
                        return ins
                    P.step("tensor", pjmm,
                           waits=[(PV, v_h), (wpj_done[0], wpj_done[1])]
                           + bank_w(bk), inc=PT)
                    t_pj = P.now(PT)
                    if half == 0:
                        P.step("vector", lambda e, oc=oc, bk=bk: e.tensor_copy(
                            y2_sb[:, oc, :], PSB[bk][:, 0:T3]),
                            waits=[(PT, t_pj)], inc=PV)
                    else:
                        P.step("vector",
                               lambda e, oc=oc, bk=bk: e.tensor_tensor(
                                   y2_sb[:, oc, :], PSB[bk][:, 0:T3],
                                   y2_sb[:, oc, :], ALU.add),
                               waits=[(PT, t_pj)], inc=PV)
                    bank_r(bk, PV)
                hsb_guard = (PT, P.now(PT))
            for oc in range(8):
                P.step("vector", lambda e, oc=oc: e.tensor_tensor(
                    y2_sb[:, oc, :], y2_sb[:, oc, :], rsq_rep[:, :], ALU.mult),
                    waits=[(PV, v_rsq)], inc=PV)
            v_y2 = P.now(PV)

            y2f = y2_sb.ap().rearrange("p a b -> p (a b)")
            mf = m_sb.ap().rearrange("p a b -> p (a b)")
            sf = spk_sb.ap().rearrange("p a b -> p (a b)")
            y1f = y1_in.ap()[:, slot, :, :].rearrange("p a b -> p (a b)")
            P.step("vector", lambda e, y2f=y2f, sf=sf: e.tensor_single_scalar(
                sf, y2f, 1.0, ALU.is_gt), waits=[(PV, v_y2)], inc=PV)
            P.step("vector", lambda e, y1f=y1f, sf=sf: e.scalar_tensor_tensor(
                y1f, sf, 0.2, y1f, ALU.mult, ALU.add), inc=PV)
            P.step("vector", lambda e, y2f=y2f, mf=mf: e.scalar_tensor_tensor(
                mf, y2f, 0.9, y2f, ALU.mult, ALU.add), inc=PV)
            P.step("vector", lambda e, mf=mf, sf=sf: e.tensor_sub(
                mf, mf, sf), inc=PV)
            for stp in range(4):
                P.step("vector", lambda e, mf=mf, sf=sf: e.tensor_single_scalar(
                    sf, mf, 1.0, ALU.is_gt), inc=PV)
                P.step("vector",
                       lambda e, y1f=y1f, sf=sf: e.scalar_tensor_tensor(
                           y1f, sf, 0.2, y1f, ALU.mult, ALU.add), inc=PV)
                if stp < 3:
                    P.step("vector",
                           lambda e, mf=mf, y2f=y2f: e.scalar_tensor_tensor(
                               mf, mf, 0.9, y2f, ALU.mult, ALU.add), inc=PV)
                    P.step("vector", lambda e, mf=mf, sf=sf: e.tensor_sub(
                        mf, mf, sf), inc=PV)
            v_out = P.now(PV)

            P.step("sync", lambda e, t=t, slot=slot: e.dma_start(
                out=outT.ap()[:, t * T3:(t + 1) * T3]
                    .rearrange("(c p) s -> p c s", p=128),
                in_=y1_in[:, slot, :, :]), waits=[(PV, v_out)],
                inc=SD, inc_k=16)
            y3_guard[slot] = (SD, P.now(SD))

        P.step("sync", lambda e: e.nop(), waits=[(SD, P.now(SD))])

        block = stk.enter_context(nc.Block())
        P.emit(block)
    return nc


_NC_CACHE = None


def kernel(L0_attn_x, L0_attn_w_in, L0_attn_w_out, L0_attn_beta,
           L0_ffn_W_fc, L0_ffn_W_proj):
    global _NC_CACHE
    x = np.asarray(L0_attn_x, dtype=np.float32)
    w_in = np.ascontiguousarray(np.asarray(L0_attn_w_in, np.float32).T)
    w_out = np.ascontiguousarray(np.asarray(L0_attn_w_out, np.float32).T)
    beta = np.ascontiguousarray(np.asarray(L0_attn_beta, np.float32).T)
    W_fc = np.ascontiguousarray(np.asarray(L0_ffn_W_fc, np.float32).T)
    W_proj = np.ascontiguousarray(np.asarray(L0_ffn_W_proj, np.float32).T)
    if _NC_CACHE is None:
        _NC_CACHE = build_nc()
    nc = _NC_CACHE
    in_maps = []
    for c in range(8):
        in_maps.append({
            "xT": np.ascontiguousarray(x[c].T),
            "w_inT": w_in, "w_outT": w_out, "betaT": beta,
            "W_fcT": W_fc, "W_projT": W_proj,
        })
    res = run_bass_kernel_spmd(nc, in_maps, core_ids=list(range(8)))
    out = np.empty((8, S, D), dtype=np.float32)
    for c in range(8):
        out[c] = res.results[c]["outT"].T
    return out


# version 04
# speedup vs baseline: 1.0635x; 1.0635x over previous
import contextlib
from fractions import Fraction

import numpy as np
import concourse.bass as bass
import concourse.mybir as mybir
from concourse.bass_utils import run_bass_kernel_spmd

F32 = mybir.dt.float32
BF16 = mybir.dt.bfloat16
ALU = mybir.AluOpType
ACT = mybir.ActivationFunctionType

D = 1024
S = 4096
H = 16
DH = 64
MJ = 11
NT1 = 8
T1 = 512
SUB = 128
NT3 = 16
T3 = 256
FD = 4096
EPS = 1e-5

ENGINES = ("tensor", "vector", "scalar", "gpsimd", "sync")


def _basis_matrices():
    def cmul(a, b):
        out = {}
        for i, ai in a.items():
            for j, bj in b.items():
                for k in (i + j, abs(i - j)):
                    out[k] = out.get(k, Fraction(0)) + ai * bj / 2
        return {k: v for k, v in out.items() if v != 0}

    def cadd(a, b, sa=Fraction(1), sb=Fraction(1)):
        out = {}
        for k, v in a.items():
            out[k] = out.get(k, Fraction(0)) + sa * v
        for k, v in b.items():
            out[k] = out.get(k, Fraction(0)) + sb * v
        return {k: v for k, v in out.items() if v != 0}

    T0, T1_ = {0: Fraction(1)}, {1: Fraction(1)}
    P = [None] * MJ
    P[0] = T0
    P[1] = cadd(T1_, {}, Fraction(2))
    P[2] = cmul(cadd(P[1], {}, Fraction(1, 2)), cadd(P[1], {}, Fraction(1, 2)))
    P[3] = cmul(cadd(P[2], T0, Fraction(1), Fraction(-3, 4)), P[1])
    P[4] = cmul(cadd(P[2], T0, Fraction(1), Fraction(-1, 2)),
                cadd(P[2], T0, Fraction(1), Fraction(-1, 2)))
    P[5] = cmul(P[2], P[3])
    P[6] = cmul(P[3], P[3])
    P[7] = cmul(P[3], P[4])
    P[8] = cmul(P[4], P[4])
    P[9] = cmul(P[4], P[5])
    P[10] = cmul(P[5], P[5])
    Mq = [[P[j].get(m, Fraction(0)) for m in range(MJ)] for j in range(MJ)]
    n = MJ
    Aug = [row[:] + [Fraction(int(i == j)) for j in range(n)]
           for i, row in enumerate(Mq)]
    for c in range(n):
        p = next(r for r in range(c, n) if Aug[r][c] != 0)
        Aug[c], Aug[p] = Aug[p], Aug[c]
        piv = Aug[c][c]
        Aug[c] = [v / piv for v in Aug[c]]
        for r in range(n):
            if r != c and Aug[r][c] != 0:
                f = Aug[r][c]
                Aug[r] = [vr - f * vc for vr, vc in zip(Aug[r], Aug[c])]
    Minv = [row[n:] for row in Aug]
    E_A = np.array([[float(Minv[j][m]) for m in range(MJ)] for j in range(MJ)],
                   dtype=np.float64)
    return E_A, E_A.T.copy()


E_A64, E_B64 = _basis_matrices()


class Prog:

    def __init__(self, nc, stk):
        self.nc = nc
        self.steps = {e: [] for e in ENGINES}
        self._sems = {}
        self._stk = stk
        self.counts = {}

    def sem(self, name):
        if name not in self._sems:
            self._sems[name] = self._stk.enter_context(self.nc.semaphore(name))
            self.counts[name] = 0
        return self._sems[name]

    def step(self, eng, fn, waits=(), inc=None, inc_k=1):
        w = []
        seen = {}
        for (s, v) in waits:
            if v is None:
                continue
            self.sem(s)
            if s not in seen or seen[s] < v:
                seen[s] = v
        w = [(self._sems[s], v) for s, v in seen.items()]
        if inc is not None:
            self.sem(inc)
            self.counts[inc] += inc_k
        self.steps[eng].append((fn, w, (self._sems[inc] if inc else None), inc_k))
        return self.counts[inc] if inc else None

    def now(self, sem_name):
        self.sem(sem_name)
        return self.counts[sem_name]

    def emit(self, block):
        for ename in ENGINES:
            steps = self.steps[ename]
            if not steps:
                continue

            def body(eng, steps=steps):
                for fn, waits, inc_sem, inc_k in steps:
                    for (sem, val) in waits:
                        eng.wait_ge(sem, val)
                    ins = fn(eng)
                    if inc_sem is not None:
                        assert ins is not None
                        ins.then_inc(inc_sem, inc_k)
            getattr(block, ename)(body)


def ceil32(x):
    return (x + 31) & ~31


def build_nc():
    nc = bass.Bass("TRN2", target_bir_lowering=False,
                   detect_race_conditions=False)

    xT = nc.dram_tensor("xT", [D, S], F32, kind="ExternalInput")
    w_inT = nc.dram_tensor("w_inT", [D, 3 * D], F32, kind="ExternalInput")
    w_outT = nc.dram_tensor("w_outT", [D, D], F32, kind="ExternalInput")
    betaT = nc.dram_tensor("betaT", [MJ, H], F32, kind="ExternalInput")
    W_fcT = nc.dram_tensor("W_fcT", [D, FD], F32, kind="ExternalInput")
    W_projT = nc.dram_tensor("W_projT", [FD, D], F32, kind="ExternalInput")
    outT = nc.dram_tensor("outT", [D, S], F32, kind="ExternalOutput")
    y1d = nc.dram_tensor("y1scratch", [D, S], F32, kind="Internal")

    EA32 = E_A64.astype(np.float32)
    assert np.array_equal(EA32.astype(np.float64), E_A64), "E_A not fp32 exact"

    with contextlib.ExitStack() as stk:
        P = Prog(nc, stk)
        BASE = 16512
        LIMIT = 212832
        assert BASE >= nc.sbuf_base and BASE + LIMIT <= nc.sbuf_top
        placed = []

        def at(name, off, shape, dtype):
            nbytes = int(np.prod(shape[1:])) * mybir.dt.size(dtype)
            assert off + nbytes <= LIMIT, f"{name}: {off}+{nbytes} > {LIMIT}"
            placed.append((name, off, off + nbytes))
            return nc.alloc_sbuf_tensor_at(name, list(shape), dtype,
                                           offset=BASE + off)

        w_inkv_bf = at("w_inkv_bf", 0, [128, 8, 2 * D], BF16)
        w_inq_bf = at("w_inq_bf", 0, [128, 8, D], BF16)
        w_outT_bf = at("w_outT_bf", 16384, [128, 8, D], BF16)
        W_fcT_bf = at("W_fcT_bf", 0, [128, 8, FD], BF16)
        o = 201472
        beta_exp = at("beta_exp", o, [128, 6 * H], F32); o += ceil32(6 * H * 4)
        iden_f32 = at("iden_f32", o, [128, 128], F32); o += 512
        iota_f32 = at("iota_f32", o, [128, 128], F32); o += 512
        ones_lhs = at("ones_lhs", o, [128, 128], BF16); o += 256
        ones_rhs = at("ones_rhs", o, [128, 512], BF16); o += 1024
        ones_row = at("ones_row", o, [1, 64], F32); o += 256
        beta_row = at("beta_row", o, [1, MJ * H], F32); o += ceil32(MJ * H * 4)
        wsum_in = at("wsum_in", o, [1, 3 * D], BF16); o += ceil32(3 * D * 2)
        negmu_row = at("negmu_row", o, [1, T1], BF16); o += ceil32(T1 * 2)
        statc = at("statc", o, [128, 8, 4], F32); o += 4 * 8 * 4
        cpool = at("cpool", o, [128, 2], F32); o += 32
        C_END = o
        assert C_END <= LIMIT, f"C region: {C_END}"
        C0 = 201472

        E0 = 32768
        p = E0
        x_roll = at("x_roll", p, [128, 8, SUB], F32); p += 8 * SUB * 4
        x_bf = at("x_bf", p, [128, 8, T1], BF16); p += 8 * T1 * 2
        x2_bf = at("x2_bf", p, [128, 8, T1], BF16); p += 8 * T1 * 2
        k2_t = at("k2_t", p, [128, 2, D], F32); p += 2 * D * 4
        f_p2 = at("f_p2", p, [128, D], F32); p += D * 4
        f_p3 = at("f_p3", p, [128, D], F32); p += D * 4
        f_p4 = at("f_p4", p, [128, D], F32); p += D * 4
        f_p5 = at("f_p5", p, [128, D], F32); p += D * 4
        v_bf = at("v_bf", p, [128, 4, D], BF16); p += 4 * D * 2
        tk_bf = at("tk_bf", p, [128, 4, H, 10, DH], BF16); p += 4 * H * 10 * DH * 2
        kvt_acc = at("kvt_acc", p, [128, 8, 10 * DH], F32); p += 8 * 10 * DH * 4
        vsum_acc = at("vsum_acc", p, [128, H * DH], F32); p += H * DH * 4
        P1_END = p
        assert P1_END <= C0, f"P1 over: {P1_END}"

        p = E0
        kvp_f = at("kvp_f", p, [128, 6, H, DH], F32); p += 6 * H * DH * 4
        t1_f = at("t1_f", p, [128, 6, H, DH], F32); p += 6 * H * DH * 4
        et32 = at("et32", p, [128, 6, 128], F32); p += 6 * 128 * 4
        TR_END = p
        tk_off = [x for x in placed if x[0] == "tk_bf"][0][1]
        assert TR_END <= tk_off, "TR scratch hits tk"

        p = E0
        x2_t = at("x2_t", p, [128, 2, 8, T1], F32); p += 2 * 8 * T1 * 4
        xb2 = at("xb2", p, [128, 8, T1], BF16); p += 8 * T1 * 2
        xsq2 = at("xsq2", p, [128, 8, T1], BF16); p += 8 * T1 * 2
        q2h = at("q2h", p, [128, 2, T1], F32); p += 2 * T1 * 4
        g_p2 = at("g_p2", p, [128, 2, T1], F32); p += 2 * T1 * 4
        g_p3 = at("g_p3", p, [128, 2, T1], F32); p += 2 * T1 * 4
        g_p4 = at("g_p4", p, [128, 2, T1], F32); p += 2 * T1 * 4
        g_p5 = at("g_p5", p, [128, 2, T1], F32); p += 2 * T1 * 4
        pf_bf = at("pf_bf", p, [128, 2, 10, T1], BF16); p += 2 * 10 * T1 * 2
        attn_bf = at("attn_bf", p, [128, 8, T1], BF16); p += 8 * T1 * 2
        y1_t = at("y1_t", p, [128, 8, T1], F32); p += 8 * T1 * 4
        rstd_rep = at("rstd_rep", p, [128, T1], F32); p += T1 * 4
        mu_rep = at("mu_rep", p, [128, T1], F32); p += T1 * 4
        msq_rep = at("msq_rep", p, [128, T1], F32); p += T1 * 4
        kvp_bf = at("kvp_bf", p, [128, MJ, 8, DH], BF16); p += MJ * 8 * DH * 2
        P2_END = p
        assert P2_END <= C0, f"P2 over: {P2_END}"
        kvpbf_off = [x for x in placed if x[0] == "kvp_bf"][0][1]
        assert kvpbf_off >= TR_END, "kvp_bf overlaps TR scratch"

        p = 65536
        W_projT_bf = at("W_projT_bf", p, [128, 32, D], BF16); p += 32 * D * 2
        y1_in = at("y1_in", p, [128, 2, 8, T3], F32); p += 2 * 8 * T3 * 4
        yb3 = at("yb3", p, [128, 8, T3], BF16); p += 8 * T3 * 2
        h_sb = at("h_sb", p, [128, 16, T3], BF16)
        ysq3 = at("ysq3", p, [128, 8, T3], BF16); p += 16 * T3 * 2
        y2_sb = at("y2_sb", p, [128, 8, T3], F32); p += 8 * T3 * 4
        m_sb = at("m_sb", p, [128, 8, T3], F32); p += 8 * T3 * 4
        spk_sb = at("spk_sb", p, [128, 8, T3], F32); p += 8 * T3 * 4
        a_sb = at("a_sb", p, [128, 2, T3], F32); p += 2 * T3 * 4
        rsq_rep = at("rsq_rep", p, [128, T3], F32); p += T3 * 4
        mu3_rep = at("mu3_rep", p, [128, T3], F32); p += T3 * 4
        msq3_rep = at("msq3_rep", p, [128, T3], F32); p += T3 * 4
        nmu3_row = at("nmu3_row", p, [1, T3], BF16); p += ceil32(T3 * 2)
        wsum_fc = at("wsum_fc", p, [1, FD], BF16); p += ceil32(FD * 2)
        P3_END = p
        assert P3_END <= C0, f"P3 over: {P3_END}"

        PSB = [nc.alloc_psum_tensor(f"psb{i}", [128, 512], F32) for i in range(8)]

        SD = "sdma"
        SST = "sdst"
        GP = ["gdma0", "gdma1", "gdma2", "gdma3"]
        PT, PV, PS, PG = "pt", "pv", "ps", "pg"
        gp_i = [0]
        gp_pending = {}
        bank_guard = {}

        def gdma(fn, waits=()):
            s = GP[gp_i[0] % len(GP)]
            gp_i[0] += 1
            w = list(waits)
            if s in gp_pending:
                w.append((s, gp_pending[s]))
            P.step("gpsimd", fn, waits=w, inc=s, inc_k=16)
            gp_pending[s] = P.now(s)
            return (s, P.now(s))

        def bank_w(*banks):
            return [bank_guard[b] for b in banks if b in bank_guard]

        def bank_r(b, sem_name):
            bank_guard[b] = (sem_name, P.now(sem_name))

        w_kv_done = gdma(lambda e: e.dma_start(
            out=w_inkv_bf[:, :, :],
            in_=w_inT.ap()[:, D:3 * D].rearrange("(c p) f -> p c f", p=128)))
        beta_done = gdma(lambda e: e.dma_start(
            out=beta_row[0:1, :],
            in_=betaT.ap().rearrange("m h -> (m h)")))

        P.step("gpsimd", lambda e: e.iota(
            iota_f32[:, :], [[1, 128]], channel_multiplier=-1,
            allow_small_or_imprecise_dtypes=True), inc=PG)
        g_iota = P.now(PG)
        P.step("vector", lambda e: e.tensor_single_scalar(
            iden_f32[:, :], iota_f32[:, :], 0.0, ALU.is_equal),
            waits=[(PG, g_iota)], inc=PV)
        P.step("vector", lambda e: e.memset(ones_lhs[:, :], 1.0), inc=PV)
        P.step("vector", lambda e: e.memset(ones_rhs[:, :], 1.0), inc=PV)
        P.step("vector", lambda e: e.memset(ones_row[0:1, :], 1.0), inc=PV)
        P.step("vector", lambda e: e.memset(cpool[:, 0:1], float(EPS)), inc=PV)
        P.step("vector", lambda e: e.memset(cpool[:, 1:2], -0.5), inc=PV)
        nc.const_aps.register(F32, float(EPS), cpool.ap()[:, 0:1])
        nc.const_aps.register(F32, -0.5, cpool.ap()[:, 1:2])
        consts_v = P.now(PV)

        for c in range(6):
            for half in range(2):
                m = 2 * c + half
                dst = (c * H, (c + 1) * H)
                if m >= MJ:
                    P.step("vector", lambda e, half=half, dst=dst: e.memset(
                        beta_exp[half * 64:(half + 1) * 64, dst[0]:dst[1]], 0.0),
                        inc=PV)
                    continue
                P.step("tensor", lambda e, m=m: e.matmul(
                    PSB[0][0:64, 0:H], ones_row[0:1, :],
                    beta_row.ap()[0:1, m * H:(m + 1) * H],
                    start=True, stop=True),
                    waits=[(PV, consts_v), (beta_done[0], beta_done[1])]
                    + bank_w(0), inc=PT)
                tb = P.now(PT)
                P.step("vector", lambda e, half=half, dst=dst: e.tensor_copy(
                    beta_exp[half * 64:(half + 1) * 64, dst[0]:dst[1]],
                    PSB[0][0:64, 0:H]), waits=[(PT, tb)], inc=PV)
                bank_r(0, PV)
        bexp_v = P.now(PV)

        def wsum_kv(e):
            ins = None
            for ci in range(8):
                for nb in range(4):
                    ins = e.matmul(PSB[nb][:, :], ones_lhs[:, :],
                                   w_inkv_bf[:, ci, nb * 512:(nb + 1) * 512],
                                   start=(ci == 0), stop=(ci == 7))
            return ins
        P.step("tensor", wsum_kv,
               waits=[(w_kv_done[0], w_kv_done[1]), (PV, consts_v)]
               + bank_w(0, 1, 2, 3), inc=PT)
        t_wkv = P.now(PT)
        for nb in range(4):
            P.step("vector", lambda e, nb=nb: e.tensor_copy(
                wsum_in.ap()[0:1, D + nb * 512:D + (nb + 1) * 512],
                PSB[nb][0:1, :]), waits=[(PT, t_wkv)], inc=PV)
            bank_r(nb, PV)
        v_wkv = P.now(PV)

        x_guard = [None]
        xbf_guard = None
        feat_guard = None
        tkv_guard = None

        for t in range(NT1):
            cast_marks = []
            for u in range(4):
                waits = []
                if x_guard[0] is not None:
                    waits.append(x_guard[0])
                c0 = t * T1 + u * SUB
                P.step("sync", lambda e, c0=c0: e.dma_start(
                    out=x_roll[:, :, :],
                    in_=xT.ap()[:, c0:c0 + SUB].rearrange("(c p) s -> p c s", p=128)),
                    waits=waits, inc=SD, inc_k=16)
                dv = P.now(SD)
                w2 = [(SD, dv)]
                if xbf_guard is not None:
                    w2.append(xbf_guard)
                P.step("scalar", lambda e, u=u: e.activation(
                    x_bf.ap()[:, :, u * SUB:(u + 1) * SUB],
                    x_roll[:, :, :], ACT.Copy), waits=w2, inc=PS)
                P.step("scalar", lambda e, u=u: e.activation(
                    x2_bf.ap()[:, :, u * SUB:(u + 1) * SUB],
                    x_roll[:, :, :], ACT.Square), inc=PS)
                x_guard[0] = (PS, P.now(PS))
                cast_marks.append(P.now(PS))
            casts_done = cast_marks[-1]

            def fsum(e):
                ins = None
                for ci in range(8):
                    ins = e.matmul(PSB[4][:, :], ones_lhs[:, :], x_bf[:, ci, :],
                                   start=(ci == 0), stop=(ci == 7))
                return ins
            P.step("tensor", fsum, waits=[(PS, casts_done)] + bank_w(4), inc=PT)
            t_sum = P.now(PT)
            P.step("vector", lambda e: e.tensor_scalar(
                negmu_row[0:1, :], PSB[4][0:1, :], -1.0 / 1024.0, None, ALU.mult),
                waits=[(PT, t_sum)], inc=PV)
            bank_r(4, PV)
            v_nmu = P.now(PV)

            for u in range(4):
                def tsum(e, u=u):
                    ins = None
                    for ci in range(8):
                        ins = e.matmul(PSB[5][:, 0:1],
                                       x_bf[:, ci, u * SUB:(u + 1) * SUB],
                                       ones_rhs[:, 0:1],
                                       start=(ci == 0), stop=(ci == 7))
                    return ins
                P.step("tensor", tsum,
                       waits=[(PS, cast_marks[u])] + bank_w(5), inc=PT)
                tv = P.now(PT)
                P.step("vector", lambda e, u=u: e.tensor_scalar(
                    statc[:, 0, u:u + 1], PSB[5][:, 0:1], 1.0 / 1024.0, None,
                    ALU.mult), waits=[(PT, tv)], inc=PV)
                bank_r(5, PV)
                def tsq(e, u=u):
                    ins = None
                    for ci in range(8):
                        ins = e.matmul(PSB[5][:, 0:1],
                                       x2_bf[:, ci, u * SUB:(u + 1) * SUB],
                                       ones_rhs[:, 0:1],
                                       start=(ci == 0), stop=(ci == 7))
                    return ins
                P.step("tensor", tsq, waits=bank_w(5), inc=PT)
                tv2 = P.now(PT)
                P.step("vector", lambda e, u=u: e.tensor_scalar(
                    statc[:, 1, u:u + 1], PSB[5][:, 0:1], 1.0 / 1024.0, None,
                    ALU.mult), waits=[(PT, tv2)], inc=PV)
                bank_r(5, PV)
            P.step("vector", lambda e: e.scalar_tensor_tensor(
                statc[:, 2, :], statc[:, 0, :], -1.0, statc[:, 0, :],
                ALU.mult, ALU.mult), waits=[(PV, P.now(PV))], inc=PV)
            P.step("vector", lambda e: e.tensor_add(
                statc[:, 2, :], statc[:, 2, :], statc[:, 1, :]),
                waits=[(PV, P.now(PV))], inc=PV)
            P.step("vector", lambda e: e.tensor_scalar(
                statc[:, 6, :], statc[:, 2, :], float(EPS), None, ALU.add),
                waits=[(PV, P.now(PV))], inc=PV)
            vva = P.now(PV)
            P.step("scalar", lambda e: e.activation(
                statc[:, 3, :], statc[:, 6, :], ACT.Sqrt),
                waits=[(PV, vva)], inc=PS)
            sva = P.now(PS)
            def recip_d(e):
                e.reciprocal(statc[:, 4, :], statc[:, 3, :])
                return e.drain()
            P.step("vector", recip_d, waits=[(PS, sva)], inc=PV)
            P.step("vector", lambda e: e.tensor_mul(
                statc[:, 7, :], statc[:, 4, :], statc[:, 4, :]),
                waits=[(PV, P.now(PV))], inc=PV)
            P.step("vector", lambda e: e.tensor_mul(
                statc[:, 7, :], statc[:, 7, :], statc[:, 6, :]),
                waits=[(PV, P.now(PV))], inc=PV)
            P.step("vector", lambda e: e.tensor_scalar(
                statc[:, 7, :], statc[:, 7, :], -0.5, 1.5, ALU.mult,
                ALU.add), waits=[(PV, P.now(PV))], inc=PV)
            P.step("vector", lambda e: e.tensor_mul(
                statc[:, 4, :], statc[:, 4, :], statc[:, 7, :]),
                waits=[(PV, P.now(PV))], inc=PV)
            P.step("vector", lambda e: e.tensor_scalar(
                statc[:, 5, :], statc[:, 4, :], 0.25, None, ALU.mult),
                waits=[(PV, P.now(PV))], inc=PV)
            rstd_all = P.now(PV)
            rstd_marks = [rstd_all] * 4

            featm = []
            for u in range(4):
                waits = ([(PS, casts_done), (PV, rstd_marks[u]), (PV, v_wkv),
                          (PV, v_nmu)] + bank_w(0, 1, 2, 3))
                def kvproj(e, u=u):
                    ins = None
                    for ci in range(8):
                        for ob in range(4):
                            ins = e.matmul(
                                PSB[ob][:, :],
                                x_bf[:, ci, u * SUB:(u + 1) * SUB],
                                w_inkv_bf[:, ci, ob * 512:(ob + 1) * 512],
                                start=(ci == 0), stop=False)
                    for ob in range(4):
                        ins = e.matmul(
                            PSB[ob][:, :],
                            negmu_row.ap()[0:1, u * SUB:(u + 1) * SUB],
                            wsum_in.ap()[0:1, D + ob * 512:D + (ob + 1) * 512],
                            start=False, stop=True)
                    return ins
                P.step("tensor", kvproj, waits=waits, inc=PT)
                t_proj = P.now(PT)
                pt_proj_last = t_proj

                ew = [(PT, t_proj), (PV, rstd_marks[u])]
                if u >= 2:
                    ew += [(PV, featm[u - 2][0]), (PS, featm[u - 2][1]),
                           (PG, featm[u - 2][2])]
                if tkv_guard is not None:
                    ew.append(tkv_guard)
                for ob in range(2):
                    P.step("vector", lambda e, ob=ob, u=u: e.tensor_scalar(
                        k2_t[:, u % 2, ob * 512:(ob + 1) * 512], PSB[ob][:, :],
                        statc.ap()[:, 5, u:u + 1], None, ALU.mult),
                        waits=ew, inc=PV)
                    P.step("vector", lambda e, ob=ob, u=u: e.tensor_scalar(
                        k2_t[:, u % 2, ob * 512:(ob + 1) * 512],
                        k2_t[:, u % 2, ob * 512:(ob + 1) * 512],
                        2.0, -2.0, ALU.min, ALU.max), inc=PV)
                for ob in range(2, 4):
                    P.step("vector", lambda e, ob=ob, u=u: e.tensor_scalar(
                        v_bf[:, u, (ob - 2) * 512:(ob - 1) * 512],
                        PSB[ob][:, :], statc.ap()[:, 4, u:u + 1], None, ALU.mult),
                        waits=ew, inc=PV)
                evac_v = P.now(PV)
                for b in range(4):
                    bank_r(b, PV)

                k2s = k2_t[:, u % 2, :]
                tkv = tk_bf.ap()[:, u, :, :, :]
                k2r = k2s.rearrange("p (h d) -> p h d", h=H)
                f2r = f_p2.ap().rearrange("p (h d) -> p h d", h=H)
                f3r = f_p3.ap().rearrange("p (h d) -> p h d", h=H)
                f4r = f_p4.ap().rearrange("p (h d) -> p h d", h=H)
                f5r = f_p5.ap().rearrange("p (h d) -> p h d", h=H)
                fw = [(PV, evac_v)]
                if u >= 1:
                    fw += [(PV, featm[u - 1][0]), (PS, featm[u - 1][1]),
                           (PG, featm[u - 1][2])]
                elif feat_guard is not None:
                    fw += list(feat_guard)
                P.step("scalar", lambda e, k2s=k2s: e.activation(
                    f_p2[:, :], k2s, ACT.Square, scale=0.5), waits=fw, inc=PS)
                s2 = P.now(PS)
                P.step("vector", lambda e, k2s=k2s: e.scalar_tensor_tensor(
                    f_p3[:, :], f_p2[:, :], -0.75, k2s, ALU.add, ALU.mult),
                    waits=[(PS, s2)] + fw, inc=PV)
                v3 = P.now(PV)
                P.step("scalar", lambda e: e.activation(
                    f_p4[:, :], f_p2[:, :], ACT.Square, bias=-0.5),
                    waits=[(PS, s2)] + fw, inc=PS)
                s4 = P.now(PS)
                P.step("vector", lambda e: e.tensor_mul(
                    f_p5[:, :], f_p2[:, :], f_p3[:, :]),
                    waits=[(PV, v3)] + fw, inc=PV)
                v5 = P.now(PV)

                tw = [] if tkv_guard is None else [tkv_guard]
                P.step("gpsimd", lambda e, tkv=tkv, k2r=k2r: e.tensor_copy(
                    tkv[:, :, 0, :], k2r), waits=[(PV, evac_v)] + tw, inc=PG)
                P.step("gpsimd", lambda e, tkv=tkv, f2r=f2r: e.tensor_copy(
                    tkv[:, :, 1, :], f2r), waits=[(PS, s2)] + tw, inc=PG)
                P.step("gpsimd", lambda e, tkv=tkv, f3r=f3r: e.tensor_copy(
                    tkv[:, :, 2, :], f3r), waits=[(PV, v3)] + tw, inc=PG)
                P.step("vector", lambda e, tkv=tkv, f4r=f4r: e.tensor_copy(
                    tkv[:, :, 3, :], f4r), waits=[(PS, s4)] + tw, inc=PV)
                P.step("vector", lambda e, tkv=tkv, f5r=f5r: e.tensor_copy(
                    tkv[:, :, 4, :], f5r), waits=[(PV, v5)] + tw, inc=PV)
                P.step("scalar", lambda e, tkv=tkv, f3r=f3r: e.activation(
                    tkv[:, :, 5, :], f3r, ACT.Square),
                    waits=[(PV, v3)] + tw, inc=PS)
                P.step("scalar", lambda e, tkv=tkv, f4r=f4r: e.activation(
                    tkv[:, :, 7, :], f4r, ACT.Square),
                    waits=[(PS, s4)] + tw, inc=PS)
                P.step("scalar", lambda e, tkv=tkv, f5r=f5r: e.activation(
                    tkv[:, :, 9, :], f5r, ACT.Square),
                    waits=[(PV, v5)] + tw, inc=PS)
                P.step("vector", lambda e, tkv=tkv, f3r=f3r, f4r=f4r:
                       e.tensor_tensor(tkv[:, :, 6, :], f3r, f4r, ALU.mult),
                       waits=[(PS, s4)] + tw, inc=PV)
                P.step("vector", lambda e, tkv=tkv, f4r=f4r, f5r=f5r:
                       e.tensor_tensor(tkv[:, :, 8, :], f4r, f5r, ALU.mult),
                       inc=PV)
                featm.append((P.now(PV), P.now(PS), P.now(PG)))
                if u == 3:
                    feat_guard = [(PV, P.now(PV)), (PS, P.now(PS)),
                                  (PG, P.now(PG))]

            def vsmm(e):
                ins = None
                for u in range(4):
                    for nb in range(2):
                        ins = e.matmul(PSB[6 + nb][:, :], ones_lhs[:, :],
                                       v_bf[:, u, nb * 512:(nb + 1) * 512],
                                       start=(u == 0), stop=(u == 3))
                return ins
            P.step("tensor", vsmm,
                   waits=[(PV, featm[3][0])] + bank_w(6, 7), inc=PT)
            t_vs = P.now(PT)
            for nb in range(2):
                if t == 0:
                    fnv = (lambda e, nb=nb: e.tensor_copy(
                        vsum_acc[:, nb * 512:(nb + 1) * 512], PSB[6 + nb][:, :]))
                else:
                    fnv = (lambda e, nb=nb: e.tensor_tensor(
                        vsum_acc[:, nb * 512:(nb + 1) * 512], PSB[6 + nb][:, :],
                        vsum_acc[:, nb * 512:(nb + 1) * 512], ALU.add))
                P.step("vector", fnv, waits=[(PT, t_vs)], inc=PV)
            bank_r(6, PV)
            bank_r(7, PV)

            fw = [(PV, featm[3][0]), (PS, featm[3][1]), (PG, featm[3][2])]
            for h in range(H):
                b0 = 2 * (h % 3)
                b1 = b0 + 1
                half = h % 2
                row0 = half * 64
                def kvmm(e, h=h, b0=b0, b1=b1, row0=row0):
                    ins = None
                    for u in range(4):
                        lhs = v_bf[:, u, h * DH:(h + 1) * DH]
                        rhs = tk_bf.ap()[:, u, h, :, :].rearrange(
                            "p j d -> p (j d)")
                        ins = e.matmul(PSB[b0][row0:row0 + 64, :], lhs,
                                       rhs[:, 0:512], start=(u == 0),
                                       stop=(u == 3), tile_position=(0, row0))
                        ins = e.matmul(PSB[b1][row0:row0 + 64, 0:128], lhs,
                                       rhs[:, 512:640], start=(u == 0),
                                       stop=(u == 3), tile_position=(0, row0))
                    return ins
                P.step("tensor", kvmm, waits=fw + bank_w(b0, b1), inc=PT)
                t_kv = P.now(PT)
                accs = kvt_acc.ap()[row0:row0 + 64, h // 2, :]
                if t == 0:
                    f1 = (lambda e, b0=b0, row0=row0, accs=accs: e.tensor_copy(
                        accs[:, 0:512], PSB[b0][row0:row0 + 64, :]))
                    f2 = (lambda e, b1=b1, row0=row0, accs=accs: e.tensor_copy(
                        accs[:, 512:640], PSB[b1][row0:row0 + 64, 0:128]))
                else:
                    f1 = (lambda e, b0=b0, row0=row0, accs=accs: e.tensor_tensor(
                        accs[:, 0:512], PSB[b0][row0:row0 + 64, :],
                        accs[:, 0:512], ALU.add))
                    f2 = (lambda e, b1=b1, row0=row0, accs=accs: e.tensor_tensor(
                        accs[:, 512:640], PSB[b1][row0:row0 + 64, 0:128],
                        accs[:, 512:640], ALU.add))
                P.step("vector", f1, waits=[(PT, t_kv)], inc=PV)
                P.step("vector", f2, inc=PV)
                bank_r(b0, PV)
                bank_r(b1, PV)
            tkv_guard = (PT, P.now(PT))
            xbf_guard = (PT, P.now(PT))

        p1_t, p1_v, p1_s, p1_g = P.now(PT), P.now(PV), P.now(PS), P.now(PG)

        w_q_done = gdma(lambda e: e.dma_start(
            out=w_inq_bf[:, :, :],
            in_=w_inT.ap()[:, 0:D].rearrange("(c p) f -> p c f", p=128)),
            waits=[(PT, p1_t)])
        w_o_done = gdma(lambda e: e.dma_start(
            out=w_outT_bf[:, :, :],
            in_=w_outT.ap().rearrange("(c p) f -> p c f", p=128)),
            waits=[(PT, p1_t)])

        def wsum_q(e):
            ins = None
            for ci in range(8):
                for nb in range(2):
                    ins = e.matmul(PSB[nb][:, :], ones_lhs[:, :],
                                   w_inq_bf[:, ci, nb * 512:(nb + 1) * 512],
                                   start=(ci == 0), stop=(ci == 7))
            return ins
        P.step("tensor", wsum_q,
               waits=[(w_q_done[0], w_q_done[1])] + bank_w(0, 1), inc=PT)
        t_wq = P.now(PT)
        for nb in range(2):
            P.step("vector", lambda e, nb=nb: e.tensor_copy(
                wsum_in.ap()[0:1, nb * 512:(nb + 1) * 512], PSB[nb][0:1, :]),
                waits=[(PT, t_wq)], inc=PV)
            bank_r(nb, PV)
        v_wq = P.now(PV)

        P.step("vector", lambda e: e.memset(kvp_f[64:128, 5, :, :], 0.0),
               waits=[(PV, p1_v)], inc=PV)
        tr_b = 2
        for h in range(H):
            half = h % 2
            row0 = half * 64
            idn = iden_f32[row0:row0 + 64, row0:row0 + 64]
            P.step("vector", lambda e, h=h: e.tensor_copy(
                kvp_f[0:64, 0, h, :], vsum_acc.ap()[0:64, h * DH:(h + 1) * DH]),
                waits=[(PV, p1_v)], inc=PV)
            def tr0(e, h=h, row0=row0, idn=idn):
                return e.transpose(
                    PSB[tr_b][0:64, 0:64],
                    kvt_acc.ap()[row0:row0 + 64, h // 2, 0:64], idn)
            P.step("tensor", tr0, waits=[(PV, p1_v)] + bank_w(tr_b), inc=PT)
            ttr = P.now(PT)
            P.step("vector", lambda e, h=h: e.tensor_copy(
                kvp_f[64:128, 0, h, :], PSB[tr_b][0:64, 0:64]),
                waits=[(PT, ttr)], inc=PV)
            bank_r(tr_b, PV)
            for c in range(1, 6):
                jm0 = 2 * c - 1
                if c < 5:
                    def trc(e, h=h, row0=row0, jm0=jm0, idn=idn):
                        return e.transpose(
                            PSB[tr_b][:, 0:64],
                            kvt_acc.ap()[row0:row0 + 64, h // 2,
                                         jm0 * 64:(jm0 + 2) * 64], idn)
                    P.step("tensor", trc, waits=bank_w(tr_b), inc=PT)
                    ttr = P.now(PT)
                    P.step("vector", lambda e, h=h, c=c: e.tensor_copy(
                        kvp_f[:, c, h, :], PSB[tr_b][:, 0:64]),
                        waits=[(PT, ttr)], inc=PV)
                else:
                    def trc(e, h=h, row0=row0, jm0=jm0, idn=idn):
                        return e.transpose(
                            PSB[tr_b][0:64, 0:64],
                            kvt_acc.ap()[row0:row0 + 64, h // 2,
                                         jm0 * 64:(jm0 + 1) * 64], idn)
                    P.step("tensor", trc, waits=bank_w(tr_b), inc=PT)
                    ttr = P.now(PT)
                    P.step("vector", lambda e, h=h, c=c: e.tensor_copy(
                        kvp_f[0:64, c, h, :], PSB[tr_b][0:64, 0:64]),
                        waits=[(PT, ttr)], inc=PV)
                bank_r(tr_b, PV)
        v_kvp = P.now(PV)

        def emit_stage(EM, src_ap, write_evac, base_wait):
            for ci in range(6):
                cjs = [cj for cj in range(6)
                       if any(EM[2 * ci + a][2 * cj + b] != 0
                              for a in range(2) for b in range(2)
                              if 2 * ci + a < MJ and 2 * cj + b < MJ)]
                def btile(e, ci=ci, cjs=tuple(cjs)):
                    ins = e.memset(et32[:, :, :], 0.0)
                    e.drain()
                    for k, cj in enumerate(cjs):
                        for a in range(2):
                            for b in range(2):
                                mi, mj = 2 * ci + a, 2 * cj + b
                                if mi >= MJ or mj >= MJ:
                                    continue
                                val = float(EM[mi][mj])
                                if val == 0.0:
                                    continue
                                ins = e.tensor_scalar_mul(
                                    et32[b * 64:(b + 1) * 64, k,
                                         a * 64:(a + 1) * 64],
                                    iden_f32[b * 64:(b + 1) * 64,
                                             b * 64:(b + 1) * 64], val)
                    return ins
                P.step("vector", btile,
                       waits=[(PT, P.now(PT)), (PV, base_wait)], inc=PV)
                vb = P.now(PV)
                for h in range(H):
                    bk = h % 2
                    def emm(e, cjs=tuple(cjs), h=h, bk=bk):
                        ins = None
                        for k, cj in enumerate(cjs):
                            ins = e.matmul(
                                PSB[bk][:, 0:64], et32[:, k, :],
                                src_ap[:, cj, h, :],
                                start=(k == 0), stop=(k == len(cjs) - 1))
                        return ins
                    P.step("tensor", emm, waits=[(PV, vb)] + bank_w(bk), inc=PT)
                    write_evac(ci, h, P.now(PT), bk)
            return

        def evac_t1(ci, h, tmm, bk):
            P.step("vector", lambda e, ci=ci, h=h, bk=bk: e.tensor_scalar(
                t1_f[:, ci, h, :], PSB[bk][:, 0:64],
                beta_exp.ap()[:, ci * H + h:ci * H + h + 1], None, ALU.mult),
                waits=[(PT, tmm)], inc=PV)
            bank_r(bk, PV)

        emit_stage([[float(x) for x in row] for row in E_A64],
                   kvp_f.ap(), evac_t1, v_kvp)
        v_t1 = P.now(PV)

        def evac_kvp(ci, h, tmm, bk):
            half = h % 2
            for a in range(2):
                j = 2 * ci + a
                if j >= MJ:
                    continue
                P.step("vector",
                       lambda e, a=a, j=j, h=h, bk=bk, half=half: e.tensor_copy(
                           kvp_bf[half * 64:(half + 1) * 64, j, h // 2, :],
                           PSB[bk][a * 64:(a + 1) * 64, 0:64]),
                       waits=[(PT, tmm)], inc=PV)
            bank_r(bk, PV)

        emit_stage([[float(x) for x in row] for row in E_B64],
                   t1_f.ap(), evac_kvp, v_t1)
        v_kvpb = P.now(PV)

        x2_guard = [None, None]
        attn_guard = None
        featg = [None, None]
        y1_dma = []
        pending_y1 = [None]

        def emit_pending_y1():
            if pending_y1[0] is not None:
                tt_, vv_ = pending_y1[0]
                P.step("scalar", lambda e, t=tt_: e.dma_start(
                    out=y1d.ap()[:, t * T1:(t + 1) * T1]
                        .rearrange("(c p) s -> p c s", p=128),
                    in_=y1_t[:, :, :]), waits=[(PV, vv_)], inc=SST, inc_k=16)
                y1_dma.append(P.now(SST))
                pending_y1[0] = None

        for t in range(NT1):
            slot = t % 2
            waits = [(PV, v_kvpb), (PV, v_wq),
                     (w_o_done[0], w_o_done[1])]
            if x2_guard[slot] is not None:
                waits.append(x2_guard[slot])
            c0 = t * T1
            P.step("sync", lambda e, c0=c0, slot=slot: e.dma_start(
                out=x2_t[:, slot, :, :],
                in_=xT.ap()[:, c0:c0 + T1].rearrange("(c p) s -> p c s", p=128)),
                waits=waits, inc=SD, inc_k=16)
            dv = P.now(SD)
            cw = [(SD, dv), (PT, P.now(PT))]
            P.step("scalar", lambda e, slot=slot: e.activation(
                xb2[:, :, :], x2_t[:, slot, :, :], ACT.Copy), waits=cw, inc=PS)
            P.step("scalar", lambda e, slot=slot: e.activation(
                xsq2[:, :, :], x2_t[:, slot, :, :], ACT.Square), inc=PS)
            s_cast = P.now(PS)
            emit_pending_y1()

            def p2sum(e):
                ins = None
                for ci in range(8):
                    ins = e.matmul(PSB[6][:, :], ones_lhs[:, :], xb2[:, ci, :],
                                   start=(ci == 0), stop=(ci == 7))
                return ins
            def p2sq(e):
                ins = None
                for ci in range(8):
                    ins = e.matmul(PSB[7][:, :], ones_lhs[:, :], xsq2[:, ci, :],
                                   start=(ci == 0), stop=(ci == 7))
                return ins
            P.step("tensor", p2sum, waits=[(PS, s_cast)] + bank_w(6), inc=PT)
            P.step("tensor", p2sq, waits=bank_w(7), inc=PT)
            t_st = P.now(PT)
            P.step("vector", lambda e: e.tensor_scalar(
                mu_rep[:, :], PSB[6][:, :], 1.0 / 1024.0, None, ALU.mult),
                waits=[(PT, t_st), (PS, P.now(PS))], inc=PV)
            P.step("vector", lambda e: e.tensor_scalar(
                negmu_row[0:1, :], PSB[6][0:1, :], -1.0 / 1024.0, None,
                ALU.mult), inc=PV)
            bank_r(6, PV)
            P.step("vector", lambda e: e.tensor_scalar(
                msq_rep[:, :], PSB[7][:, :], 1.0 / 1024.0, None, ALU.mult),
                inc=PV)
            bank_r(7, PV)
            v_nmu2 = P.now(PV)
            P.step("vector", lambda e: e.scalar_tensor_tensor(
                rstd_rep[:, :], mu_rep[:, :], -1.0, mu_rep[:, :],
                ALU.mult, ALU.mult), waits=[(PV, P.now(PV))], inc=PV)
            P.step("vector", lambda e: e.tensor_add(
                rstd_rep[:, :], rstd_rep[:, :], msq_rep[:, :]),
                waits=[(PV, P.now(PV))], inc=PV)
            P.step("vector", lambda e: e.tensor_scalar(
                msq_rep[:, :], rstd_rep[:, :], float(EPS), None, ALU.add),
                waits=[(PV, P.now(PV))], inc=PV)
            vv = P.now(PV)
            P.step("scalar", lambda e: e.activation(
                mu_rep[:, :], msq_rep[:, :], ACT.Sqrt), waits=[(PV, vv)],
                inc=PS)
            sv = P.now(PS)
            def recip2_d(e):
                e.reciprocal(rstd_rep[:, :], mu_rep[:, :])
                return e.drain()
            P.step("vector", recip2_d, waits=[(PS, sv)], inc=PV)
            P.step("vector", lambda e: e.tensor_mul(
                mu_rep[:, :], rstd_rep[:, :], rstd_rep[:, :]),
                waits=[(PV, P.now(PV))], inc=PV)
            P.step("vector", lambda e: e.tensor_mul(
                mu_rep[:, :], mu_rep[:, :], msq_rep[:, :]),
                waits=[(PV, P.now(PV))], inc=PV)
            P.step("vector", lambda e: e.tensor_scalar(
                mu_rep[:, :], mu_rep[:, :], -0.5, 1.5, ALU.mult, ALU.add),
                waits=[(PV, P.now(PV))], inc=PV)
            P.step("vector", lambda e: e.tensor_mul(
                rstd_rep[:, :], rstd_rep[:, :], mu_rep[:, :]),
                waits=[(PV, P.now(PV))], inc=PV)
            v_rstd = P.now(PV)

            for hp in range(8):
                bq = hp % 2
                par = hp % 2
                q2s = q2h.ap()[:, par, :]
                gp2 = g_p2.ap()[:, par, :]
                gp3 = g_p3.ap()[:, par, :]
                gp4 = g_p4.ap()[:, par, :]
                gp5 = g_p5.ap()[:, par, :]
                pfs = pf_bf.ap()[:, par, :, :]
                def qproj(e, hp=hp, bq=bq):
                    ins = None
                    for ci in range(8):
                        ins = e.matmul(
                            PSB[bq][:, :],
                            w_inq_bf[:, ci, hp * 128:(hp + 1) * 128],
                            xb2[:, ci, :], start=(ci == 0), stop=False)
                    ins = e.matmul(
                        PSB[bq][:, :],
                        wsum_in.ap()[0:1, hp * 128:(hp + 1) * 128],
                        negmu_row.ap()[0:1, :], start=False, stop=True)
                    return ins
                P.step("tensor", qproj,
                       waits=[(PS, s_cast), (PV, v_nmu2)] + bank_w(bq), inc=PT)
                t_q = P.now(PT)
                qw = [(PT, t_q), (PV, v_rstd)]
                if featg[par] is not None:
                    qw += [featg[par]]
                P.step("vector", lambda e, bq=bq, q2s=q2s: e.scalar_tensor_tensor(
                    q2s, PSB[bq][:, :], 0.25, rstd_rep[:, :],
                    ALU.mult, ALU.mult), waits=qw, inc=PV)
                P.step("vector", lambda e, q2s=q2s: e.tensor_scalar(
                    q2s, q2s, 2.0, -2.0, ALU.min, ALU.max),
                    waits=[(PV, P.now(PV))], inc=PV)
                v_q2 = P.now(PV)
                bank_r(bq, PV)

                P.step("scalar", lambda e, q2s=q2s, gp2=gp2: e.activation(
                    gp2, q2s, ACT.Square, scale=0.5),
                    waits=[(PV, v_q2)], inc=PS)
                s2 = P.now(PS)
                P.step("vector", lambda e, q2s=q2s, gp2=gp2, gp3=gp3:
                       e.scalar_tensor_tensor(
                           gp3, gp2, -0.75, q2s, ALU.add, ALU.mult),
                       waits=[(PS, s2)], inc=PV)
                v3 = P.now(PV)
                P.step("scalar", lambda e, gp2=gp2, gp4=gp4: e.activation(
                    gp4, gp2, ACT.Square, bias=-0.5),
                    waits=[(PS, s2)], inc=PS)
                s4 = P.now(PS)
                P.step("vector", lambda e, gp2=gp2, gp3=gp3, gp5=gp5:
                       e.tensor_mul(gp5, gp2, gp3),
                       waits=[(PV, v3)], inc=PV)
                v5 = P.now(PV)
                P.step("gpsimd", lambda e, pfs=pfs, q2s=q2s: e.tensor_copy(
                    pfs[:, 0, :], q2s), waits=[(PV, v_q2)], inc=PG)
                P.step("gpsimd", lambda e, pfs=pfs, gp2=gp2: e.tensor_copy(
                    pfs[:, 1, :], gp2), waits=[(PS, s2)], inc=PG)
                P.step("gpsimd", lambda e, pfs=pfs, gp3=gp3: e.tensor_copy(
                    pfs[:, 2, :], gp3), waits=[(PV, v3)], inc=PG)
                P.step("vector", lambda e, pfs=pfs, gp4=gp4: e.tensor_copy(
                    pfs[:, 3, :], gp4), waits=[(PS, s4)], inc=PV)
                P.step("vector", lambda e, pfs=pfs, gp5=gp5: e.tensor_copy(
                    pfs[:, 4, :], gp5), waits=[(PV, v5)], inc=PV)
                P.step("scalar", lambda e, pfs=pfs, gp3=gp3: e.activation(
                    pfs[:, 5, :], gp3, ACT.Square), waits=[(PV, v3)], inc=PS)
                P.step("scalar", lambda e, pfs=pfs, gp4=gp4: e.activation(
                    pfs[:, 7, :], gp4, ACT.Square), waits=[(PS, s4)], inc=PS)
                P.step("scalar", lambda e, pfs=pfs, gp5=gp5: e.activation(
                    pfs[:, 9, :], gp5, ACT.Square), waits=[(PV, v5)], inc=PS)
                P.step("vector", lambda e, pfs=pfs, gp3=gp3, gp4=gp4:
                       e.tensor_tensor(pfs[:, 6, :], gp3, gp4, ALU.mult),
                       waits=[(PS, s4)], inc=PV)
                P.step("vector", lambda e, pfs=pfs, gp4=gp4, gp5=gp5:
                       e.tensor_tensor(pfs[:, 8, :], gp4, gp5, ALU.mult),
                       inc=PV)
                fv, fs, fg = P.now(PV), P.now(PS), P.now(PG)

                be = 2 + 2 * (hp % 2)
                def einsum(e, hp=hp, be=be, pfs=pfs):
                    ins = None
                    for j in range(MJ):
                        for half in range(2):
                            h = 2 * hp + half
                            lhs = kvp_bf.ap()[half * 64:(half + 1) * 64,
                                              j, h // 2, :]
                            rhs = (ones_rhs[half * 64:(half + 1) * 64, :]
                                   if j == 0 else
                                   pfs[half * 64:(half + 1) * 64, j - 1, :])
                            ins = e.matmul(
                                PSB[be + half][half * 64:half * 64 + 64, :],
                                lhs, rhs, start=(j == 0), stop=(j == MJ - 1),
                                tile_position=(half * 64, half * 64))
                    return ins
                P.step("tensor", einsum,
                       waits=[(PV, fv), (PS, fs), (PG, fg)]
                       + bank_w(be, be + 1), inc=PT)
                t_e = P.now(PT)
                featg[par] = (PT, t_e)
                aw = [(PT, t_e)]
                if attn_guard is not None:
                    aw.append(attn_guard)
                P.step("scalar", lambda e, hp=hp, be=be: e.activation(
                    attn_bf.ap()[0:64, hp, :], PSB[be][0:64, :], ACT.Copy),
                    waits=aw, inc=PS)
                P.step("scalar", lambda e, hp=hp, be=be: e.activation(
                    attn_bf.ap()[64:128, hp, :], PSB[be + 1][64:128, :],
                    ACT.Copy), inc=PS)
                bank_r(be, PS)
                bank_r(be + 1, PS)
            s_attn = P.now(PS)

            for oc in range(8):
                bw_ = 6 + (oc % 2)
                def womm(e, oc=oc, bw_=bw_):
                    ins = None
                    for ac in range(8):
                        ins = e.matmul(
                            PSB[bw_][:, :],
                            w_outT_bf[:, ac, oc * 128:(oc + 1) * 128],
                            attn_bf[:, ac, :], start=(ac == 0), stop=(ac == 7))
                    return ins
                P.step("tensor", womm,
                       waits=[(PS, s_attn)] + bank_w(bw_), inc=PT)
                t_w = P.now(PT)
                yw = [(PT, t_w)]
                if t > 0:
                    yw.append((SST, y1_dma[-1]))
                P.step("vector",
                       lambda e, oc=oc, bw_=bw_, slot=slot: e.tensor_tensor(
                           y1_t[:, oc, :], PSB[bw_][:, :],
                           x2_t[:, slot, oc, :], ALU.add),
                       waits=yw, inc=PV)
                bank_r(bw_, PV)
            v_y1 = P.now(PV)
            x2_guard[slot] = (PV, v_y1)
            attn_guard = (PT, P.now(PT))
            pending_y1[0] = (t, v_y1)

        emit_pending_y1()
        p2_t, p2_v, p2_s, p2_g = P.now(PT), P.now(PV), P.now(PS), P.now(PG)

        wfc_done = gdma(lambda e: e.dma_start(
            out=W_fcT_bf[:, :, :],
            in_=W_fcT.ap().rearrange("(c p) f -> p c f", p=128)),
            waits=[(PT, p2_t), (PV, p2_v), (PS, p2_s), (PG, p2_g)])
        wpj_done = gdma(lambda e: e.dma_start(
            out=W_projT_bf[:, :, :],
            in_=W_projT.ap().rearrange("(c p) f -> p c f", p=128)),
            waits=[(PT, p2_t), (PV, p2_v)])

        def wsumfc(e):
            ins = None
            for nb in range(8):
                for ci in range(8):
                    ins = e.matmul(PSB[nb][:, :], ones_lhs[:, :],
                                   W_fcT_bf[:, ci, nb * 512:(nb + 1) * 512],
                                   start=(ci == 0), stop=(ci == 7))
            return ins
        P.step("tensor", wsumfc,
               waits=[(wfc_done[0], wfc_done[1])] + bank_w(*range(8)), inc=PT)
        t_wfc = P.now(PT)
        for nb in range(8):
            P.step("vector", lambda e, nb=nb: e.tensor_copy(
                wsum_fc.ap()[0:1, nb * 512:(nb + 1) * 512], PSB[nb][0:1, :]),
                waits=[(PT, t_wfc)], inc=PV)
            bank_r(nb, PV)
        v_wfc = P.now(PV)

        y3_guard = [None, None]
        hsb_guard = None
        pending_out = [None]

        def emit_pending_out():
            if pending_out[0] is not None:
                tt_, slot_, vv_ = pending_out[0]
                P.step("scalar", lambda e, t=tt_, slot=slot_: e.dma_start(
                    out=outT.ap()[:, t * T3:(t + 1) * T3]
                        .rearrange("(c p) s -> p c s", p=128),
                    in_=y1_in[:, slot, :, :]), waits=[(PV, vv_)],
                    inc=SST, inc_k=16)
                y3_guard[slot_] = (SST, P.now(SST))
                pending_out[0] = None

        for t in range(NT3):
            slot = t % 2
            waits = [(PV, v_wfc), (wpj_done[0], wpj_done[1]),
                     (SST, y1_dma[t // 2])]
            if y3_guard[slot] is not None:
                waits.append(y3_guard[slot])
            c0 = t * T3
            P.step("sync", lambda e, c0=c0, slot=slot: e.dma_start(
                out=y1_in[:, slot, :, :],
                in_=y1d.ap()[:, c0:c0 + T3].rearrange("(c p) s -> p c s", p=128)),
                waits=waits, inc=SD, inc_k=16)
            dv = P.now(SD)
            cw = [(SD, dv), (PT, P.now(PT))]
            P.step("scalar", lambda e, slot=slot: e.activation(
                yb3[:, :, :], y1_in[:, slot, :, :], ACT.Copy), waits=cw, inc=PS)
            P.step("scalar", lambda e, slot=slot: e.activation(
                ysq3[:, :, :], y1_in[:, slot, :, :], ACT.Square), inc=PS)
            s_cast = P.now(PS)
            emit_pending_out()
            def p3sum(e):
                ins = None
                for ci in range(8):
                    ins = e.matmul(PSB[6][:, 0:T3], ones_lhs[:, :],
                                   yb3[:, ci, :], start=(ci == 0), stop=(ci == 7))
                return ins
            def p3sq(e):
                ins = None
                for ci in range(8):
                    ins = e.matmul(PSB[7][:, 0:T3], ones_lhs[:, :],
                                   ysq3[:, ci, :], start=(ci == 0), stop=(ci == 7))
                return ins
            P.step("tensor", p3sum, waits=[(PS, s_cast)] + bank_w(6), inc=PT)
            P.step("tensor", p3sq, waits=bank_w(7), inc=PT)
            t_st = P.now(PT)
            P.step("vector", lambda e: e.tensor_scalar(
                mu3_rep[:, :], PSB[6][:, 0:T3], 1.0 / 1024.0, None, ALU.mult),
                waits=[(PT, t_st)], inc=PV)
            P.step("vector", lambda e: e.tensor_scalar(
                nmu3_row[0:1, :], PSB[6][0:1, 0:T3], -1.0 / 1024.0, None,
                ALU.mult), inc=PV)
            bank_r(6, PV)
            P.step("vector", lambda e: e.tensor_scalar(
                msq3_rep[:, :], PSB[7][:, 0:T3], 1.0 / 1024.0, None, ALU.mult),
                inc=PV)
            bank_r(7, PV)
            v_nmu3 = P.now(PV)
            P.step("vector", lambda e: e.scalar_tensor_tensor(
                rsq_rep[:, :], mu3_rep[:, :], -1.0, mu3_rep[:, :],
                ALU.mult, ALU.mult), waits=[(PV, P.now(PV))], inc=PV)
            P.step("vector", lambda e: e.tensor_add(
                rsq_rep[:, :], rsq_rep[:, :], msq3_rep[:, :]),
                waits=[(PV, P.now(PV))], inc=PV)
            P.step("vector", lambda e: e.tensor_scalar(
                msq3_rep[:, :], rsq_rep[:, :], float(EPS), None, ALU.add),
                waits=[(PV, P.now(PV))], inc=PV)
            def recip3_d(e):
                e.reciprocal(rsq_rep[:, :], msq3_rep[:, :])
                return e.drain()
            P.step("vector", recip3_d, waits=[(PV, P.now(PV))], inc=PV)
            v_rsq = P.now(PV)

            for half in range(2):
                h_marks = []
                for fc in range(16):
                    fci = half * 16 + fc
                    bk = fc % 2
                    def fcmm(e, fci=fci, bk=bk):
                        ins = None
                        for ci in range(8):
                            ins = e.matmul(
                                PSB[bk][:, 0:T3],
                                W_fcT_bf[:, ci, fci * 128:(fci + 1) * 128],
                                yb3[:, ci, :], start=(ci == 0), stop=False)
                        ins = e.matmul(
                            PSB[bk][:, 0:T3],
                            wsum_fc.ap()[0:1, fci * 128:(fci + 1) * 128],
                            nmu3_row.ap()[0:1, :], start=False, stop=True)
                        return ins
                    P.step("tensor", fcmm,
                           waits=[(PS, s_cast), (PV, v_nmu3)] + bank_w(bk),
                           inc=PT)
                    t_fc = P.now(PT)
                    P.step("scalar", lambda e, bk=bk: e.activation(
                        a_sb[:, bk, :], PSB[bk][:, 0:T3], ACT.Relu),
                        waits=[(PT, t_fc)], inc=PS)
                    sa = P.now(PS)
                    bank_r(bk, PS)
                    hw = [(PS, sa)]
                    if hsb_guard is not None:
                        hw.append(hsb_guard)
                    P.step("vector", lambda e, bk=bk, fc=fc: e.tensor_tensor(
                        h_sb[:, fc, :], a_sb[:, bk, :], a_sb[:, bk, :],
                        ALU.mult), waits=hw, inc=PV)
                    h_marks.append(P.now(PV))
                v_h = h_marks[-1]
                for oc in range(8):
                    bk = 2 + (oc % 4)
                    def pjmm(e, oc=oc, bk=bk, half=half):
                        ins = None
                        for fc in range(16):
                            fci = half * 16 + fc
                            ins = e.matmul(
                                PSB[bk][:, 0:T3],
                                W_projT_bf[:, fci, oc * 128:(oc + 1) * 128],
                                h_sb[:, fc, :], start=(fc == 0), stop=(fc == 15))
                        return ins
                    P.step("tensor", pjmm,
                           waits=[(PV, v_h), (wpj_done[0], wpj_done[1])]
                           + bank_w(bk), inc=PT)
                    t_pj = P.now(PT)
                    if half == 0:
                        P.step("vector", lambda e, oc=oc, bk=bk: e.tensor_copy(
                            y2_sb[:, oc, :], PSB[bk][:, 0:T3]),
                            waits=[(PT, t_pj)], inc=PV)
                    else:
                        P.step("vector",
                               lambda e, oc=oc, bk=bk: e.tensor_tensor(
                                   y2_sb[:, oc, :], PSB[bk][:, 0:T3],
                                   y2_sb[:, oc, :], ALU.add),
                               waits=[(PT, t_pj)], inc=PV)
                    bank_r(bk, PV)
                hsb_guard = (PT, P.now(PT))
            for oc in range(8):
                P.step("vector", lambda e, oc=oc: e.tensor_tensor(
                    y2_sb[:, oc, :], y2_sb[:, oc, :], rsq_rep[:, :], ALU.mult),
                    waits=[(PV, v_rsq)], inc=PV)
            v_y2 = P.now(PV)

            y2f = y2_sb.ap().rearrange("p a b -> p (a b)")
            mf = m_sb.ap().rearrange("p a b -> p (a b)")
            sf = spk_sb.ap().rearrange("p a b -> p (a b)")
            y1f = y1_in.ap()[:, slot, :, :].rearrange("p a b -> p (a b)")
            P.step("vector", lambda e, y2f=y2f, sf=sf: e.tensor_single_scalar(
                sf, y2f, 1.0, ALU.is_gt), waits=[(PV, v_y2)], inc=PV)
            P.step("vector", lambda e, y1f=y1f, sf=sf: e.scalar_tensor_tensor(
                y1f, sf, 0.2, y1f, ALU.mult, ALU.add), inc=PV)
            P.step("vector", lambda e, y2f=y2f, mf=mf: e.scalar_tensor_tensor(
                mf, y2f, 0.9, y2f, ALU.mult, ALU.add), inc=PV)
            P.step("vector", lambda e, mf=mf, sf=sf: e.tensor_sub(
                mf, mf, sf), inc=PV)
            for stp in range(4):
                P.step("vector", lambda e, mf=mf, sf=sf: e.tensor_single_scalar(
                    sf, mf, 1.0, ALU.is_gt), inc=PV)
                P.step("vector",
                       lambda e, y1f=y1f, sf=sf: e.scalar_tensor_tensor(
                           y1f, sf, 0.2, y1f, ALU.mult, ALU.add), inc=PV)
                if stp < 3:
                    P.step("vector",
                           lambda e, mf=mf, y2f=y2f: e.scalar_tensor_tensor(
                               mf, mf, 0.9, y2f, ALU.mult, ALU.add), inc=PV)
                    P.step("vector", lambda e, mf=mf, sf=sf: e.tensor_sub(
                        mf, mf, sf), inc=PV)
            v_out = P.now(PV)

            pending_out[0] = (t, slot, v_out)

        emit_pending_out()
        P.step("sync", lambda e: e.nop(),
               waits=[(SD, P.now(SD)), (SST, P.now(SST))])

        block = stk.enter_context(nc.Block())
        P.emit(block)
    return nc


_NC_CACHE = None


def kernel(L0_attn_x, L0_attn_w_in, L0_attn_w_out, L0_attn_beta,
           L0_ffn_W_fc, L0_ffn_W_proj):
    global _NC_CACHE
    x = np.asarray(L0_attn_x, dtype=np.float32)
    w_in = np.ascontiguousarray(np.asarray(L0_attn_w_in, np.float32).T)
    w_out = np.ascontiguousarray(np.asarray(L0_attn_w_out, np.float32).T)
    beta = np.ascontiguousarray(np.asarray(L0_attn_beta, np.float32).T)
    W_fc = np.ascontiguousarray(np.asarray(L0_ffn_W_fc, np.float32).T)
    W_proj = np.ascontiguousarray(np.asarray(L0_ffn_W_proj, np.float32).T)
    if _NC_CACHE is None:
        _NC_CACHE = build_nc()
    nc = _NC_CACHE
    in_maps = []
    for c in range(8):
        in_maps.append({
            "xT": np.ascontiguousarray(x[c].T),
            "w_inT": w_in, "w_outT": w_out, "betaT": beta,
            "W_fcT": W_fc, "W_projT": W_proj,
        })
    res = run_bass_kernel_spmd(nc, in_maps, core_ids=list(range(8)))
    out = np.empty((8, S, D), dtype=np.float32)
    for c in range(8):
        out[c] = res.results[c]["outT"].T
    return out


# version 05
# speedup vs baseline: 1.1069x; 1.0408x over previous
import contextlib
from fractions import Fraction

import numpy as np
import concourse.bass as bass
import concourse.mybir as mybir
from concourse.bass_utils import run_bass_kernel_spmd

F32 = mybir.dt.float32
BF16 = mybir.dt.bfloat16
ALU = mybir.AluOpType
ACT = mybir.ActivationFunctionType

D = 1024
S = 4096
H = 16
DH = 64
MJ = 11
NT1 = 8
T1 = 512
SUB = 128
NT3 = 16
T3 = 256
FD = 4096
EPS = 1e-5

ENGINES = ("tensor", "vector", "scalar", "gpsimd", "sync")


def _basis_matrices():
    def cmul(a, b):
        out = {}
        for i, ai in a.items():
            for j, bj in b.items():
                for k in (i + j, abs(i - j)):
                    out[k] = out.get(k, Fraction(0)) + ai * bj / 2
        return {k: v for k, v in out.items() if v != 0}

    def cadd(a, b, sa=Fraction(1), sb=Fraction(1)):
        out = {}
        for k, v in a.items():
            out[k] = out.get(k, Fraction(0)) + sa * v
        for k, v in b.items():
            out[k] = out.get(k, Fraction(0)) + sb * v
        return {k: v for k, v in out.items() if v != 0}

    T0, T1_ = {0: Fraction(1)}, {1: Fraction(1)}
    P = [None] * MJ
    P[0] = T0
    P[1] = cadd(T1_, {}, Fraction(2))
    P[2] = cmul(cadd(P[1], {}, Fraction(1, 2)), cadd(P[1], {}, Fraction(1, 2)))
    P[3] = cmul(cadd(P[2], T0, Fraction(1), Fraction(-3, 4)), P[1])
    P[4] = cmul(cadd(P[2], T0, Fraction(1), Fraction(-1, 2)),
                cadd(P[2], T0, Fraction(1), Fraction(-1, 2)))
    P[5] = cmul(P[2], P[3])
    P[6] = cmul(P[3], P[3])
    P[7] = cmul(P[3], P[4])
    P[8] = cmul(P[4], P[4])
    P[9] = cmul(P[4], P[5])
    P[10] = cmul(P[5], P[5])
    Mq = [[P[j].get(m, Fraction(0)) for m in range(MJ)] for j in range(MJ)]
    n = MJ
    Aug = [row[:] + [Fraction(int(i == j)) for j in range(n)]
           for i, row in enumerate(Mq)]
    for c in range(n):
        p = next(r for r in range(c, n) if Aug[r][c] != 0)
        Aug[c], Aug[p] = Aug[p], Aug[c]
        piv = Aug[c][c]
        Aug[c] = [v / piv for v in Aug[c]]
        for r in range(n):
            if r != c and Aug[r][c] != 0:
                f = Aug[r][c]
                Aug[r] = [vr - f * vc for vr, vc in zip(Aug[r], Aug[c])]
    Minv = [row[n:] for row in Aug]
    E_A = np.array([[float(Minv[j][m]) for m in range(MJ)] for j in range(MJ)],
                   dtype=np.float64)
    return E_A, E_A.T.copy()


E_A64, E_B64 = _basis_matrices()


class Prog:

    def __init__(self, nc, stk):
        self.nc = nc
        self.steps = {e: [] for e in ENGINES}
        self._sems = {}
        self._stk = stk
        self.counts = {}

    def sem(self, name):
        if name not in self._sems:
            self._sems[name] = self._stk.enter_context(self.nc.semaphore(name))
            self.counts[name] = 0
        return self._sems[name]

    def step(self, eng, fn, waits=(), inc=None, inc_k=1):
        w = []
        seen = {}
        for (s, v) in waits:
            if v is None:
                continue
            self.sem(s)
            if s not in seen or seen[s] < v:
                seen[s] = v
        w = [(self._sems[s], v) for s, v in seen.items()]
        if inc is not None:
            self.sem(inc)
            self.counts[inc] += inc_k
        self.steps[eng].append((fn, w, (self._sems[inc] if inc else None), inc_k))
        return self.counts[inc] if inc else None

    def now(self, sem_name):
        self.sem(sem_name)
        return self.counts[sem_name]

    def emit(self, block):
        for ename in ENGINES:
            steps = self.steps[ename]
            if not steps:
                continue

            def body(eng, steps=steps):
                for fn, waits, inc_sem, inc_k in steps:
                    for (sem, val) in waits:
                        eng.wait_ge(sem, val)
                    ins = fn(eng)
                    if inc_sem is not None:
                        assert ins is not None
                        ins.then_inc(inc_sem, inc_k)
            getattr(block, ename)(body)


def ceil32(x):
    return (x + 31) & ~31


def build_nc():
    nc = bass.Bass("TRN2", target_bir_lowering=False,
                   detect_race_conditions=False)

    xT = nc.dram_tensor("xT", [D, S], F32, kind="ExternalInput")
    w_inT = nc.dram_tensor("w_inT", [D, 3 * D], F32, kind="ExternalInput")
    w_outT = nc.dram_tensor("w_outT", [D, D], F32, kind="ExternalInput")
    betaT = nc.dram_tensor("betaT", [MJ, H], F32, kind="ExternalInput")
    W_fcT = nc.dram_tensor("W_fcT", [D, FD], F32, kind="ExternalInput")
    W_projT = nc.dram_tensor("W_projT", [FD, D], F32, kind="ExternalInput")
    outT = nc.dram_tensor("outT", [D, S], F32, kind="ExternalOutput")
    y1d = nc.dram_tensor("y1scratch", [D, S], F32, kind="Internal")

    EA32 = E_A64.astype(np.float32)
    assert np.array_equal(EA32.astype(np.float64), E_A64), "E_A not fp32 exact"

    with contextlib.ExitStack() as stk:
        P = Prog(nc, stk)
        BASE = 16512
        LIMIT = 212832
        assert BASE >= nc.sbuf_base and BASE + LIMIT <= nc.sbuf_top
        placed = []

        def at(name, off, shape, dtype):
            nbytes = int(np.prod(shape[1:])) * mybir.dt.size(dtype)
            assert off + nbytes <= LIMIT, f"{name}: {off}+{nbytes} > {LIMIT}"
            placed.append((name, off, off + nbytes))
            return nc.alloc_sbuf_tensor_at(name, list(shape), dtype,
                                           offset=BASE + off)

        w_inkv_bf = at("w_inkv_bf", 0, [128, 8, 2 * D], BF16)
        w_inq_bf = at("w_inq_bf", 0, [128, 8, D], BF16)
        w_outT_bf = at("w_outT_bf", 16384, [128, 8, D], BF16)
        W_fcT_bf = at("W_fcT_bf", 0, [128, 8, FD], BF16)
        o = 202272
        beta_exp = at("beta_exp", o, [128, 6 * H], F32); o += ceil32(6 * H * 4)
        iden_f32 = at("iden_f32", o, [128, 128], F32); o += 512
        iota_f32 = at("iota_f32", o, [128, 128], BF16); o += 256
        ones_rhs = at("ones_rhs", o, [128, 512], BF16); o += 1024
        ones_row = at("ones_row", o, [1, 64], F32); o += 256
        beta_row = at("beta_row", o, [1, MJ * H], F32); o += ceil32(MJ * H * 4)
        wsum_in = at("wsum_in", o, [1, 3 * D], BF16); o += ceil32(3 * D * 2)
        negmu_row = at("negmu_row", o, [1, T1], BF16); o += ceil32(T1 * 2)
        statc = at("statc", o, [128, 8, 4], F32); o += 4 * 8 * 4
        cpool = at("cpool", o, [128, 2], F32); o += 32
        C_END = o
        assert C_END <= LIMIT, f"C region: {C_END}"
        C0 = 202272

        E0 = 32768
        p = E0
        x_roll = at("x_roll", p, [128, 2, 8, SUB], F32); p += 2 * 8 * SUB * 4
        x_bf = at("x_bf", p, [128, 8, T1], BF16); p += 8 * T1 * 2
        x2_bf = at("x2_bf", p, [128, 8, T1], BF16); p += 8 * T1 * 2
        k2_t = at("k2_t", p, [128, 2, D], F32); p += 2 * D * 4
        f_p2 = at("f_p2", p, [128, D], F32); p += D * 4
        f_p3 = at("f_p3", p, [128, D], F32); p += D * 4
        f_p4 = at("f_p4", p, [128, D], F32); p += D * 4
        f_p5 = at("f_p5", p, [128, D], F32); p += D * 4
        v_bf = at("v_bf", p, [128, 4, D], BF16); p += 4 * D * 2
        tk_bf = at("tk_bf", p, [128, 4, H, 10, DH], BF16); p += 4 * H * 10 * DH * 2
        kvt_acc = at("kvt_acc", p, [128, 8, 10 * DH], F32); p += 8 * 10 * DH * 4
        vsum_acc = at("vsum_acc", p, [128, H * DH], F32); p += H * DH * 4
        P1_END = p
        assert P1_END <= C0, f"P1 over: {P1_END}"

        p = E0
        kvp_f = at("kvp_f", p, [128, 6, H, DH], F32); p += 6 * H * DH * 4
        t1_f = at("t1_f", p, [128, 6, H, DH], F32); p += 6 * H * DH * 4
        et32 = at("et32", p, [128, 6, 128], F32); p += 6 * 128 * 4
        TR_END = p
        tk_off = [x for x in placed if x[0] == "tk_bf"][0][1]
        assert TR_END <= tk_off, "TR scratch hits tk"

        p = E0
        x2_t = at("x2_t", p, [128, 2, 8, T1], F32); p += 2 * 8 * T1 * 4
        xb2 = at("xb2", p, [128, 8, T1], BF16); p += 8 * T1 * 2
        xsq2 = at("xsq2", p, [128, 8, T1], BF16); p += 8 * T1 * 2
        q2h = at("q2h", p, [128, 2, T1], F32); p += 2 * T1 * 4
        g_p2 = at("g_p2", p, [128, 2, T1], F32); p += 2 * T1 * 4
        g_p3 = at("g_p3", p, [128, 2, T1], F32); p += 2 * T1 * 4
        g_p4 = at("g_p4", p, [128, 2, T1], F32); p += 2 * T1 * 4
        g_p5 = at("g_p5", p, [128, 2, T1], F32); p += 2 * T1 * 4
        pf_bf = at("pf_bf", p, [128, 2, 10, T1], BF16); p += 2 * 10 * T1 * 2
        attn_bf = at("attn_bf", p, [128, 8, T1], BF16); p += 8 * T1 * 2
        y1_t = at("y1_t", p, [128, 8, T1], F32); p += 8 * T1 * 4
        rstd_rep = at("rstd_rep", p, [128, T1], F32); p += T1 * 4
        mu_rep = at("mu_rep", p, [128, T1], F32); p += T1 * 4
        msq_rep = at("msq_rep", p, [128, T1], F32); p += T1 * 4
        kvp_bf = at("kvp_bf", p, [128, MJ, 8, DH], BF16); p += MJ * 8 * DH * 2
        P2_END = p
        assert P2_END <= C0, f"P2 over: {P2_END}"
        kvpbf_off = [x for x in placed if x[0] == "kvp_bf"][0][1]
        assert kvpbf_off >= TR_END, "kvp_bf overlaps TR scratch"

        p = 65536
        W_projT_bf = at("W_projT_bf", p, [128, 32, D], BF16); p += 32 * D * 2
        y1_in = at("y1_in", p, [128, 2, 8, T3], F32); p += 2 * 8 * T3 * 4
        yb3 = at("yb3", p, [128, 8, T3], BF16); p += 8 * T3 * 2
        h_sb = at("h_sb", p, [128, 16, T3], BF16); p += 16 * T3 * 2
        ysq3 = at("ysq3", p, [128, 8, T3], BF16); p += 8 * T3 * 2
        y2_sb = at("y2_sb", p, [128, 8, T3], F32); p += 8 * T3 * 4
        m_sb = at("m_sb", p, [128, 8, T3], F32); p += 8 * T3 * 4
        spk_sb = at("spk_sb", p, [128, 8, T3], F32); p += 8 * T3 * 4
        a_sb = at("a_sb", p, [128, 2, T3], F32); p += 2 * T3 * 4
        rsq_rep = at("rsq_rep", p, [128, T3], F32); p += T3 * 4
        mu3_rep = at("mu3_rep", p, [128, T3], F32); p += T3 * 4
        msq3_rep = at("msq3_rep", p, [128, T3], F32); p += T3 * 4
        nmu3_row = at("nmu3_row", p, [1, T3], BF16); p += ceil32(T3 * 2)
        wsum_fc = at("wsum_fc", p, [1, FD], BF16); p += ceil32(FD * 2)
        P3_END = p
        assert P3_END <= C0, f"P3 over: {P3_END}"

        PSB = [nc.alloc_psum_tensor(f"psb{i}", [128, 512], F32) for i in range(8)]

        SD = "sdma"
        SDX = ["sdx0", "sdx1"]
        SDQ = ["sdq0", "sdq1"]
        SDY = ["sdy0", "sdy1"]
        SSO = ["sso0", "sso1"]
        SST = "sdst"
        GP = ["gdma0", "gdma1", "gdma2", "gdma3"]
        PT, PV, PS, PG = "pt", "pv", "ps", "pg"
        gp_i = [0]
        gp_pending = {}
        bank_guard = {}

        def gdma(fn, waits=()):
            s = GP[gp_i[0] % len(GP)]
            gp_i[0] += 1
            w = list(waits)
            if s in gp_pending:
                w.append((s, gp_pending[s]))
            P.step("gpsimd", fn, waits=w, inc=s, inc_k=16)
            gp_pending[s] = P.now(s)
            return (s, P.now(s))

        def bank_w(*banks):
            return [bank_guard[b] for b in banks if b in bank_guard]

        def bank_r(b, sem_name):
            bank_guard[b] = (sem_name, P.now(sem_name))

        w_kv_done = gdma(lambda e: e.dma_start(
            out=w_inkv_bf[:, :, :],
            in_=w_inT.ap()[:, D:3 * D].rearrange("(c p) f -> p c f", p=128)))
        beta_done = gdma(lambda e: e.dma_start(
            out=beta_row[0:1, :],
            in_=betaT.ap().rearrange("m h -> (m h)")))

        P.step("gpsimd", lambda e: e.iota(
            iota_f32[:, :], [[1, 128]], channel_multiplier=-1,
            allow_small_or_imprecise_dtypes=True), inc=PG)
        g_iota = P.now(PG)
        P.step("vector", lambda e: e.tensor_single_scalar(
            iden_f32[:, :], iota_f32[:, :], 0.0, ALU.is_equal),
            waits=[(PG, g_iota)], inc=PV)
        P.step("vector", lambda e: e.memset(ones_rhs[:, :], 1.0), inc=PV)
        P.step("vector", lambda e: e.memset(ones_row[0:1, :], 1.0), inc=PV)
        P.step("vector", lambda e: e.memset(cpool[:, 0:1], float(EPS)), inc=PV)
        P.step("vector", lambda e: e.memset(cpool[:, 1:2], -0.5), inc=PV)
        nc.const_aps.register(F32, float(EPS), cpool.ap()[:, 0:1])
        nc.const_aps.register(F32, -0.5, cpool.ap()[:, 1:2])
        consts_v = P.now(PV)

        for c in range(6):
            for half in range(2):
                m = 2 * c + half
                dst = (c * H, (c + 1) * H)
                if m >= MJ:
                    P.step("vector", lambda e, half=half, dst=dst: e.memset(
                        beta_exp[half * 64:(half + 1) * 64, dst[0]:dst[1]], 0.0),
                        inc=PV)
                    continue
                P.step("tensor", lambda e, m=m: e.matmul(
                    PSB[0][0:64, 0:H], ones_row[0:1, :],
                    beta_row.ap()[0:1, m * H:(m + 1) * H],
                    start=True, stop=True),
                    waits=[(PV, consts_v), (beta_done[0], beta_done[1])]
                    + bank_w(0), inc=PT)
                tb = P.now(PT)
                P.step("vector", lambda e, half=half, dst=dst: e.tensor_copy(
                    beta_exp[half * 64:(half + 1) * 64, dst[0]:dst[1]],
                    PSB[0][0:64, 0:H]), waits=[(PT, tb)], inc=PV)
                bank_r(0, PV)
        bexp_v = P.now(PV)

        def wsum_kv(e):
            ins = None
            for ci in range(8):
                for nb in range(4):
                    ins = e.matmul(PSB[nb][:, :], ones_rhs[:, 0:128],
                                   w_inkv_bf[:, ci, nb * 512:(nb + 1) * 512],
                                   start=(ci == 0), stop=(ci == 7))
            return ins
        P.step("tensor", wsum_kv,
               waits=[(w_kv_done[0], w_kv_done[1]), (PV, consts_v)]
               + bank_w(0, 1, 2, 3), inc=PT)
        t_wkv = P.now(PT)
        for nb in range(4):
            P.step("vector", lambda e, nb=nb: e.tensor_copy(
                wsum_in.ap()[0:1, D + nb * 512:D + (nb + 1) * 512],
                PSB[nb][0:1, :]), waits=[(PT, t_wkv)], inc=PV)
            bank_r(nb, PV)
        v_wkv = P.now(PV)

        x_guard = [None, None]
        xbf_guard = None
        feat_guard = None
        tkv_guard = None

        for t in range(NT1):
            cast_marks = []
            for u in range(4):
                slot = u % 2
                waits = []
                if x_guard[slot] is not None:
                    waits.append(x_guard[slot])
                c0 = t * T1 + u * SUB
                P.step("sync", lambda e, c0=c0, slot=slot: e.dma_start(
                    out=x_roll[:, slot, :, :],
                    in_=xT.ap()[:, c0:c0 + SUB].rearrange("(c p) s -> p c s", p=128)),
                    waits=waits, inc=SDX[slot], inc_k=16)
                dv = P.now(SDX[slot])
                w2 = [(SDX[slot], dv)]
                if xbf_guard is not None:
                    w2.append(xbf_guard)
                P.step("scalar", lambda e, u=u, slot=slot: e.activation(
                    x_bf.ap()[:, :, u * SUB:(u + 1) * SUB],
                    x_roll[:, slot, :, :], ACT.Copy), waits=w2, inc=PS)
                P.step("scalar", lambda e, u=u, slot=slot: e.activation(
                    x2_bf.ap()[:, :, u * SUB:(u + 1) * SUB],
                    x_roll[:, slot, :, :], ACT.Square), inc=PS)
                x_guard[slot] = (PS, P.now(PS))
                cast_marks.append(P.now(PS))
            casts_done = cast_marks[-1]

            def fsum(e):
                ins = None
                for ci in range(8):
                    ins = e.matmul(PSB[4][:, :], ones_rhs[:, 0:128], x_bf[:, ci, :],
                                   start=(ci == 0), stop=(ci == 7))
                return ins
            P.step("tensor", fsum, waits=[(PS, casts_done)] + bank_w(4), inc=PT)
            t_sum = P.now(PT)
            P.step("vector", lambda e: e.tensor_scalar(
                negmu_row[0:1, :], PSB[4][0:1, :], -1.0 / 1024.0, None, ALU.mult),
                waits=[(PT, t_sum)], inc=PV)
            bank_r(4, PV)
            v_nmu = P.now(PV)

            for u in range(4):
                def tsum(e, u=u):
                    ins = None
                    for ci in range(8):
                        ins = e.matmul(PSB[5][:, 0:1],
                                       x_bf[:, ci, u * SUB:(u + 1) * SUB],
                                       ones_rhs[:, 0:1],
                                       start=(ci == 0), stop=(ci == 7))
                    return ins
                P.step("tensor", tsum,
                       waits=[(PS, cast_marks[u])] + bank_w(5), inc=PT)
                tv = P.now(PT)
                P.step("vector", lambda e, u=u: e.tensor_scalar(
                    statc[:, 0, u:u + 1], PSB[5][:, 0:1], 1.0 / 1024.0, None,
                    ALU.mult), waits=[(PT, tv)], inc=PV)
                bank_r(5, PV)
                def tsq(e, u=u):
                    ins = None
                    for ci in range(8):
                        ins = e.matmul(PSB[5][:, 0:1],
                                       x2_bf[:, ci, u * SUB:(u + 1) * SUB],
                                       ones_rhs[:, 0:1],
                                       start=(ci == 0), stop=(ci == 7))
                    return ins
                P.step("tensor", tsq, waits=bank_w(5), inc=PT)
                tv2 = P.now(PT)
                P.step("vector", lambda e, u=u: e.tensor_scalar(
                    statc[:, 1, u:u + 1], PSB[5][:, 0:1], 1.0 / 1024.0, None,
                    ALU.mult), waits=[(PT, tv2)], inc=PV)
                bank_r(5, PV)
            P.step("vector", lambda e: e.scalar_tensor_tensor(
                statc[:, 2, :], statc[:, 0, :], -1.0, statc[:, 0, :],
                ALU.mult, ALU.mult), waits=[(PV, P.now(PV))], inc=PV)
            P.step("vector", lambda e: e.tensor_add(
                statc[:, 2, :], statc[:, 2, :], statc[:, 1, :]),
                waits=[(PV, P.now(PV))], inc=PV)
            P.step("vector", lambda e: e.tensor_scalar(
                statc[:, 6, :], statc[:, 2, :], float(EPS), None, ALU.add),
                waits=[(PV, P.now(PV))], inc=PV)
            vva = P.now(PV)
            P.step("scalar", lambda e: e.activation(
                statc[:, 3, :], statc[:, 6, :], ACT.Sqrt),
                waits=[(PV, vva)], inc=PS)
            sva = P.now(PS)
            def recip_d(e):
                e.reciprocal(statc[:, 4, :], statc[:, 3, :])
                return e.drain()
            P.step("vector", recip_d, waits=[(PS, sva)], inc=PV)
            P.step("vector", lambda e: e.tensor_mul(
                statc[:, 7, :], statc[:, 4, :], statc[:, 4, :]),
                waits=[(PV, P.now(PV))], inc=PV)
            P.step("vector", lambda e: e.tensor_mul(
                statc[:, 7, :], statc[:, 7, :], statc[:, 6, :]),
                waits=[(PV, P.now(PV))], inc=PV)
            P.step("vector", lambda e: e.tensor_scalar(
                statc[:, 7, :], statc[:, 7, :], -0.5, 1.5, ALU.mult,
                ALU.add), waits=[(PV, P.now(PV))], inc=PV)
            P.step("vector", lambda e: e.tensor_mul(
                statc[:, 4, :], statc[:, 4, :], statc[:, 7, :]),
                waits=[(PV, P.now(PV))], inc=PV)
            P.step("vector", lambda e: e.tensor_scalar(
                statc[:, 5, :], statc[:, 4, :], 0.25, None, ALU.mult),
                waits=[(PV, P.now(PV))], inc=PV)
            rstd_all = P.now(PV)
            rstd_marks = [rstd_all] * 4

            featm = []
            for u in range(4):
                waits = ([(PS, casts_done), (PV, v_wkv),
                          (PV, v_nmu)] + bank_w(0, 1, 2, 3))
                def kvproj(e, u=u):
                    ins = None
                    for ci in range(8):
                        for ob in range(4):
                            ins = e.matmul(
                                PSB[ob][:, :],
                                x_bf[:, ci, u * SUB:(u + 1) * SUB],
                                w_inkv_bf[:, ci, ob * 512:(ob + 1) * 512],
                                start=(ci == 0), stop=False)
                    for ob in range(4):
                        ins = e.matmul(
                            PSB[ob][:, :],
                            negmu_row.ap()[0:1, u * SUB:(u + 1) * SUB],
                            wsum_in.ap()[0:1, D + ob * 512:D + (ob + 1) * 512],
                            start=False, stop=True)
                    return ins
                P.step("tensor", kvproj, waits=waits, inc=PT)
                t_proj = P.now(PT)
                pt_proj_last = t_proj

                ew = [(PT, t_proj), (PV, rstd_marks[u])]
                if u >= 2:
                    ew += [(PV, featm[u - 2][0]), (PS, featm[u - 2][1]),
                           (PG, featm[u - 2][2])]
                if tkv_guard is not None:
                    ew.append(tkv_guard)
                for ob in range(2):
                    P.step("vector", lambda e, ob=ob, u=u: e.tensor_scalar(
                        k2_t[:, u % 2, ob * 512:(ob + 1) * 512], PSB[ob][:, :],
                        statc.ap()[:, 5, u:u + 1], None, ALU.mult),
                        waits=ew, inc=PV)
                    P.step("vector", lambda e, ob=ob, u=u: e.tensor_scalar(
                        k2_t[:, u % 2, ob * 512:(ob + 1) * 512],
                        k2_t[:, u % 2, ob * 512:(ob + 1) * 512],
                        2.0, -2.0, ALU.min, ALU.max), inc=PV)
                for ob in range(2, 4):
                    P.step("vector", lambda e, ob=ob, u=u: e.tensor_scalar(
                        v_bf[:, u, (ob - 2) * 512:(ob - 1) * 512],
                        PSB[ob][:, :], statc.ap()[:, 4, u:u + 1], None, ALU.mult),
                        waits=ew, inc=PV)
                evac_v = P.now(PV)
                for b in range(4):
                    bank_r(b, PV)

                k2s = k2_t[:, u % 2, :]
                tkv = tk_bf.ap()[:, u, :, :, :]
                k2r = k2s.rearrange("p (h d) -> p h d", h=H)
                f2r = f_p2.ap().rearrange("p (h d) -> p h d", h=H)
                f3r = f_p3.ap().rearrange("p (h d) -> p h d", h=H)
                f4r = f_p4.ap().rearrange("p (h d) -> p h d", h=H)
                f5r = f_p5.ap().rearrange("p (h d) -> p h d", h=H)
                fw = [(PV, evac_v)]
                if u >= 1:
                    fw += [(PV, featm[u - 1][0]), (PS, featm[u - 1][1]),
                           (PG, featm[u - 1][2])]
                elif feat_guard is not None:
                    fw += list(feat_guard)
                P.step("scalar", lambda e, k2s=k2s: e.activation(
                    f_p2[:, :], k2s, ACT.Square, scale=0.5), waits=fw, inc=PS)
                s2 = P.now(PS)
                P.step("vector", lambda e, k2s=k2s: e.scalar_tensor_tensor(
                    f_p3[:, :], f_p2[:, :], -0.75, k2s, ALU.add, ALU.mult),
                    waits=[(PS, s2)] + fw, inc=PV)
                v3 = P.now(PV)
                P.step("scalar", lambda e: e.activation(
                    f_p4[:, :], f_p2[:, :], ACT.Square, bias=-0.5),
                    waits=[(PS, s2)] + fw, inc=PS)
                s4 = P.now(PS)
                P.step("vector", lambda e: e.tensor_mul(
                    f_p5[:, :], f_p2[:, :], f_p3[:, :]),
                    waits=[(PV, v3)] + fw, inc=PV)
                v5 = P.now(PV)

                tw = [] if tkv_guard is None else [tkv_guard]
                P.step("gpsimd", lambda e, tkv=tkv, k2r=k2r: e.tensor_copy(
                    tkv[:, :, 0, :], k2r), waits=[(PV, evac_v)] + tw, inc=PG)
                P.step("gpsimd", lambda e, tkv=tkv, f2r=f2r: e.tensor_copy(
                    tkv[:, :, 1, :], f2r), waits=[(PS, s2)] + tw, inc=PG)
                P.step("gpsimd", lambda e, tkv=tkv, f3r=f3r: e.tensor_copy(
                    tkv[:, :, 2, :], f3r), waits=[(PV, v3)] + tw, inc=PG)
                P.step("vector", lambda e, tkv=tkv, f4r=f4r: e.tensor_copy(
                    tkv[:, :, 3, :], f4r), waits=[(PS, s4)] + tw, inc=PV)
                P.step("vector", lambda e, tkv=tkv, f5r=f5r: e.tensor_copy(
                    tkv[:, :, 4, :], f5r), waits=[(PV, v5)] + tw, inc=PV)
                P.step("scalar", lambda e, tkv=tkv, f3r=f3r: e.activation(
                    tkv[:, :, 5, :], f3r, ACT.Square),
                    waits=[(PV, v3)] + tw, inc=PS)
                P.step("scalar", lambda e, tkv=tkv, f4r=f4r: e.activation(
                    tkv[:, :, 7, :], f4r, ACT.Square),
                    waits=[(PS, s4)] + tw, inc=PS)
                P.step("scalar", lambda e, tkv=tkv, f5r=f5r: e.activation(
                    tkv[:, :, 9, :], f5r, ACT.Square),
                    waits=[(PV, v5)] + tw, inc=PS)
                P.step("vector", lambda e, tkv=tkv, f3r=f3r, f4r=f4r:
                       e.tensor_tensor(tkv[:, :, 6, :], f3r, f4r, ALU.mult),
                       waits=[(PS, s4)] + tw, inc=PV)
                P.step("vector", lambda e, tkv=tkv, f4r=f4r, f5r=f5r:
                       e.tensor_tensor(tkv[:, :, 8, :], f4r, f5r, ALU.mult),
                       inc=PV)
                featm.append((P.now(PV), P.now(PS), P.now(PG)))
                if u == 3:
                    feat_guard = [(PV, P.now(PV)), (PS, P.now(PS)),
                                  (PG, P.now(PG))]

            def vsmm(e):
                ins = None
                for u in range(4):
                    for nb in range(2):
                        ins = e.matmul(PSB[6 + nb][:, :], ones_rhs[:, 0:128],
                                       v_bf[:, u, nb * 512:(nb + 1) * 512],
                                       start=(u == 0), stop=(u == 3))
                return ins
            P.step("tensor", vsmm,
                   waits=[(PV, featm[3][0])] + bank_w(6, 7), inc=PT)
            t_vs = P.now(PT)
            for nb in range(2):
                if t == 0:
                    fnv = (lambda e, nb=nb: e.tensor_copy(
                        vsum_acc[:, nb * 512:(nb + 1) * 512], PSB[6 + nb][:, :]))
                else:
                    fnv = (lambda e, nb=nb: e.tensor_tensor(
                        vsum_acc[:, nb * 512:(nb + 1) * 512], PSB[6 + nb][:, :],
                        vsum_acc[:, nb * 512:(nb + 1) * 512], ALU.add))
                P.step("vector", fnv, waits=[(PT, t_vs)], inc=PV)
            bank_r(6, PV)
            bank_r(7, PV)

            fw = [(PV, featm[3][0]), (PS, featm[3][1]), (PG, featm[3][2])]
            for h in range(H):
                b0 = 2 * (h % 3)
                b1 = b0 + 1
                half = h % 2
                row0 = half * 64
                def kvmm(e, h=h, b0=b0, b1=b1, row0=row0):
                    ins = None
                    for u in range(4):
                        lhs = v_bf[:, u, h * DH:(h + 1) * DH]
                        rhs = tk_bf.ap()[:, u, h, :, :].rearrange(
                            "p j d -> p (j d)")
                        ins = e.matmul(PSB[b0][row0:row0 + 64, :], lhs,
                                       rhs[:, 0:512], start=(u == 0),
                                       stop=(u == 3), tile_position=(0, row0))
                        ins = e.matmul(PSB[b1][row0:row0 + 64, 0:128], lhs,
                                       rhs[:, 512:640], start=(u == 0),
                                       stop=(u == 3), tile_position=(0, row0))
                    return ins
                P.step("tensor", kvmm, waits=fw + bank_w(b0, b1), inc=PT)
                t_kv = P.now(PT)
                accs = kvt_acc.ap()[row0:row0 + 64, h // 2, :]
                if t == 0:
                    f1 = (lambda e, b0=b0, row0=row0, accs=accs: e.tensor_copy(
                        accs[:, 0:512], PSB[b0][row0:row0 + 64, :]))
                    f2 = (lambda e, b1=b1, row0=row0, accs=accs: e.tensor_copy(
                        accs[:, 512:640], PSB[b1][row0:row0 + 64, 0:128]))
                else:
                    f1 = (lambda e, b0=b0, row0=row0, accs=accs: e.tensor_tensor(
                        accs[:, 0:512], PSB[b0][row0:row0 + 64, :],
                        accs[:, 0:512], ALU.add))
                    f2 = (lambda e, b1=b1, row0=row0, accs=accs: e.tensor_tensor(
                        accs[:, 512:640], PSB[b1][row0:row0 + 64, 0:128],
                        accs[:, 512:640], ALU.add))
                P.step("vector", f1, waits=[(PT, t_kv)], inc=PV)
                P.step("vector", f2, inc=PV)
                bank_r(b0, PV)
                bank_r(b1, PV)
            tkv_guard = (PT, P.now(PT))
            xbf_guard = (PT, pt_proj_last)

        p1_t, p1_v, p1_s, p1_g = P.now(PT), P.now(PV), P.now(PS), P.now(PG)

        w_q_done = gdma(lambda e: e.dma_start(
            out=w_inq_bf[:, :, :],
            in_=w_inT.ap()[:, 0:D].rearrange("(c p) f -> p c f", p=128)),
            waits=[(PT, p1_t)])
        w_o_done = gdma(lambda e: e.dma_start(
            out=w_outT_bf[:, :, :],
            in_=w_outT.ap().rearrange("(c p) f -> p c f", p=128)),
            waits=[(PT, p1_t)])

        def wsum_q(e):
            ins = None
            for ci in range(8):
                for nb in range(2):
                    ins = e.matmul(PSB[nb][:, :], ones_rhs[:, 0:128],
                                   w_inq_bf[:, ci, nb * 512:(nb + 1) * 512],
                                   start=(ci == 0), stop=(ci == 7))
            return ins
        P.step("tensor", wsum_q,
               waits=[(w_q_done[0], w_q_done[1])] + bank_w(0, 1), inc=PT)
        t_wq = P.now(PT)
        for nb in range(2):
            P.step("vector", lambda e, nb=nb: e.tensor_copy(
                wsum_in.ap()[0:1, nb * 512:(nb + 1) * 512], PSB[nb][0:1, :]),
                waits=[(PT, t_wq)], inc=PV)
            bank_r(nb, PV)
        v_wq = P.now(PV)

        P.step("vector", lambda e: e.memset(kvp_f[64:128, 5, :, :], 0.0),
               waits=[(PV, p1_v)], inc=PV)
        tr_b = 2
        for h in range(H):
            half = h % 2
            row0 = half * 64
            idn = iden_f32[row0:row0 + 64, row0:row0 + 64]
            P.step("vector", lambda e, h=h: e.tensor_copy(
                kvp_f[0:64, 0, h, :], vsum_acc.ap()[0:64, h * DH:(h + 1) * DH]),
                waits=[(PV, p1_v)], inc=PV)
            def tr0(e, h=h, row0=row0, idn=idn):
                return e.transpose(
                    PSB[tr_b][0:64, 0:64],
                    kvt_acc.ap()[row0:row0 + 64, h // 2, 0:64], idn)
            P.step("tensor", tr0, waits=[(PV, p1_v)] + bank_w(tr_b), inc=PT)
            ttr = P.now(PT)
            P.step("vector", lambda e, h=h: e.tensor_copy(
                kvp_f[64:128, 0, h, :], PSB[tr_b][0:64, 0:64]),
                waits=[(PT, ttr)], inc=PV)
            bank_r(tr_b, PV)
            for c in range(1, 6):
                jm0 = 2 * c - 1
                if c < 5:
                    def trc(e, h=h, row0=row0, jm0=jm0, idn=idn):
                        return e.transpose(
                            PSB[tr_b][:, 0:64],
                            kvt_acc.ap()[row0:row0 + 64, h // 2,
                                         jm0 * 64:(jm0 + 2) * 64], idn)
                    P.step("tensor", trc, waits=bank_w(tr_b), inc=PT)
                    ttr = P.now(PT)
                    P.step("vector", lambda e, h=h, c=c: e.tensor_copy(
                        kvp_f[:, c, h, :], PSB[tr_b][:, 0:64]),
                        waits=[(PT, ttr)], inc=PV)
                else:
                    def trc(e, h=h, row0=row0, jm0=jm0, idn=idn):
                        return e.transpose(
                            PSB[tr_b][0:64, 0:64],
                            kvt_acc.ap()[row0:row0 + 64, h // 2,
                                         jm0 * 64:(jm0 + 1) * 64], idn)
                    P.step("tensor", trc, waits=bank_w(tr_b), inc=PT)
                    ttr = P.now(PT)
                    P.step("vector", lambda e, h=h, c=c: e.tensor_copy(
                        kvp_f[0:64, c, h, :], PSB[tr_b][0:64, 0:64]),
                        waits=[(PT, ttr)], inc=PV)
                bank_r(tr_b, PV)
        v_kvp = P.now(PV)

        def emit_stage(EM, src_ap, write_evac, base_wait):
            for ci in range(6):
                cjs = [cj for cj in range(6)
                       if any(EM[2 * ci + a][2 * cj + b] != 0
                              for a in range(2) for b in range(2)
                              if 2 * ci + a < MJ and 2 * cj + b < MJ)]
                def btile(e, ci=ci, cjs=tuple(cjs)):
                    ins = e.memset(et32[:, :, :], 0.0)
                    e.drain()
                    for k, cj in enumerate(cjs):
                        for a in range(2):
                            for b in range(2):
                                mi, mj = 2 * ci + a, 2 * cj + b
                                if mi >= MJ or mj >= MJ:
                                    continue
                                val = float(EM[mi][mj])
                                if val == 0.0:
                                    continue
                                ins = e.tensor_scalar_mul(
                                    et32[b * 64:(b + 1) * 64, k,
                                         a * 64:(a + 1) * 64],
                                    iden_f32[b * 64:(b + 1) * 64,
                                             b * 64:(b + 1) * 64], val)
                    return ins
                P.step("vector", btile,
                       waits=[(PT, P.now(PT)), (PV, base_wait)], inc=PV)
                vb = P.now(PV)
                for h in range(H):
                    bk = h % 2
                    def emm(e, cjs=tuple(cjs), h=h, bk=bk):
                        ins = None
                        for k, cj in enumerate(cjs):
                            ins = e.matmul(
                                PSB[bk][:, 0:64], et32[:, k, :],
                                src_ap[:, cj, h, :],
                                start=(k == 0), stop=(k == len(cjs) - 1))
                        return ins
                    P.step("tensor", emm, waits=[(PV, vb)] + bank_w(bk), inc=PT)
                    write_evac(ci, h, P.now(PT), bk)
            return

        def evac_t1(ci, h, tmm, bk):
            P.step("vector", lambda e, ci=ci, h=h, bk=bk: e.tensor_scalar(
                t1_f[:, ci, h, :], PSB[bk][:, 0:64],
                beta_exp.ap()[:, ci * H + h:ci * H + h + 1], None, ALU.mult),
                waits=[(PT, tmm)], inc=PV)
            bank_r(bk, PV)

        emit_stage([[float(x) for x in row] for row in E_A64],
                   kvp_f.ap(), evac_t1, v_kvp)
        v_t1 = P.now(PV)

        def evac_kvp(ci, h, tmm, bk):
            half = h % 2
            for a in range(2):
                j = 2 * ci + a
                if j >= MJ:
                    continue
                P.step("vector",
                       lambda e, a=a, j=j, h=h, bk=bk, half=half: e.tensor_copy(
                           kvp_bf[half * 64:(half + 1) * 64, j, h // 2, :],
                           PSB[bk][a * 64:(a + 1) * 64, 0:64]),
                       waits=[(PT, tmm)], inc=PV)
            bank_r(bk, PV)

        emit_stage([[float(x) for x in row] for row in E_B64],
                   t1_f.ap(), evac_kvp, v_t1)
        v_kvpb = P.now(PV)

        x2_guard = [None, None]
        attn_guard = None
        featg = [None, None]
        y1_dma = []
        pending_y1 = [None]

        def emit_pending_y1():
            if pending_y1[0] is not None:
                tt_, vv_ = pending_y1[0]
                P.step("scalar", lambda e, t=tt_: e.dma_start(
                    out=y1d.ap()[:, t * T1:(t + 1) * T1]
                        .rearrange("(c p) s -> p c s", p=128),
                    in_=y1_t[:, :, :]), waits=[(PV, vv_)], inc=SST, inc_k=16)
                y1_dma.append(P.now(SST))
                pending_y1[0] = None

        for t in range(NT1):
            slot = t % 2
            waits = [(PV, v_kvpb), (PV, v_wq),
                     (w_o_done[0], w_o_done[1])]
            if x2_guard[slot] is not None:
                waits.append(x2_guard[slot])
            c0 = t * T1
            P.step("sync", lambda e, c0=c0, slot=slot: e.dma_start(
                out=x2_t[:, slot, :, :],
                in_=xT.ap()[:, c0:c0 + T1].rearrange("(c p) s -> p c s", p=128)),
                waits=waits, inc=SDQ[slot], inc_k=16)
            dv = P.now(SDQ[slot])
            cw = [(SDQ[slot], dv), (PT, P.now(PT))]
            P.step("scalar", lambda e, slot=slot: e.activation(
                xb2[:, :, :], x2_t[:, slot, :, :], ACT.Copy), waits=cw, inc=PS)
            P.step("scalar", lambda e, slot=slot: e.activation(
                xsq2[:, :, :], x2_t[:, slot, :, :], ACT.Square), inc=PS)
            s_cast = P.now(PS)
            emit_pending_y1()

            def p2sum(e):
                ins = None
                for ci in range(8):
                    ins = e.matmul(PSB[6][:, :], ones_rhs[:, 0:128], xb2[:, ci, :],
                                   start=(ci == 0), stop=(ci == 7))
                return ins
            def p2sq(e):
                ins = None
                for ci in range(8):
                    ins = e.matmul(PSB[7][:, :], ones_rhs[:, 0:128], xsq2[:, ci, :],
                                   start=(ci == 0), stop=(ci == 7))
                return ins
            P.step("tensor", p2sum, waits=[(PS, s_cast)] + bank_w(6), inc=PT)
            P.step("tensor", p2sq, waits=bank_w(7), inc=PT)
            t_st = P.now(PT)
            P.step("vector", lambda e: e.tensor_scalar(
                mu_rep[:, :], PSB[6][:, :], 1.0 / 1024.0, None, ALU.mult),
                waits=[(PT, t_st), (PS, P.now(PS))], inc=PV)
            P.step("vector", lambda e: e.tensor_scalar(
                negmu_row[0:1, :], PSB[6][0:1, :], -1.0 / 1024.0, None,
                ALU.mult), inc=PV)
            bank_r(6, PV)
            P.step("vector", lambda e: e.tensor_scalar(
                msq_rep[:, :], PSB[7][:, :], 1.0 / 1024.0, None, ALU.mult),
                inc=PV)
            bank_r(7, PV)
            v_nmu2 = P.now(PV)
            P.step("vector", lambda e: e.scalar_tensor_tensor(
                rstd_rep[:, :], mu_rep[:, :], -1.0, mu_rep[:, :],
                ALU.mult, ALU.mult), waits=[(PV, P.now(PV))], inc=PV)
            P.step("vector", lambda e: e.tensor_add(
                rstd_rep[:, :], rstd_rep[:, :], msq_rep[:, :]),
                waits=[(PV, P.now(PV))], inc=PV)
            P.step("vector", lambda e: e.tensor_scalar(
                msq_rep[:, :], rstd_rep[:, :], float(EPS), None, ALU.add),
                waits=[(PV, P.now(PV))], inc=PV)
            vv = P.now(PV)
            P.step("scalar", lambda e: e.activation(
                mu_rep[:, :], msq_rep[:, :], ACT.Sqrt), waits=[(PV, vv)],
                inc=PS)
            sv = P.now(PS)
            def recip2_d(e):
                e.reciprocal(rstd_rep[:, :], mu_rep[:, :])
                return e.drain()
            P.step("vector", recip2_d, waits=[(PS, sv)], inc=PV)
            P.step("vector", lambda e: e.tensor_mul(
                mu_rep[:, :], rstd_rep[:, :], rstd_rep[:, :]),
                waits=[(PV, P.now(PV))], inc=PV)
            P.step("vector", lambda e: e.tensor_mul(
                mu_rep[:, :], mu_rep[:, :], msq_rep[:, :]),
                waits=[(PV, P.now(PV))], inc=PV)
            P.step("vector", lambda e: e.tensor_scalar(
                mu_rep[:, :], mu_rep[:, :], -0.5, 1.5, ALU.mult, ALU.add),
                waits=[(PV, P.now(PV))], inc=PV)
            P.step("vector", lambda e: e.tensor_mul(
                rstd_rep[:, :], rstd_rep[:, :], mu_rep[:, :]),
                waits=[(PV, P.now(PV))], inc=PV)
            v_rstd = P.now(PV)

            for hp in range(8):
                bq = hp % 2
                par = hp % 2
                q2s = q2h.ap()[:, par, :]
                gp2 = g_p2.ap()[:, par, :]
                gp3 = g_p3.ap()[:, par, :]
                gp4 = g_p4.ap()[:, par, :]
                gp5 = g_p5.ap()[:, par, :]
                pfs = pf_bf.ap()[:, par, :, :]
                def qproj(e, hp=hp, bq=bq):
                    ins = None
                    for ci in range(8):
                        ins = e.matmul(
                            PSB[bq][:, :],
                            w_inq_bf[:, ci, hp * 128:(hp + 1) * 128],
                            xb2[:, ci, :], start=(ci == 0), stop=False)
                    ins = e.matmul(
                        PSB[bq][:, :],
                        wsum_in.ap()[0:1, hp * 128:(hp + 1) * 128],
                        negmu_row.ap()[0:1, :], start=False, stop=True)
                    return ins
                P.step("tensor", qproj,
                       waits=[(PS, s_cast), (PV, v_nmu2)] + bank_w(bq), inc=PT)
                t_q = P.now(PT)
                qw = [(PT, t_q), (PV, v_rstd)]
                if featg[par] is not None:
                    qw += [featg[par]]
                P.step("vector", lambda e, bq=bq, q2s=q2s: e.scalar_tensor_tensor(
                    q2s, PSB[bq][:, :], 0.25, rstd_rep[:, :],
                    ALU.mult, ALU.mult), waits=qw, inc=PV)
                P.step("vector", lambda e, q2s=q2s: e.tensor_scalar(
                    q2s, q2s, 2.0, -2.0, ALU.min, ALU.max),
                    waits=[(PV, P.now(PV))], inc=PV)
                v_q2 = P.now(PV)
                bank_r(bq, PV)

                P.step("scalar", lambda e, q2s=q2s, gp2=gp2: e.activation(
                    gp2, q2s, ACT.Square, scale=0.5),
                    waits=[(PV, v_q2)], inc=PS)
                s2 = P.now(PS)
                P.step("vector", lambda e, q2s=q2s, gp2=gp2, gp3=gp3:
                       e.scalar_tensor_tensor(
                           gp3, gp2, -0.75, q2s, ALU.add, ALU.mult),
                       waits=[(PS, s2)], inc=PV)
                v3 = P.now(PV)
                P.step("scalar", lambda e, gp2=gp2, gp4=gp4: e.activation(
                    gp4, gp2, ACT.Square, bias=-0.5),
                    waits=[(PS, s2)], inc=PS)
                s4 = P.now(PS)
                P.step("vector", lambda e, gp2=gp2, gp3=gp3, gp5=gp5:
                       e.tensor_mul(gp5, gp2, gp3),
                       waits=[(PV, v3)], inc=PV)
                v5 = P.now(PV)
                P.step("gpsimd", lambda e, pfs=pfs, q2s=q2s: e.tensor_copy(
                    pfs[:, 0, :], q2s), waits=[(PV, v_q2)], inc=PG)
                P.step("gpsimd", lambda e, pfs=pfs, gp2=gp2: e.tensor_copy(
                    pfs[:, 1, :], gp2), waits=[(PS, s2)], inc=PG)
                P.step("gpsimd", lambda e, pfs=pfs, gp3=gp3: e.tensor_copy(
                    pfs[:, 2, :], gp3), waits=[(PV, v3)], inc=PG)
                P.step("vector", lambda e, pfs=pfs, gp4=gp4: e.tensor_copy(
                    pfs[:, 3, :], gp4), waits=[(PS, s4)], inc=PV)
                P.step("vector", lambda e, pfs=pfs, gp5=gp5: e.tensor_copy(
                    pfs[:, 4, :], gp5), waits=[(PV, v5)], inc=PV)
                P.step("scalar", lambda e, pfs=pfs, gp3=gp3: e.activation(
                    pfs[:, 5, :], gp3, ACT.Square), waits=[(PV, v3)], inc=PS)
                P.step("scalar", lambda e, pfs=pfs, gp4=gp4: e.activation(
                    pfs[:, 7, :], gp4, ACT.Square), waits=[(PS, s4)], inc=PS)
                P.step("scalar", lambda e, pfs=pfs, gp5=gp5: e.activation(
                    pfs[:, 9, :], gp5, ACT.Square), waits=[(PV, v5)], inc=PS)
                P.step("vector", lambda e, pfs=pfs, gp3=gp3, gp4=gp4:
                       e.tensor_tensor(pfs[:, 6, :], gp3, gp4, ALU.mult),
                       waits=[(PS, s4)], inc=PV)
                P.step("vector", lambda e, pfs=pfs, gp4=gp4, gp5=gp5:
                       e.tensor_tensor(pfs[:, 8, :], gp4, gp5, ALU.mult),
                       inc=PV)
                fv, fs, fg = P.now(PV), P.now(PS), P.now(PG)

                be = 2 + 2 * (hp % 2)
                def einsum(e, hp=hp, be=be, pfs=pfs):
                    ins = None
                    for j in range(MJ):
                        for half in range(2):
                            h = 2 * hp + half
                            lhs = kvp_bf.ap()[half * 64:(half + 1) * 64,
                                              j, h // 2, :]
                            rhs = (ones_rhs[half * 64:(half + 1) * 64, :]
                                   if j == 0 else
                                   pfs[half * 64:(half + 1) * 64, j - 1, :])
                            ins = e.matmul(
                                PSB[be + half][half * 64:half * 64 + 64, :],
                                lhs, rhs, start=(j == 0), stop=(j == MJ - 1),
                                tile_position=(half * 64, half * 64))
                    return ins
                P.step("tensor", einsum,
                       waits=[(PV, fv), (PS, fs), (PG, fg)]
                       + bank_w(be, be + 1), inc=PT)
                t_e = P.now(PT)
                featg[par] = (PT, t_e)
                aw = [(PT, t_e)]
                if attn_guard is not None:
                    aw.append(attn_guard)
                P.step("scalar", lambda e, hp=hp, be=be: e.activation(
                    attn_bf.ap()[0:64, hp, :], PSB[be][0:64, :], ACT.Copy),
                    waits=aw, inc=PS)
                P.step("scalar", lambda e, hp=hp, be=be: e.activation(
                    attn_bf.ap()[64:128, hp, :], PSB[be + 1][64:128, :],
                    ACT.Copy), inc=PS)
                bank_r(be, PS)
                bank_r(be + 1, PS)
            s_attn = P.now(PS)

            for oc in range(8):
                bw_ = 6 + (oc % 2)
                def womm(e, oc=oc, bw_=bw_):
                    ins = None
                    for ac in range(8):
                        ins = e.matmul(
                            PSB[bw_][:, :],
                            w_outT_bf[:, ac, oc * 128:(oc + 1) * 128],
                            attn_bf[:, ac, :], start=(ac == 0), stop=(ac == 7))
                    return ins
                P.step("tensor", womm,
                       waits=[(PS, s_attn)] + bank_w(bw_), inc=PT)
                t_w = P.now(PT)
                yw = [(PT, t_w)]
                if t > 0:
                    yw.append((SST, y1_dma[-1]))
                P.step("vector",
                       lambda e, oc=oc, bw_=bw_, slot=slot: e.tensor_tensor(
                           y1_t[:, oc, :], PSB[bw_][:, :],
                           x2_t[:, slot, oc, :], ALU.add),
                       waits=yw, inc=PV)
                bank_r(bw_, PV)
            v_y1 = P.now(PV)
            x2_guard[slot] = (PV, v_y1)
            attn_guard = (PT, P.now(PT))
            pending_y1[0] = (t, v_y1)

        emit_pending_y1()
        p2_t, p2_v, p2_s, p2_g = P.now(PT), P.now(PV), P.now(PS), P.now(PG)

        wfc_done = gdma(lambda e: e.dma_start(
            out=W_fcT_bf[:, :, :],
            in_=W_fcT.ap().rearrange("(c p) f -> p c f", p=128)),
            waits=[(PT, p2_t), (PV, p2_v), (PS, p2_s), (PG, p2_g)])
        wpj_done = gdma(lambda e: e.dma_start(
            out=W_projT_bf[:, :, :],
            in_=W_projT.ap().rearrange("(c p) f -> p c f", p=128)),
            waits=[(PT, p2_t), (PV, p2_v)])

        def wsumfc(e):
            ins = None
            for nb in range(8):
                for ci in range(8):
                    ins = e.matmul(PSB[nb][:, :], ones_rhs[:, 0:128],
                                   W_fcT_bf[:, ci, nb * 512:(nb + 1) * 512],
                                   start=(ci == 0), stop=(ci == 7))
            return ins
        P.step("tensor", wsumfc,
               waits=[(wfc_done[0], wfc_done[1])] + bank_w(*range(8)), inc=PT)
        t_wfc = P.now(PT)
        for nb in range(8):
            P.step("vector", lambda e, nb=nb: e.tensor_copy(
                wsum_fc.ap()[0:1, nb * 512:(nb + 1) * 512], PSB[nb][0:1, :]),
                waits=[(PT, t_wfc)], inc=PV)
            bank_r(nb, PV)
        v_wfc = P.now(PV)

        y3_guard = [None, None]
        hsb_guard = None
        pt_fclast = [None]
        pending_out = [None]

        def emit_pending_out():
            if pending_out[0] is not None:
                tt_, slot_, vv_ = pending_out[0]
                P.step("scalar", lambda e, t=tt_, slot=slot_: e.dma_start(
                    out=outT.ap()[:, t * T3:(t + 1) * T3]
                        .rearrange("(c p) s -> p c s", p=128),
                    in_=y1_in[:, slot, :, :]), waits=[(PV, vv_)],
                    inc=SSO[slot_], inc_k=16)
                y3_guard[slot_] = (SSO[slot_], P.now(SSO[slot_]))
                pending_out[0] = None

        for t in range(NT3):
            slot = t % 2
            waits = [(PV, v_wfc), (wpj_done[0], wpj_done[1]),
                     (SST, y1_dma[t // 2])]
            if y3_guard[slot] is not None:
                waits.append(y3_guard[slot])
            c0 = t * T3
            P.step("sync", lambda e, c0=c0, slot=slot: e.dma_start(
                out=y1_in[:, slot, :, :],
                in_=y1d.ap()[:, c0:c0 + T3].rearrange("(c p) s -> p c s", p=128)),
                waits=waits, inc=SDY[slot], inc_k=16)
            dv = P.now(SDY[slot])
            cw = [(SDY[slot], dv), (PT, pt_fclast[0] if pt_fclast[0] is not None
                   else P.now(PT))]
            P.step("scalar", lambda e, slot=slot: e.activation(
                yb3[:, :, :], y1_in[:, slot, :, :], ACT.Copy), waits=cw, inc=PS)
            P.step("scalar", lambda e, slot=slot: e.activation(
                ysq3[:, :, :], y1_in[:, slot, :, :], ACT.Square), inc=PS)
            s_cast = P.now(PS)
            emit_pending_out()
            def p3sum(e):
                ins = None
                for ci in range(8):
                    ins = e.matmul(PSB[6][:, 0:T3], ones_rhs[:, 0:128],
                                   yb3[:, ci, :], start=(ci == 0), stop=(ci == 7))
                return ins
            def p3sq(e):
                ins = None
                for ci in range(8):
                    ins = e.matmul(PSB[7][:, 0:T3], ones_rhs[:, 0:128],
                                   ysq3[:, ci, :], start=(ci == 0), stop=(ci == 7))
                return ins
            P.step("tensor", p3sum, waits=[(PS, s_cast)] + bank_w(6), inc=PT)
            P.step("tensor", p3sq, waits=bank_w(7), inc=PT)
            t_st = P.now(PT)
            P.step("scalar", lambda e: e.activation(
                mu3_rep[:, :], PSB[6][:, 0:T3], ACT.Copy, scale=1.0 / 1024.0),
                waits=[(PT, t_st)], inc=PS)
            P.step("scalar", lambda e: e.activation(
                nmu3_row[0:1, :], PSB[6][0:1, 0:T3], ACT.Copy,
                scale=-1.0 / 1024.0), inc=PS)
            s_nmu3 = P.now(PS)
            bank_r(6, PS)
            P.step("scalar", lambda e: e.activation(
                msq3_rep[:, :], PSB[7][:, 0:T3], ACT.Copy, scale=1.0 / 1024.0),
                inc=PS)
            s_msq3 = P.now(PS)
            bank_r(7, PS)
            P.step("vector", lambda e: e.scalar_tensor_tensor(
                rsq_rep[:, :], mu3_rep[:, :], -1.0, mu3_rep[:, :],
                ALU.mult, ALU.mult),
                waits=[(PV, P.now(PV)), (PS, s_msq3)], inc=PV)
            P.step("vector", lambda e: e.tensor_add(
                rsq_rep[:, :], rsq_rep[:, :], msq3_rep[:, :]),
                waits=[(PV, P.now(PV))], inc=PV)
            P.step("vector", lambda e: e.tensor_scalar(
                msq3_rep[:, :], rsq_rep[:, :], float(EPS), None, ALU.add),
                waits=[(PV, P.now(PV))], inc=PV)
            def recip3_d(e):
                e.reciprocal(rsq_rep[:, :], msq3_rep[:, :])
                return e.drain()
            P.step("vector", recip3_d, waits=[(PV, P.now(PV))], inc=PV)
            v_rsq = P.now(PV)

            for half in range(2):
                h_marks = []
                for fc in range(16):
                    fci = half * 16 + fc
                    bk = fc % 2
                    def fcmm(e, fci=fci, bk=bk):
                        ins = None
                        for ci in range(8):
                            ins = e.matmul(
                                PSB[bk][:, 0:T3],
                                W_fcT_bf[:, ci, fci * 128:(fci + 1) * 128],
                                yb3[:, ci, :], start=(ci == 0), stop=False)
                        ins = e.matmul(
                            PSB[bk][:, 0:T3],
                            wsum_fc.ap()[0:1, fci * 128:(fci + 1) * 128],
                            nmu3_row.ap()[0:1, :], start=False, stop=True)
                        return ins
                    P.step("tensor", fcmm,
                           waits=[(PS, s_cast), (PS, s_nmu3)] + bank_w(bk),
                           inc=PT)
                    t_fc = P.now(PT)
                    pt_fclast[0] = t_fc
                    P.step("scalar", lambda e, bk=bk: e.activation(
                        a_sb[:, bk, :], PSB[bk][:, 0:T3], ACT.Relu),
                        waits=[(PT, t_fc)], inc=PS)
                    sa = P.now(PS)
                    bank_r(bk, PS)
                    hw = [(PS, sa)]
                    if hsb_guard is not None:
                        hw.append(hsb_guard)
                    P.step("scalar", lambda e, bk=bk, fc=fc: e.activation(
                        h_sb[:, fc, :], a_sb[:, bk, :], ACT.Square),
                        waits=hw, inc=PS)
                    h_marks.append(P.now(PS))
                v_h = h_marks[-1]
                for oc in range(8):
                    bk = 2 + (oc % 4)
                    def pjmm(e, oc=oc, bk=bk, half=half):
                        ins = None
                        for fc in range(16):
                            fci = half * 16 + fc
                            ins = e.matmul(
                                PSB[bk][:, 0:T3],
                                W_projT_bf[:, fci, oc * 128:(oc + 1) * 128],
                                h_sb[:, fc, :], start=(fc == 0), stop=(fc == 15))
                        return ins
                    P.step("tensor", pjmm,
                           waits=[(PS, v_h), (wpj_done[0], wpj_done[1])]
                           + bank_w(bk), inc=PT)
                    t_pj = P.now(PT)
                    if half == 0:
                        P.step("vector", lambda e, oc=oc, bk=bk: e.tensor_copy(
                            y2_sb[:, oc, :], PSB[bk][:, 0:T3]),
                            waits=[(PT, t_pj)], inc=PV)
                    else:
                        P.step("vector",
                               lambda e, oc=oc, bk=bk: e.tensor_tensor(
                                   y2_sb[:, oc, :], PSB[bk][:, 0:T3],
                                   y2_sb[:, oc, :], ALU.add),
                               waits=[(PT, t_pj)], inc=PV)
                    bank_r(bk, PV)
                hsb_guard = (PT, P.now(PT))
            for oc in range(8):
                P.step("vector", lambda e, oc=oc: e.tensor_tensor(
                    y2_sb[:, oc, :], y2_sb[:, oc, :], rsq_rep[:, :], ALU.mult),
                    waits=[(PV, v_rsq)], inc=PV)
            v_y2 = P.now(PV)

            y2f = y2_sb.ap().rearrange("p a b -> p (a b)")
            mf = m_sb.ap().rearrange("p a b -> p (a b)")
            sf = spk_sb.ap().rearrange("p a b -> p (a b)")
            y1f = y1_in.ap()[:, slot, :, :].rearrange("p a b -> p (a b)")
            P.step("vector", lambda e, y2f=y2f, sf=sf: e.tensor_single_scalar(
                sf, y2f, 1.0, ALU.is_gt), waits=[(PV, v_y2)], inc=PV)
            P.step("vector", lambda e, y1f=y1f, sf=sf: e.scalar_tensor_tensor(
                y1f, sf, 0.2, y1f, ALU.mult, ALU.add), inc=PV)
            P.step("vector", lambda e, y2f=y2f, mf=mf: e.scalar_tensor_tensor(
                mf, y2f, 0.9, y2f, ALU.mult, ALU.add), inc=PV)
            P.step("vector", lambda e, mf=mf, sf=sf: e.tensor_sub(
                mf, mf, sf), inc=PV)
            for stp in range(4):
                P.step("vector", lambda e, mf=mf, sf=sf: e.tensor_single_scalar(
                    sf, mf, 1.0, ALU.is_gt), inc=PV)
                P.step("vector",
                       lambda e, y1f=y1f, sf=sf: e.scalar_tensor_tensor(
                           y1f, sf, 0.2, y1f, ALU.mult, ALU.add), inc=PV)
                if stp < 3:
                    P.step("vector",
                           lambda e, mf=mf, y2f=y2f: e.scalar_tensor_tensor(
                               mf, mf, 0.9, y2f, ALU.mult, ALU.add), inc=PV)
                    P.step("vector", lambda e, mf=mf, sf=sf: e.tensor_sub(
                        mf, mf, sf), inc=PV)
            v_out = P.now(PV)

            pending_out[0] = (t, slot, v_out)

        emit_pending_out()
        end_sems = [SST] + SDX + SDQ + SDY + SSO
        P.step("sync", lambda e: e.nop(),
               waits=[(sname, P.now(sname)) for sname in end_sems])

        block = stk.enter_context(nc.Block())
        P.emit(block)
    return nc


_NC_CACHE = None


def kernel(L0_attn_x, L0_attn_w_in, L0_attn_w_out, L0_attn_beta,
           L0_ffn_W_fc, L0_ffn_W_proj):
    global _NC_CACHE
    x = np.asarray(L0_attn_x, dtype=np.float32)
    w_in = np.ascontiguousarray(np.asarray(L0_attn_w_in, np.float32).T)
    w_out = np.ascontiguousarray(np.asarray(L0_attn_w_out, np.float32).T)
    beta = np.ascontiguousarray(np.asarray(L0_attn_beta, np.float32).T)
    W_fc = np.ascontiguousarray(np.asarray(L0_ffn_W_fc, np.float32).T)
    W_proj = np.ascontiguousarray(np.asarray(L0_ffn_W_proj, np.float32).T)
    if _NC_CACHE is None:
        _NC_CACHE = build_nc()
    nc = _NC_CACHE
    in_maps = []
    for c in range(8):
        in_maps.append({
            "xT": np.ascontiguousarray(x[c].T),
            "w_inT": w_in, "w_outT": w_out, "betaT": beta,
            "W_fcT": W_fc, "W_projT": W_proj,
        })
    res = run_bass_kernel_spmd(nc, in_maps, core_ids=list(range(8)))
    out = np.empty((8, S, D), dtype=np.float32)
    for c in range(8):
        out[c] = res.results[c]["outT"].T
    return out


# version 06
# speedup vs baseline: 1.2263x; 1.1079x over previous
import contextlib
from fractions import Fraction

import numpy as np
import concourse.bass as bass
import concourse.mybir as mybir
from concourse.bass_utils import run_bass_kernel_spmd

F32 = mybir.dt.float32
BF16 = mybir.dt.bfloat16
ALU = mybir.AluOpType
ACT = mybir.ActivationFunctionType

D = 1024
S = 4096
H = 16
DH = 64
MJ = 11
NT1 = 8
T1 = 512
SUB = 128
NT3 = 16
T3 = 256
FD = 4096
EPS = 1e-5

ENGINES = ("tensor", "vector", "scalar", "gpsimd", "sync")


def _basis_matrices():
    def cmul(a, b):
        out = {}
        for i, ai in a.items():
            for j, bj in b.items():
                for k in (i + j, abs(i - j)):
                    out[k] = out.get(k, Fraction(0)) + ai * bj / 2
        return {k: v for k, v in out.items() if v != 0}

    def cadd(a, b, sa=Fraction(1), sb=Fraction(1)):
        out = {}
        for k, v in a.items():
            out[k] = out.get(k, Fraction(0)) + sa * v
        for k, v in b.items():
            out[k] = out.get(k, Fraction(0)) + sb * v
        return {k: v for k, v in out.items() if v != 0}

    T0, T1_ = {0: Fraction(1)}, {1: Fraction(1)}
    P = [None] * MJ
    P[0] = T0
    P[1] = cadd(T1_, {}, Fraction(2))
    P[2] = cmul(cadd(P[1], {}, Fraction(1, 2)), cadd(P[1], {}, Fraction(1, 2)))
    P[3] = cmul(cadd(P[2], T0, Fraction(1), Fraction(-3, 4)), P[1])
    P[4] = cmul(cadd(P[2], T0, Fraction(1), Fraction(-1, 2)),
                cadd(P[2], T0, Fraction(1), Fraction(-1, 2)))
    P[5] = cmul(P[2], P[3])
    P[6] = cmul(P[3], P[3])
    P[7] = cmul(P[3], P[4])
    P[8] = cmul(P[4], P[4])
    P[9] = cmul(P[4], P[5])
    P[10] = cmul(P[5], P[5])
    Mq = [[P[j].get(m, Fraction(0)) for m in range(MJ)] for j in range(MJ)]
    n = MJ
    Aug = [row[:] + [Fraction(int(i == j)) for j in range(n)]
           for i, row in enumerate(Mq)]
    for c in range(n):
        p = next(r for r in range(c, n) if Aug[r][c] != 0)
        Aug[c], Aug[p] = Aug[p], Aug[c]
        piv = Aug[c][c]
        Aug[c] = [v / piv for v in Aug[c]]
        for r in range(n):
            if r != c and Aug[r][c] != 0:
                f = Aug[r][c]
                Aug[r] = [vr - f * vc for vr, vc in zip(Aug[r], Aug[c])]
    Minv = [row[n:] for row in Aug]
    E_A = np.array([[float(Minv[j][m]) for m in range(MJ)] for j in range(MJ)],
                   dtype=np.float64)
    return E_A, E_A.T.copy()


E_A64, E_B64 = _basis_matrices()


class Prog:

    def __init__(self, nc, stk):
        self.nc = nc
        self.steps = {e: [] for e in ENGINES}
        self._sems = {}
        self._stk = stk
        self.counts = {}

    def sem(self, name):
        if name not in self._sems:
            self._sems[name] = self._stk.enter_context(self.nc.semaphore(name))
            self.counts[name] = 0
        return self._sems[name]

    def step(self, eng, fn, waits=(), inc=None, inc_k=1):
        w = []
        seen = {}
        for (s, v) in waits:
            if v is None:
                continue
            self.sem(s)
            if s not in seen or seen[s] < v:
                seen[s] = v
        w = [(self._sems[s], v) for s, v in seen.items()]
        if inc is not None:
            self.sem(inc)
            self.counts[inc] += inc_k
        self.steps[eng].append((fn, w, (self._sems[inc] if inc else None), inc_k))
        return self.counts[inc] if inc else None

    def now(self, sem_name):
        self.sem(sem_name)
        return self.counts[sem_name]

    def emit(self, block):
        for ename in ENGINES:
            steps = self.steps[ename]
            if not steps:
                continue

            def body(eng, steps=steps):
                for fn, waits, inc_sem, inc_k in steps:
                    for (sem, val) in waits:
                        eng.wait_ge(sem, val)
                    ins = fn(eng)
                    if inc_sem is not None:
                        assert ins is not None
                        ins.then_inc(inc_sem, inc_k)
            getattr(block, ename)(body)


def ceil32(x):
    return (x + 31) & ~31


def build_nc():
    nc = bass.Bass("TRN2", target_bir_lowering=False,
                   detect_race_conditions=False)

    xT = nc.dram_tensor("xT", [D, S], F32, kind="ExternalInput")
    w_inT = nc.dram_tensor("w_inT", [D, 3 * D], F32, kind="ExternalInput")
    w_outT = nc.dram_tensor("w_outT", [D, D], F32, kind="ExternalInput")
    betaT = nc.dram_tensor("betaT", [MJ, H], F32, kind="ExternalInput")
    W_fcT = nc.dram_tensor("W_fcT", [D, FD], F32, kind="ExternalInput")
    W_projT = nc.dram_tensor("W_projT", [FD, D], F32, kind="ExternalInput")
    outT = nc.dram_tensor("outT", [D, S], F32, kind="ExternalOutput")
    y1d = nc.dram_tensor("y1scratch", [D, S], F32, kind="Internal")

    EA32 = E_A64.astype(np.float32)
    assert np.array_equal(EA32.astype(np.float64), E_A64), "E_A not fp32 exact"

    with contextlib.ExitStack() as stk:
        P = Prog(nc, stk)
        BASE = 16512
        LIMIT = 212832
        assert BASE >= nc.sbuf_base and BASE + LIMIT <= nc.sbuf_top
        placed = []

        def at(name, off, shape, dtype):
            nbytes = int(np.prod(shape[1:])) * mybir.dt.size(dtype)
            assert off + nbytes <= LIMIT, f"{name}: {off}+{nbytes} > {LIMIT}"
            placed.append((name, off, off + nbytes))
            return nc.alloc_sbuf_tensor_at(name, list(shape), dtype,
                                           offset=BASE + off)

        w_inkv_bf = at("w_inkv_bf", 0, [128, 8, 2 * D], BF16)
        w_inq_bf = at("w_inq_bf", 0, [128, 8, D], BF16)
        w_outT_bf = at("w_outT_bf", 16384, [128, 8, D], BF16)
        W_fcT_bf = at("W_fcT_bf", 0, [128, 8, FD], BF16)
        o = 202272
        beta_exp = at("beta_exp", o, [128, 6 * H], F32); o += ceil32(6 * H * 4)
        iden_f32 = at("iden_f32", o, [128, 128], F32); o += 512
        iota_f32 = at("iota_f32", o, [128, 128], BF16); o += 256
        ones_rhs = at("ones_rhs", o, [128, 512], BF16); o += 1024
        ones_row = at("ones_row", o, [1, 64], F32); o += 256
        beta_row = at("beta_row", o, [1, MJ * H], F32); o += ceil32(MJ * H * 4)
        wsum_in = at("wsum_in", o, [1, 3 * D], BF16); o += ceil32(3 * D * 2)
        negmu_row = at("negmu_row", o, [1, T1], BF16); o += ceil32(T1 * 2)
        statc = at("statc", o, [128, 8, 4], F32); o += 4 * 8 * 4
        cpool = at("cpool", o, [128, 2], F32); o += 32
        C_END = o
        assert C_END <= LIMIT, f"C region: {C_END}"
        C0 = 202272

        E0 = 32768
        p = E0
        x_roll = at("x_roll", p, [128, 2, 8, SUB], F32); p += 2 * 8 * SUB * 4
        x_bf = at("x_bf", p, [128, 8, T1], BF16); p += 8 * T1 * 2
        x2_bf = at("x2_bf", p, [128, 8, T1], BF16); p += 8 * T1 * 2
        k2_t = at("k2_t", p, [128, 2, D], F32); p += 2 * D * 4
        f_p2 = at("f_p2", p, [128, D], F32); p += D * 4
        f_p3 = at("f_p3", p, [128, D], F32); p += D * 4
        f_p4 = at("f_p4", p, [128, D], F32); p += D * 4
        f_p5 = at("f_p5", p, [128, D], F32); p += D * 4
        v_bf = at("v_bf", p, [128, 4, D], BF16); p += 4 * D * 2
        tk_bf = at("tk_bf", p, [128, 4, H, 10, DH], BF16); p += 4 * H * 10 * DH * 2
        kvt_acc = at("kvt_acc", p, [128, 8, 10 * DH], F32); p += 8 * 10 * DH * 4
        vsum_acc = at("vsum_acc", p, [128, H * DH], F32); p += H * DH * 4
        P1_END = p
        assert P1_END <= C0, f"P1 over: {P1_END}"

        p = E0
        kvp_f = at("kvp_f", p, [128, 6, H, DH], F32); p += 6 * H * DH * 4
        t1_f = at("t1_f", p, [128, 6, H, DH], F32); p += 6 * H * DH * 4
        et32 = at("et32", p, [128, 6, 128], F32); p += 6 * 128 * 4
        TR_END = p
        tk_off = [x for x in placed if x[0] == "tk_bf"][0][1]
        assert TR_END <= tk_off, "TR scratch hits tk"

        p = E0
        x2_t = at("x2_t", p, [128, 2, 8, T1], F32); p += 2 * 8 * T1 * 4
        xb2 = at("xb2", p, [128, 8, T1], BF16); p += 8 * T1 * 2
        xsq2 = at("xsq2", p, [128, 8, T1], BF16); p += 8 * T1 * 2
        q2h = at("q2h", p, [128, 2, T1], F32); p += 2 * T1 * 4
        g_p2 = at("g_p2", p, [128, 2, T1], F32); p += 2 * T1 * 4
        g_p3 = at("g_p3", p, [128, 2, T1], F32); p += 2 * T1 * 4
        g_p4 = at("g_p4", p, [128, 2, T1], F32); p += 2 * T1 * 4
        g_p5 = at("g_p5", p, [128, 2, T1], F32); p += 2 * T1 * 4
        pf_bf = at("pf_bf", p, [128, 2, 10, T1], BF16); p += 2 * 10 * T1 * 2
        attn_bf = at("attn_bf", p, [128, 8, T1], BF16); p += 8 * T1 * 2
        y1_t = at("y1_t", p, [128, 8, T1], F32); p += 8 * T1 * 4
        rstd_rep = at("rstd_rep", p, [128, T1], F32); p += T1 * 4
        mu_rep = at("mu_rep", p, [128, T1], F32); p += T1 * 4
        msq_rep = at("msq_rep", p, [128, T1], F32); p += T1 * 4
        kvp_bf = at("kvp_bf", p, [128, MJ, 8, DH], BF16); p += MJ * 8 * DH * 2
        P2_END = p
        assert P2_END <= C0, f"P2 over: {P2_END}"
        kvpbf_off = [x for x in placed if x[0] == "kvp_bf"][0][1]
        assert kvpbf_off >= TR_END, "kvp_bf overlaps TR scratch"

        p = 65536
        W_projT_bf = at("W_projT_bf", p, [128, 32, D], BF16); p += 32 * D * 2
        y1_in = at("y1_in", p, [128, 2, 8, T3], F32); p += 2 * 8 * T3 * 4
        yb3 = at("yb3", p, [128, 8, T3], BF16); p += 8 * T3 * 2
        h_sb = at("h_sb", p, [128, 16, T3], BF16); p += 16 * T3 * 2
        ysq3 = at("ysq3", p, [128, 8, T3], BF16); p += 8 * T3 * 2
        y2_sb = at("y2_sb", p, [128, 8, T3], F32); p += 8 * T3 * 4
        m_sb = at("m_sb", p, [128, 8, T3], F32); p += 8 * T3 * 4
        spk_sb = at("spk_sb", p, [128, 8, T3], F32); p += 8 * T3 * 4
        a_sb = at("a_sb", p, [128, 2, T3], F32); p += 2 * T3 * 4
        rsq_rep = at("rsq_rep", p, [128, T3], F32); p += T3 * 4
        mu3_rep = at("mu3_rep", p, [128, T3], F32); p += T3 * 4
        msq3_rep = at("msq3_rep", p, [128, T3], F32); p += T3 * 4
        nmu3_row = at("nmu3_row", p, [1, T3], BF16); p += ceil32(T3 * 2)
        wsum_fc = at("wsum_fc", p, [1, FD], BF16); p += ceil32(FD * 2)
        P3_END = p
        assert P3_END <= C0, f"P3 over: {P3_END}"

        PSB = [nc.alloc_psum_tensor(f"psb{i}", [128, 512], F32) for i in range(8)]

        SD = "sdma"
        SDX = ["sdx0", "sdx1"]
        SDQ = ["sdq0", "sdq1"]
        SDY = ["sdy0", "sdy1"]
        SSO = ["sso0", "sso1"]
        SST = "sdst"
        GP = ["gdma0", "gdma1", "gdma2", "gdma3"]
        PT, PV, PS, PG = "pt", "pv", "ps", "pg"
        gp_i = [0]
        gp_pending = {}
        bank_guard = {}

        def gdma(fn, waits=()):
            s = GP[gp_i[0] % len(GP)]
            gp_i[0] += 1
            w = list(waits)
            if s in gp_pending:
                w.append((s, gp_pending[s]))
            P.step("gpsimd", fn, waits=w, inc=s, inc_k=16)
            gp_pending[s] = P.now(s)
            return (s, P.now(s))

        def bank_w(*banks):
            out = []
            for b in banks:
                out += list(bank_guard.get(b, {}).items())
            return out

        def bank_r(b, sem_name):
            bank_guard.setdefault(b, {})[sem_name] = P.now(sem_name)

        w_kv_done = gdma(lambda e: e.dma_start(
            out=w_inkv_bf[:, :, :],
            in_=w_inT.ap()[:, D:3 * D].rearrange("(c p) f -> p c f", p=128)))
        beta_done = gdma(lambda e: e.dma_start(
            out=beta_row[0:1, :],
            in_=betaT.ap().rearrange("m h -> (m h)")))

        P.step("gpsimd", lambda e: e.iota(
            iota_f32[:, :], [[1, 128]], channel_multiplier=-1,
            allow_small_or_imprecise_dtypes=True), inc=PG)
        g_iota = P.now(PG)
        P.step("vector", lambda e: e.tensor_single_scalar(
            iden_f32[:, :], iota_f32[:, :], 0.0, ALU.is_equal),
            waits=[(PG, g_iota)], inc=PV)
        P.step("vector", lambda e: e.memset(ones_rhs[:, :], 1.0), inc=PV)
        P.step("vector", lambda e: e.memset(ones_row[0:1, :], 1.0), inc=PV)
        P.step("vector", lambda e: e.memset(cpool[:, 0:1], float(EPS)), inc=PV)
        P.step("vector", lambda e: e.memset(cpool[:, 1:2], -0.5), inc=PV)
        nc.const_aps.register(F32, float(EPS), cpool.ap()[:, 0:1])
        nc.const_aps.register(F32, -0.5, cpool.ap()[:, 1:2])
        consts_v = P.now(PV)

        for c in range(6):
            for half in range(2):
                m = 2 * c + half
                dst = (c * H, (c + 1) * H)
                if m >= MJ:
                    P.step("vector", lambda e, half=half, dst=dst: e.memset(
                        beta_exp[half * 64:(half + 1) * 64, dst[0]:dst[1]], 0.0),
                        inc=PV)
                    continue
                P.step("tensor", lambda e, m=m: e.matmul(
                    PSB[0][0:64, 0:H], ones_row[0:1, :],
                    beta_row.ap()[0:1, m * H:(m + 1) * H],
                    start=True, stop=True),
                    waits=[(PV, consts_v), (beta_done[0], beta_done[1])]
                    + bank_w(0), inc=PT)
                tb = P.now(PT)
                P.step("vector", lambda e, half=half, dst=dst: e.tensor_copy(
                    beta_exp[half * 64:(half + 1) * 64, dst[0]:dst[1]],
                    PSB[0][0:64, 0:H]), waits=[(PT, tb)], inc=PV)
                bank_r(0, PV)
        bexp_v = P.now(PV)

        def wsum_kv(e):
            ins = None
            for ci in range(8):
                for nb in range(4):
                    ins = e.matmul(PSB[nb][:, :], ones_rhs[:, 0:128],
                                   w_inkv_bf[:, ci, nb * 512:(nb + 1) * 512],
                                   start=(ci == 0), stop=(ci == 7))
            return ins
        P.step("tensor", wsum_kv,
               waits=[(w_kv_done[0], w_kv_done[1]), (PV, consts_v)]
               + bank_w(0, 1, 2, 3), inc=PT)
        t_wkv = P.now(PT)
        for nb in range(4):
            P.step("vector", lambda e, nb=nb: e.tensor_copy(
                wsum_in.ap()[0:1, D + nb * 512:D + (nb + 1) * 512],
                PSB[nb][0:1, :]), waits=[(PT, t_wkv)], inc=PV)
            bank_r(nb, PV)
        v_wkv = P.now(PV)

        x_guard = [None, None]
        xbf_guard = None
        feat_guard = None
        tkv_guard = None

        for t in range(NT1):
            cast_marks = []
            for u in range(4):
                slot = u % 2
                waits = []
                if x_guard[slot] is not None:
                    waits.append(x_guard[slot])
                c0 = t * T1 + u * SUB
                P.step("sync", lambda e, c0=c0, slot=slot: e.dma_start(
                    out=x_roll[:, slot, :, :],
                    in_=xT.ap()[:, c0:c0 + SUB].rearrange("(c p) s -> p c s", p=128)),
                    waits=waits, inc=SDX[slot], inc_k=16)
                dv = P.now(SDX[slot])
                w2 = [(SDX[slot], dv)]
                if xbf_guard is not None:
                    w2.append(xbf_guard)
                P.step("scalar", lambda e, u=u, slot=slot: e.activation(
                    x_bf.ap()[:, :, u * SUB:(u + 1) * SUB],
                    x_roll[:, slot, :, :], ACT.Copy), waits=w2, inc=PS)
                P.step("scalar", lambda e, u=u, slot=slot: e.activation(
                    x2_bf.ap()[:, :, u * SUB:(u + 1) * SUB],
                    x_roll[:, slot, :, :], ACT.Square), inc=PS)
                x_guard[slot] = (PS, P.now(PS))
                cast_marks.append(P.now(PS))
            casts_done = cast_marks[-1]

            def fsum(e):
                ins = None
                for ci in range(8):
                    ins = e.matmul(PSB[4][:, :], ones_rhs[:, 0:128], x_bf[:, ci, :],
                                   start=(ci == 0), stop=(ci == 7))
                return ins
            P.step("tensor", fsum, waits=[(PS, casts_done)] + bank_w(4), inc=PT)
            t_sum = P.now(PT)
            P.step("vector", lambda e: e.tensor_scalar(
                negmu_row[0:1, :], PSB[4][0:1, :], -1.0 / 1024.0, None, ALU.mult),
                waits=[(PT, t_sum)], inc=PV)
            bank_r(4, PV)
            v_nmu = P.now(PV)

            for u in range(4):
                def tsum(e, u=u):
                    ins = None
                    for ci in range(8):
                        ins = e.matmul(PSB[5][:, 0:1],
                                       x_bf[:, ci, u * SUB:(u + 1) * SUB],
                                       ones_rhs[:, 0:1],
                                       start=(ci == 0), stop=(ci == 7))
                    return ins
                P.step("tensor", tsum,
                       waits=[(PS, cast_marks[u])] + bank_w(5), inc=PT)
                tv = P.now(PT)
                P.step("vector", lambda e, u=u: e.tensor_scalar(
                    statc[:, 0, u:u + 1], PSB[5][:, 0:1], 1.0 / 1024.0, None,
                    ALU.mult), waits=[(PT, tv)], inc=PV)
                bank_r(5, PV)
                def tsq(e, u=u):
                    ins = None
                    for ci in range(8):
                        ins = e.matmul(PSB[5][:, 0:1],
                                       x2_bf[:, ci, u * SUB:(u + 1) * SUB],
                                       ones_rhs[:, 0:1],
                                       start=(ci == 0), stop=(ci == 7))
                    return ins
                P.step("tensor", tsq, waits=bank_w(5), inc=PT)
                tv2 = P.now(PT)
                P.step("vector", lambda e, u=u: e.tensor_scalar(
                    statc[:, 1, u:u + 1], PSB[5][:, 0:1], 1.0 / 1024.0, None,
                    ALU.mult), waits=[(PT, tv2)], inc=PV)
                bank_r(5, PV)
            P.step("vector", lambda e: e.scalar_tensor_tensor(
                statc[:, 2, :], statc[:, 0, :], -1.0, statc[:, 0, :],
                ALU.mult, ALU.mult), waits=[(PV, P.now(PV))], inc=PV)
            P.step("vector", lambda e: e.tensor_add(
                statc[:, 2, :], statc[:, 2, :], statc[:, 1, :]),
                waits=[(PV, P.now(PV))], inc=PV)
            P.step("vector", lambda e: e.tensor_scalar(
                statc[:, 6, :], statc[:, 2, :], float(EPS), None, ALU.add),
                waits=[(PV, P.now(PV))], inc=PV)
            vva = P.now(PV)
            P.step("scalar", lambda e: e.activation(
                statc[:, 3, :], statc[:, 6, :], ACT.Sqrt),
                waits=[(PV, vva)], inc=PS)
            sva = P.now(PS)
            def recip_d(e):
                e.reciprocal(statc[:, 4, :], statc[:, 3, :])
                return e.drain()
            P.step("vector", recip_d, waits=[(PS, sva)], inc=PV)
            P.step("vector", lambda e: e.tensor_mul(
                statc[:, 7, :], statc[:, 4, :], statc[:, 4, :]),
                waits=[(PV, P.now(PV))], inc=PV)
            P.step("vector", lambda e: e.tensor_mul(
                statc[:, 7, :], statc[:, 7, :], statc[:, 6, :]),
                waits=[(PV, P.now(PV))], inc=PV)
            P.step("vector", lambda e: e.tensor_scalar(
                statc[:, 7, :], statc[:, 7, :], -0.5, 1.5, ALU.mult,
                ALU.add), waits=[(PV, P.now(PV))], inc=PV)
            P.step("vector", lambda e: e.tensor_mul(
                statc[:, 4, :], statc[:, 4, :], statc[:, 7, :]),
                waits=[(PV, P.now(PV))], inc=PV)
            P.step("vector", lambda e: e.tensor_scalar(
                statc[:, 5, :], statc[:, 4, :], 0.25, None, ALU.mult),
                waits=[(PV, P.now(PV))], inc=PV)
            rstd_all = P.now(PV)
            rstd_marks = [rstd_all] * 4

            featm = []
            for u in range(4):
                waits = ([(PS, casts_done), (PV, v_wkv),
                          (PV, v_nmu)] + bank_w(0, 1, 2, 3))
                def kvproj(e, u=u):
                    ins = None
                    for ci in range(8):
                        for ob in range(4):
                            ins = e.matmul(
                                PSB[ob][:, :],
                                x_bf[:, ci, u * SUB:(u + 1) * SUB],
                                w_inkv_bf[:, ci, ob * 512:(ob + 1) * 512],
                                start=(ci == 0), stop=False)
                    for ob in range(4):
                        ins = e.matmul(
                            PSB[ob][:, :],
                            negmu_row.ap()[0:1, u * SUB:(u + 1) * SUB],
                            wsum_in.ap()[0:1, D + ob * 512:D + (ob + 1) * 512],
                            start=False, stop=True)
                    return ins
                P.step("tensor", kvproj, waits=waits, inc=PT)
                t_proj = P.now(PT)
                pt_proj_last = t_proj

                ew = [(PT, t_proj), (PV, rstd_marks[u])]
                if u >= 2:
                    ew += [(PV, featm[u - 2][0]), (PS, featm[u - 2][1]),
                           (PG, featm[u - 2][2])]
                if tkv_guard is not None:
                    ew.append(tkv_guard)
                for ob in range(2):
                    P.step("vector", lambda e, ob=ob, u=u: e.tensor_scalar(
                        k2_t[:, u % 2, ob * 512:(ob + 1) * 512], PSB[ob][:, :],
                        statc.ap()[:, 5, u:u + 1], None, ALU.mult),
                        waits=ew, inc=PV)
                    P.step("vector", lambda e, ob=ob, u=u: e.tensor_scalar(
                        k2_t[:, u % 2, ob * 512:(ob + 1) * 512],
                        k2_t[:, u % 2, ob * 512:(ob + 1) * 512],
                        2.0, -2.0, ALU.min, ALU.max), inc=PV)
                for ob in range(2, 4):
                    P.step("vector", lambda e, ob=ob, u=u: e.tensor_scalar(
                        v_bf[:, u, (ob - 2) * 512:(ob - 1) * 512],
                        PSB[ob][:, :], statc.ap()[:, 4, u:u + 1], None, ALU.mult),
                        waits=ew, inc=PV)
                evac_v = P.now(PV)
                for b in range(4):
                    bank_r(b, PV)

                k2s = k2_t[:, u % 2, :]
                tkv = tk_bf.ap()[:, u, :, :, :]
                k2r = k2s.rearrange("p (h d) -> p h d", h=H)
                f2r = f_p2.ap().rearrange("p (h d) -> p h d", h=H)
                f3r = f_p3.ap().rearrange("p (h d) -> p h d", h=H)
                f4r = f_p4.ap().rearrange("p (h d) -> p h d", h=H)
                f5r = f_p5.ap().rearrange("p (h d) -> p h d", h=H)
                fw = [(PV, evac_v)]
                if u >= 1:
                    fw += [(PV, featm[u - 1][0]), (PS, featm[u - 1][1]),
                           (PG, featm[u - 1][2])]
                elif feat_guard is not None:
                    fw += list(feat_guard)
                P.step("scalar", lambda e, k2s=k2s: e.activation(
                    f_p2[:, :], k2s, ACT.Square, scale=0.5), waits=fw, inc=PS)
                s2 = P.now(PS)
                P.step("vector", lambda e, k2s=k2s: e.scalar_tensor_tensor(
                    f_p3[:, :], f_p2[:, :], -0.75, k2s, ALU.add, ALU.mult),
                    waits=[(PS, s2)] + fw, inc=PV)
                v3 = P.now(PV)
                P.step("scalar", lambda e: e.activation(
                    f_p4[:, :], f_p2[:, :], ACT.Square, bias=-0.5),
                    waits=[(PS, s2)] + fw, inc=PS)
                s4 = P.now(PS)
                P.step("vector", lambda e: e.tensor_mul(
                    f_p5[:, :], f_p2[:, :], f_p3[:, :]),
                    waits=[(PV, v3)] + fw, inc=PV)
                v5 = P.now(PV)

                tw = [] if tkv_guard is None else [tkv_guard]
                P.step("gpsimd", lambda e, tkv=tkv, k2r=k2r: e.tensor_copy(
                    tkv[:, :, 0, :], k2r), waits=[(PV, evac_v)] + tw, inc=PG)
                P.step("gpsimd", lambda e, tkv=tkv, f2r=f2r: e.tensor_copy(
                    tkv[:, :, 1, :], f2r), waits=[(PS, s2)] + tw, inc=PG)
                P.step("gpsimd", lambda e, tkv=tkv, f3r=f3r: e.tensor_copy(
                    tkv[:, :, 2, :], f3r), waits=[(PV, v3)] + tw, inc=PG)
                P.step("vector", lambda e, tkv=tkv, f4r=f4r: e.tensor_copy(
                    tkv[:, :, 3, :], f4r), waits=[(PS, s4)] + tw, inc=PV)
                P.step("vector", lambda e, tkv=tkv, f5r=f5r: e.tensor_copy(
                    tkv[:, :, 4, :], f5r), waits=[(PV, v5)] + tw, inc=PV)
                P.step("scalar", lambda e, tkv=tkv, f3r=f3r: e.activation(
                    tkv[:, :, 5, :], f3r, ACT.Square),
                    waits=[(PV, v3)] + tw, inc=PS)
                P.step("scalar", lambda e, tkv=tkv, f4r=f4r: e.activation(
                    tkv[:, :, 7, :], f4r, ACT.Square),
                    waits=[(PS, s4)] + tw, inc=PS)
                P.step("scalar", lambda e, tkv=tkv, f5r=f5r: e.activation(
                    tkv[:, :, 9, :], f5r, ACT.Square),
                    waits=[(PV, v5)] + tw, inc=PS)
                P.step("vector", lambda e, tkv=tkv, f3r=f3r, f4r=f4r:
                       e.tensor_tensor(tkv[:, :, 6, :], f3r, f4r, ALU.mult),
                       waits=[(PS, s4)] + tw, inc=PV)
                P.step("vector", lambda e, tkv=tkv, f4r=f4r, f5r=f5r:
                       e.tensor_tensor(tkv[:, :, 8, :], f4r, f5r, ALU.mult),
                       inc=PV)
                featm.append((P.now(PV), P.now(PS), P.now(PG)))
                if u == 3:
                    feat_guard = [(PV, P.now(PV)), (PS, P.now(PS)),
                                  (PG, P.now(PG))]

            def vsmm(e):
                ins = None
                for u in range(4):
                    for nb in range(2):
                        ins = e.matmul(PSB[6 + nb][:, :], ones_rhs[:, 0:128],
                                       v_bf[:, u, nb * 512:(nb + 1) * 512],
                                       start=(u == 0), stop=(u == 3))
                return ins
            P.step("tensor", vsmm,
                   waits=[(PV, featm[3][0])] + bank_w(6, 7), inc=PT)
            t_vs = P.now(PT)
            for nb in range(2):
                if t == 0:
                    fnv = (lambda e, nb=nb: e.tensor_copy(
                        vsum_acc[:, nb * 512:(nb + 1) * 512], PSB[6 + nb][:, :]))
                else:
                    fnv = (lambda e, nb=nb: e.tensor_tensor(
                        vsum_acc[:, nb * 512:(nb + 1) * 512], PSB[6 + nb][:, :],
                        vsum_acc[:, nb * 512:(nb + 1) * 512], ALU.add))
                P.step("vector", fnv, waits=[(PT, t_vs)], inc=PV)
            bank_r(6, PV)
            bank_r(7, PV)

            fw = [(PV, featm[3][0]), (PS, featm[3][1]), (PG, featm[3][2])]
            for h in range(H):
                b0 = 2 * (h % 3)
                b1 = b0 + 1
                half = h % 2
                row0 = half * 64
                def kvmm(e, h=h, b0=b0, b1=b1, row0=row0):
                    ins = None
                    for u in range(4):
                        lhs = v_bf[:, u, h * DH:(h + 1) * DH]
                        rhs = tk_bf.ap()[:, u, h, :, :].rearrange(
                            "p j d -> p (j d)")
                        ins = e.matmul(PSB[b0][row0:row0 + 64, :], lhs,
                                       rhs[:, 0:512], start=(u == 0),
                                       stop=(u == 3), tile_position=(0, row0))
                        ins = e.matmul(PSB[b1][row0:row0 + 64, 0:128], lhs,
                                       rhs[:, 512:640], start=(u == 0),
                                       stop=(u == 3), tile_position=(0, row0))
                    return ins
                P.step("tensor", kvmm, waits=fw + bank_w(b0, b1), inc=PT)
                t_kv = P.now(PT)
                accs = kvt_acc.ap()[row0:row0 + 64, h // 2, :]
                if t == 0:
                    f1 = (lambda e, b0=b0, row0=row0, accs=accs: e.tensor_copy(
                        accs[:, 0:512], PSB[b0][row0:row0 + 64, :]))
                    f2 = (lambda e, b1=b1, row0=row0, accs=accs: e.tensor_copy(
                        accs[:, 512:640], PSB[b1][row0:row0 + 64, 0:128]))
                else:
                    f1 = (lambda e, b0=b0, row0=row0, accs=accs: e.tensor_tensor(
                        accs[:, 0:512], PSB[b0][row0:row0 + 64, :],
                        accs[:, 0:512], ALU.add))
                    f2 = (lambda e, b1=b1, row0=row0, accs=accs: e.tensor_tensor(
                        accs[:, 512:640], PSB[b1][row0:row0 + 64, 0:128],
                        accs[:, 512:640], ALU.add))
                P.step("vector", f1, waits=[(PT, t_kv)], inc=PV)
                P.step("vector", f2, inc=PV)
                bank_r(b0, PV)
                bank_r(b1, PV)
            tkv_guard = (PT, P.now(PT))
            xbf_guard = (PT, pt_proj_last)

        p1_t, p1_v, p1_s, p1_g = P.now(PT), P.now(PV), P.now(PS), P.now(PG)

        w_q_done = gdma(lambda e: e.dma_start(
            out=w_inq_bf[:, :, :],
            in_=w_inT.ap()[:, 0:D].rearrange("(c p) f -> p c f", p=128)),
            waits=[(PT, p1_t)])
        w_o_done = gdma(lambda e: e.dma_start(
            out=w_outT_bf[:, :, :],
            in_=w_outT.ap().rearrange("(c p) f -> p c f", p=128)),
            waits=[(PT, p1_t)])

        def wsum_q(e):
            ins = None
            for ci in range(8):
                for nb in range(2):
                    ins = e.matmul(PSB[nb][:, :], ones_rhs[:, 0:128],
                                   w_inq_bf[:, ci, nb * 512:(nb + 1) * 512],
                                   start=(ci == 0), stop=(ci == 7))
            return ins
        P.step("tensor", wsum_q,
               waits=[(w_q_done[0], w_q_done[1])] + bank_w(0, 1), inc=PT)
        t_wq = P.now(PT)
        for nb in range(2):
            P.step("vector", lambda e, nb=nb: e.tensor_copy(
                wsum_in.ap()[0:1, nb * 512:(nb + 1) * 512], PSB[nb][0:1, :]),
                waits=[(PT, t_wq)], inc=PV)
            bank_r(nb, PV)
        v_wq = P.now(PV)

        P.step("vector", lambda e: e.memset(kvp_f[64:128, 5, :, :], 0.0),
               waits=[(PV, p1_v)], inc=PV)
        tr_b = 2
        for h in range(H):
            half = h % 2
            row0 = half * 64
            idn = iden_f32[row0:row0 + 64, row0:row0 + 64]
            P.step("vector", lambda e, h=h: e.tensor_copy(
                kvp_f[0:64, 0, h, :], vsum_acc.ap()[0:64, h * DH:(h + 1) * DH]),
                waits=[(PV, p1_v)], inc=PV)
            def tr0(e, h=h, row0=row0, idn=idn):
                return e.transpose(
                    PSB[tr_b][0:64, 0:64],
                    kvt_acc.ap()[row0:row0 + 64, h // 2, 0:64], idn)
            P.step("tensor", tr0, waits=[(PV, p1_v)] + bank_w(tr_b), inc=PT)
            ttr = P.now(PT)
            P.step("vector", lambda e, h=h: e.tensor_copy(
                kvp_f[64:128, 0, h, :], PSB[tr_b][0:64, 0:64]),
                waits=[(PT, ttr)], inc=PV)
            bank_r(tr_b, PV)
            for c in range(1, 6):
                jm0 = 2 * c - 1
                if c < 5:
                    def trc(e, h=h, row0=row0, jm0=jm0, idn=idn):
                        return e.transpose(
                            PSB[tr_b][:, 0:64],
                            kvt_acc.ap()[row0:row0 + 64, h // 2,
                                         jm0 * 64:(jm0 + 2) * 64], idn)
                    P.step("tensor", trc, waits=bank_w(tr_b), inc=PT)
                    ttr = P.now(PT)
                    P.step("vector", lambda e, h=h, c=c: e.tensor_copy(
                        kvp_f[:, c, h, :], PSB[tr_b][:, 0:64]),
                        waits=[(PT, ttr)], inc=PV)
                else:
                    def trc(e, h=h, row0=row0, jm0=jm0, idn=idn):
                        return e.transpose(
                            PSB[tr_b][0:64, 0:64],
                            kvt_acc.ap()[row0:row0 + 64, h // 2,
                                         jm0 * 64:(jm0 + 1) * 64], idn)
                    P.step("tensor", trc, waits=bank_w(tr_b), inc=PT)
                    ttr = P.now(PT)
                    P.step("vector", lambda e, h=h, c=c: e.tensor_copy(
                        kvp_f[0:64, c, h, :], PSB[tr_b][0:64, 0:64]),
                        waits=[(PT, ttr)], inc=PV)
                bank_r(tr_b, PV)
        v_kvp = P.now(PV)

        def emit_stage(EM, src_ap, write_evac, base_wait):
            for ci in range(6):
                cjs = [cj for cj in range(6)
                       if any(EM[2 * ci + a][2 * cj + b] != 0
                              for a in range(2) for b in range(2)
                              if 2 * ci + a < MJ and 2 * cj + b < MJ)]
                def btile(e, ci=ci, cjs=tuple(cjs)):
                    ins = e.memset(et32[:, :, :], 0.0)
                    e.drain()
                    for k, cj in enumerate(cjs):
                        for a in range(2):
                            for b in range(2):
                                mi, mj = 2 * ci + a, 2 * cj + b
                                if mi >= MJ or mj >= MJ:
                                    continue
                                val = float(EM[mi][mj])
                                if val == 0.0:
                                    continue
                                ins = e.tensor_scalar_mul(
                                    et32[b * 64:(b + 1) * 64, k,
                                         a * 64:(a + 1) * 64],
                                    iden_f32[b * 64:(b + 1) * 64,
                                             b * 64:(b + 1) * 64], val)
                    return ins
                P.step("vector", btile,
                       waits=[(PT, P.now(PT)), (PV, base_wait)], inc=PV)
                vb = P.now(PV)
                for h in range(H):
                    bk = h % 2
                    def emm(e, cjs=tuple(cjs), h=h, bk=bk):
                        ins = None
                        for k, cj in enumerate(cjs):
                            ins = e.matmul(
                                PSB[bk][:, 0:64], et32[:, k, :],
                                src_ap[:, cj, h, :],
                                start=(k == 0), stop=(k == len(cjs) - 1))
                        return ins
                    P.step("tensor", emm, waits=[(PV, vb)] + bank_w(bk), inc=PT)
                    write_evac(ci, h, P.now(PT), bk)
            return

        def evac_t1(ci, h, tmm, bk):
            P.step("vector", lambda e, ci=ci, h=h, bk=bk: e.tensor_scalar(
                t1_f[:, ci, h, :], PSB[bk][:, 0:64],
                beta_exp.ap()[:, ci * H + h:ci * H + h + 1], None, ALU.mult),
                waits=[(PT, tmm)], inc=PV)
            bank_r(bk, PV)

        emit_stage([[float(x) for x in row] for row in E_A64],
                   kvp_f.ap(), evac_t1, v_kvp)
        v_t1 = P.now(PV)

        def evac_kvp(ci, h, tmm, bk):
            half = h % 2
            for a in range(2):
                j = 2 * ci + a
                if j >= MJ:
                    continue
                P.step("vector",
                       lambda e, a=a, j=j, h=h, bk=bk, half=half: e.tensor_copy(
                           kvp_bf[half * 64:(half + 1) * 64, j, h // 2, :],
                           PSB[bk][a * 64:(a + 1) * 64, 0:64]),
                       waits=[(PT, tmm)], inc=PV)
            bank_r(bk, PV)

        emit_stage([[float(x) for x in row] for row in E_B64],
                   t1_f.ap(), evac_kvp, v_t1)
        v_kvpb = P.now(PV)

        x2_guard = [None, None]
        attn_guard = None
        featg = [None, None]
        y1_dma = []
        pending_y1 = [None]

        def emit_pending_y1():
            if pending_y1[0] is not None:
                tt_, vv_ = pending_y1[0]
                P.step("scalar", lambda e, t=tt_: e.dma_start(
                    out=y1d.ap()[:, t * T1:(t + 1) * T1]
                        .rearrange("(c p) s -> p c s", p=128),
                    in_=y1_t[:, :, :]), waits=[(PV, vv_)], inc=SST, inc_k=16)
                y1_dma.append(P.now(SST))
                pending_y1[0] = None

        for t in range(NT1):
            slot = t % 2
            waits = [(PV, v_kvpb), (PV, v_wq),
                     (w_o_done[0], w_o_done[1])]
            if x2_guard[slot] is not None:
                waits.append(x2_guard[slot])
            c0 = t * T1
            P.step("sync", lambda e, c0=c0, slot=slot: e.dma_start(
                out=x2_t[:, slot, :, :],
                in_=xT.ap()[:, c0:c0 + T1].rearrange("(c p) s -> p c s", p=128)),
                waits=waits, inc=SDQ[slot], inc_k=16)
            dv = P.now(SDQ[slot])
            cw = [(SDQ[slot], dv), (PT, P.now(PT))]
            P.step("scalar", lambda e, slot=slot: e.activation(
                xb2[:, :, :], x2_t[:, slot, :, :], ACT.Copy), waits=cw, inc=PS)
            P.step("scalar", lambda e, slot=slot: e.activation(
                xsq2[:, :, :], x2_t[:, slot, :, :], ACT.Square), inc=PS)
            s_cast = P.now(PS)
            emit_pending_y1()

            def p2sum(e):
                ins = None
                for ci in range(8):
                    ins = e.matmul(PSB[6][:, :], ones_rhs[:, 0:128], xb2[:, ci, :],
                                   start=(ci == 0), stop=(ci == 7))
                return ins
            def p2sq(e):
                ins = None
                for ci in range(8):
                    ins = e.matmul(PSB[7][:, :], ones_rhs[:, 0:128], xsq2[:, ci, :],
                                   start=(ci == 0), stop=(ci == 7))
                return ins
            P.step("tensor", p2sum, waits=[(PS, s_cast)] + bank_w(6), inc=PT)
            P.step("tensor", p2sq, waits=bank_w(7), inc=PT)
            t_st = P.now(PT)
            P.step("vector", lambda e: e.tensor_scalar(
                mu_rep[:, :], PSB[6][:, :], 1.0 / 1024.0, None, ALU.mult),
                waits=[(PT, t_st), (PS, P.now(PS))], inc=PV)
            P.step("vector", lambda e: e.tensor_scalar(
                negmu_row[0:1, :], PSB[6][0:1, :], -1.0 / 1024.0, None,
                ALU.mult), inc=PV)
            bank_r(6, PV)
            P.step("vector", lambda e: e.tensor_scalar(
                msq_rep[:, :], PSB[7][:, :], 1.0 / 1024.0, None, ALU.mult),
                inc=PV)
            bank_r(7, PV)
            v_nmu2 = P.now(PV)
            P.step("vector", lambda e: e.scalar_tensor_tensor(
                rstd_rep[:, :], mu_rep[:, :], -1.0, mu_rep[:, :],
                ALU.mult, ALU.mult), waits=[(PV, P.now(PV))], inc=PV)
            P.step("vector", lambda e: e.tensor_add(
                rstd_rep[:, :], rstd_rep[:, :], msq_rep[:, :]),
                waits=[(PV, P.now(PV))], inc=PV)
            P.step("vector", lambda e: e.tensor_scalar(
                msq_rep[:, :], rstd_rep[:, :], float(EPS), None, ALU.add),
                waits=[(PV, P.now(PV))], inc=PV)
            vv = P.now(PV)
            P.step("scalar", lambda e: e.activation(
                mu_rep[:, :], msq_rep[:, :], ACT.Sqrt), waits=[(PV, vv)],
                inc=PS)
            sv = P.now(PS)
            def recip2_d(e):
                e.reciprocal(rstd_rep[:, :], mu_rep[:, :])
                return e.drain()
            P.step("vector", recip2_d, waits=[(PS, sv)], inc=PV)
            P.step("vector", lambda e: e.tensor_mul(
                mu_rep[:, :], rstd_rep[:, :], rstd_rep[:, :]),
                waits=[(PV, P.now(PV))], inc=PV)
            P.step("vector", lambda e: e.tensor_mul(
                mu_rep[:, :], mu_rep[:, :], msq_rep[:, :]),
                waits=[(PV, P.now(PV))], inc=PV)
            P.step("vector", lambda e: e.tensor_scalar(
                mu_rep[:, :], mu_rep[:, :], -0.5, 1.5, ALU.mult, ALU.add),
                waits=[(PV, P.now(PV))], inc=PV)
            P.step("vector", lambda e: e.tensor_mul(
                rstd_rep[:, :], rstd_rep[:, :], mu_rep[:, :]),
                waits=[(PV, P.now(PV))], inc=PV)
            v_rstd = P.now(PV)

            pending_e = []

            def emit_einsum():
                hp, be, pfs, fv, fs, fg = pending_e.pop(0)
                def einsum(e, hp=hp, be=be, pfs=pfs):
                    ins = None
                    for j in range(MJ):
                        for half in range(2):
                            h = 2 * hp + half
                            lhs = kvp_bf.ap()[half * 64:(half + 1) * 64,
                                              j, h // 2, :]
                            rhs = (ones_rhs[half * 64:(half + 1) * 64, :]
                                   if j == 0 else
                                   pfs[half * 64:(half + 1) * 64, j - 1, :])
                            ins = e.matmul(
                                PSB[be + half][half * 64:half * 64 + 64, :],
                                lhs, rhs, start=(j == 0), stop=(j == MJ - 1),
                                tile_position=(half * 64, half * 64))
                    return ins
                P.step("tensor", einsum,
                       waits=[(PV, fv), (PS, fs), (PG, fg)]
                       + bank_w(be, be + 1), inc=PT)
                t_e = P.now(PT)
                featg[hp % 2] = (PT, t_e)
                aw = [(PT, t_e)]
                if attn_guard is not None:
                    aw.append(attn_guard)
                P.step("scalar", lambda e, hp=hp, be=be: e.activation(
                    attn_bf.ap()[0:64, hp, :], PSB[be][0:64, :], ACT.Copy),
                    waits=aw, inc=PS)
                P.step("scalar", lambda e, hp=hp, be=be: e.activation(
                    attn_bf.ap()[64:128, hp, :], PSB[be + 1][64:128, :],
                    ACT.Copy), inc=PS)
                bank_r(be, PS)
                bank_r(be + 1, PS)

            for hp in range(8):
                if len(pending_e) >= 2:
                    emit_einsum()
                bq = hp % 2
                par = hp % 2
                q2s = q2h.ap()[:, par, :]
                gp2 = g_p2.ap()[:, par, :]
                gp3 = g_p3.ap()[:, par, :]
                gp4 = g_p4.ap()[:, par, :]
                gp5 = g_p5.ap()[:, par, :]
                pfs = pf_bf.ap()[:, par, :, :]
                def qproj(e, hp=hp, bq=bq):
                    ins = None
                    for ci in range(8):
                        ins = e.matmul(
                            PSB[bq][:, :],
                            w_inq_bf[:, ci, hp * 128:(hp + 1) * 128],
                            xb2[:, ci, :], start=(ci == 0), stop=False)
                    ins = e.matmul(
                        PSB[bq][:, :],
                        wsum_in.ap()[0:1, hp * 128:(hp + 1) * 128],
                        negmu_row.ap()[0:1, :], start=False, stop=True)
                    return ins
                P.step("tensor", qproj,
                       waits=[(PS, s_cast), (PV, v_nmu2)] + bank_w(bq), inc=PT)
                t_q = P.now(PT)
                qw = [(PT, t_q), (PV, v_rstd)]
                if featg[par] is not None:
                    qw += [featg[par]]
                P.step("vector", lambda e, bq=bq, q2s=q2s: e.scalar_tensor_tensor(
                    q2s, PSB[bq][:, :], 0.25, rstd_rep[:, :],
                    ALU.mult, ALU.mult), waits=qw, inc=PV)
                P.step("vector", lambda e, q2s=q2s: e.tensor_scalar(
                    q2s, q2s, 2.0, -2.0, ALU.min, ALU.max),
                    waits=[(PV, P.now(PV))], inc=PV)
                v_q2 = P.now(PV)
                bank_r(bq, PV)

                P.step("scalar", lambda e, q2s=q2s, gp2=gp2: e.activation(
                    gp2, q2s, ACT.Square, scale=0.5),
                    waits=[(PV, v_q2)], inc=PS)
                s2 = P.now(PS)
                P.step("vector", lambda e, q2s=q2s, gp2=gp2, gp3=gp3:
                       e.scalar_tensor_tensor(
                           gp3, gp2, -0.75, q2s, ALU.add, ALU.mult),
                       waits=[(PS, s2)], inc=PV)
                v3 = P.now(PV)
                P.step("scalar", lambda e, gp2=gp2, gp4=gp4: e.activation(
                    gp4, gp2, ACT.Square, bias=-0.5),
                    waits=[(PS, s2)], inc=PS)
                s4 = P.now(PS)
                P.step("vector", lambda e, gp2=gp2, gp3=gp3, gp5=gp5:
                       e.tensor_mul(gp5, gp2, gp3),
                       waits=[(PV, v3)], inc=PV)
                v5 = P.now(PV)
                P.step("gpsimd", lambda e, pfs=pfs, q2s=q2s: e.tensor_copy(
                    pfs[:, 0, :], q2s), waits=[(PV, v_q2)], inc=PG)
                P.step("gpsimd", lambda e, pfs=pfs, gp2=gp2: e.tensor_copy(
                    pfs[:, 1, :], gp2), waits=[(PS, s2)], inc=PG)
                P.step("gpsimd", lambda e, pfs=pfs, gp3=gp3: e.tensor_copy(
                    pfs[:, 2, :], gp3), waits=[(PV, v3)], inc=PG)
                P.step("vector", lambda e, pfs=pfs, gp4=gp4: e.tensor_copy(
                    pfs[:, 3, :], gp4), waits=[(PS, s4)], inc=PV)
                P.step("vector", lambda e, pfs=pfs, gp5=gp5: e.tensor_copy(
                    pfs[:, 4, :], gp5), waits=[(PV, v5)], inc=PV)
                P.step("scalar", lambda e, pfs=pfs, gp3=gp3: e.activation(
                    pfs[:, 5, :], gp3, ACT.Square), waits=[(PV, v3)], inc=PS)
                P.step("scalar", lambda e, pfs=pfs, gp4=gp4: e.activation(
                    pfs[:, 7, :], gp4, ACT.Square), waits=[(PS, s4)], inc=PS)
                P.step("scalar", lambda e, pfs=pfs, gp5=gp5: e.activation(
                    pfs[:, 9, :], gp5, ACT.Square), waits=[(PV, v5)], inc=PS)
                P.step("vector", lambda e, pfs=pfs, gp3=gp3, gp4=gp4:
                       e.tensor_tensor(pfs[:, 6, :], gp3, gp4, ALU.mult),
                       waits=[(PS, s4)], inc=PV)
                P.step("vector", lambda e, pfs=pfs, gp4=gp4, gp5=gp5:
                       e.tensor_tensor(pfs[:, 8, :], gp4, gp5, ALU.mult),
                       inc=PV)
                pending_e.append((hp, 2 + 2 * (hp % 2), pfs,
                                  P.now(PV), P.now(PS), P.now(PG)))
            while pending_e:
                emit_einsum()
            s_attn = P.now(PS)

            for oc in range(8):
                bw_ = 6 + (oc % 2)
                def womm(e, oc=oc, bw_=bw_):
                    ins = None
                    for ac in range(8):
                        ins = e.matmul(
                            PSB[bw_][:, :],
                            w_outT_bf[:, ac, oc * 128:(oc + 1) * 128],
                            attn_bf[:, ac, :], start=(ac == 0), stop=(ac == 7))
                    return ins
                P.step("tensor", womm,
                       waits=[(PS, s_attn)] + bank_w(bw_), inc=PT)
                t_w = P.now(PT)
                yw = [(PT, t_w)]
                if t > 0:
                    yw.append((SST, y1_dma[-1]))
                P.step("vector",
                       lambda e, oc=oc, bw_=bw_, slot=slot: e.tensor_tensor(
                           y1_t[:, oc, :], PSB[bw_][:, :],
                           x2_t[:, slot, oc, :], ALU.add),
                       waits=yw, inc=PV)
                bank_r(bw_, PV)
            v_y1 = P.now(PV)
            x2_guard[slot] = (PV, v_y1)
            attn_guard = (PT, P.now(PT))
            pending_y1[0] = (t, v_y1)

        emit_pending_y1()
        p2_t, p2_v, p2_s, p2_g = P.now(PT), P.now(PV), P.now(PS), P.now(PG)

        wfc_done = gdma(lambda e: e.dma_start(
            out=W_fcT_bf[:, :, :],
            in_=W_fcT.ap().rearrange("(c p) f -> p c f", p=128)),
            waits=[(PT, p2_t), (PV, p2_v), (PS, p2_s), (PG, p2_g)])
        wpj_done = gdma(lambda e: e.dma_start(
            out=W_projT_bf[:, :, :],
            in_=W_projT.ap().rearrange("(c p) f -> p c f", p=128)),
            waits=[(PT, p2_t), (PV, p2_v)])

        def wsumfc(e):
            ins = None
            for nb in range(8):
                for ci in range(8):
                    ins = e.matmul(PSB[nb][:, :], ones_rhs[:, 0:128],
                                   W_fcT_bf[:, ci, nb * 512:(nb + 1) * 512],
                                   start=(ci == 0), stop=(ci == 7))
            return ins
        P.step("tensor", wsumfc,
               waits=[(wfc_done[0], wfc_done[1])] + bank_w(*range(8)), inc=PT)
        t_wfc = P.now(PT)
        for nb in range(8):
            P.step("vector", lambda e, nb=nb: e.tensor_copy(
                wsum_fc.ap()[0:1, nb * 512:(nb + 1) * 512], PSB[nb][0:1, :]),
                waits=[(PT, t_wfc)], inc=PV)
            bank_r(nb, PV)
        v_wfc = P.now(PV)

        y3_guard = [None, None]
        hsb_guard = None
        pt_fclast = [None]
        pending_out = [None]

        def emit_pending_out():
            if pending_out[0] is not None:
                tt_, slot_, vv_ = pending_out[0]
                P.step("scalar", lambda e, t=tt_, slot=slot_: e.dma_start(
                    out=outT.ap()[:, t * T3:(t + 1) * T3]
                        .rearrange("(c p) s -> p c s", p=128),
                    in_=y1_in[:, slot, :, :]), waits=[(PV, vv_)],
                    inc=SSO[slot_], inc_k=16)
                y3_guard[slot_] = (SSO[slot_], P.now(SSO[slot_]))
                pending_out[0] = None

        for t in range(NT3):
            slot = t % 2
            waits = [(PV, v_wfc), (wpj_done[0], wpj_done[1]),
                     (SST, y1_dma[t // 2])]
            if y3_guard[slot] is not None:
                waits.append(y3_guard[slot])
            c0 = t * T3
            P.step("sync", lambda e, c0=c0, slot=slot: e.dma_start(
                out=y1_in[:, slot, :, :],
                in_=y1d.ap()[:, c0:c0 + T3].rearrange("(c p) s -> p c s", p=128)),
                waits=waits, inc=SDY[slot], inc_k=16)
            dv = P.now(SDY[slot])
            cw = [(SDY[slot], dv), (PT, pt_fclast[0] if pt_fclast[0] is not None
                   else P.now(PT))]
            P.step("scalar", lambda e, slot=slot: e.activation(
                yb3[:, :, :], y1_in[:, slot, :, :], ACT.Copy), waits=cw, inc=PS)
            P.step("scalar", lambda e, slot=slot: e.activation(
                ysq3[:, :, :], y1_in[:, slot, :, :], ACT.Square), inc=PS)
            s_cast = P.now(PS)
            emit_pending_out()
            def p3sum(e):
                ins = None
                for ci in range(8):
                    ins = e.matmul(PSB[6][:, 0:T3], ones_rhs[:, 0:128],
                                   yb3[:, ci, :], start=(ci == 0), stop=(ci == 7))
                return ins
            def p3sq(e):
                ins = None
                for ci in range(8):
                    ins = e.matmul(PSB[7][:, 0:T3], ones_rhs[:, 0:128],
                                   ysq3[:, ci, :], start=(ci == 0), stop=(ci == 7))
                return ins
            P.step("tensor", p3sum, waits=[(PS, s_cast)] + bank_w(6), inc=PT)
            P.step("tensor", p3sq, waits=bank_w(7), inc=PT)
            t_st = P.now(PT)
            P.step("scalar", lambda e: e.activation(
                mu3_rep[:, :], PSB[6][:, 0:T3], ACT.Copy, scale=1.0 / 1024.0),
                waits=[(PT, t_st)], inc=PS)
            P.step("scalar", lambda e: e.activation(
                nmu3_row[0:1, :], PSB[6][0:1, 0:T3], ACT.Copy,
                scale=-1.0 / 1024.0), inc=PS)
            s_nmu3 = P.now(PS)
            bank_r(6, PS)
            P.step("scalar", lambda e: e.activation(
                msq3_rep[:, :], PSB[7][:, 0:T3], ACT.Copy, scale=1.0 / 1024.0),
                inc=PS)
            s_msq3 = P.now(PS)
            bank_r(7, PS)
            P.step("vector", lambda e: e.scalar_tensor_tensor(
                rsq_rep[:, :], mu3_rep[:, :], -1.0, mu3_rep[:, :],
                ALU.mult, ALU.mult),
                waits=[(PV, P.now(PV)), (PS, s_msq3)], inc=PV)
            P.step("vector", lambda e: e.tensor_add(
                rsq_rep[:, :], rsq_rep[:, :], msq3_rep[:, :]),
                waits=[(PV, P.now(PV))], inc=PV)
            P.step("vector", lambda e: e.tensor_scalar(
                msq3_rep[:, :], rsq_rep[:, :], float(EPS), None, ALU.add),
                waits=[(PV, P.now(PV))], inc=PV)
            def recip3_d(e):
                e.reciprocal(rsq_rep[:, :], msq3_rep[:, :])
                return e.drain()
            P.step("vector", recip3_d, waits=[(PV, P.now(PV))], inc=PV)
            v_rsq = P.now(PV)

            for half in range(2):
                h_marks = []
                for fc in range(16):
                    fci = half * 16 + fc
                    bk = fc % 2
                    def fcmm(e, fci=fci, bk=bk):
                        ins = None
                        for ci in range(8):
                            ins = e.matmul(
                                PSB[bk][:, 0:T3],
                                W_fcT_bf[:, ci, fci * 128:(fci + 1) * 128],
                                yb3[:, ci, :], start=(ci == 0), stop=False)
                        ins = e.matmul(
                            PSB[bk][:, 0:T3],
                            wsum_fc.ap()[0:1, fci * 128:(fci + 1) * 128],
                            nmu3_row.ap()[0:1, :], start=False, stop=True)
                        return ins
                    P.step("tensor", fcmm,
                           waits=[(PS, s_cast), (PS, s_nmu3)] + bank_w(bk),
                           inc=PT)
                    t_fc = P.now(PT)
                    pt_fclast[0] = t_fc
                    P.step("scalar", lambda e, bk=bk: e.activation(
                        a_sb[:, bk, :], PSB[bk][:, 0:T3], ACT.Relu),
                        waits=[(PT, t_fc)], inc=PS)
                    sa = P.now(PS)
                    bank_r(bk, PS)
                    hw = [(PS, sa)]
                    if hsb_guard is not None:
                        hw.append(hsb_guard)
                    P.step("scalar", lambda e, bk=bk, fc=fc: e.activation(
                        h_sb[:, fc, :], a_sb[:, bk, :], ACT.Square),
                        waits=hw, inc=PS)
                    h_marks.append(P.now(PS))
                v_h = h_marks[-1]
                for oc in range(8):
                    bk = 2 + (oc % 4)
                    def pjmm(e, oc=oc, bk=bk, half=half):
                        ins = None
                        for fc in range(16):
                            fci = half * 16 + fc
                            ins = e.matmul(
                                PSB[bk][:, 0:T3],
                                W_projT_bf[:, fci, oc * 128:(oc + 1) * 128],
                                h_sb[:, fc, :], start=(fc == 0), stop=(fc == 15))
                        return ins
                    P.step("tensor", pjmm,
                           waits=[(PS, v_h), (wpj_done[0], wpj_done[1])]
                           + bank_w(bk), inc=PT)
                    t_pj = P.now(PT)
                    if half == 0:
                        P.step("vector", lambda e, oc=oc, bk=bk: e.tensor_copy(
                            y2_sb[:, oc, :], PSB[bk][:, 0:T3]),
                            waits=[(PT, t_pj)], inc=PV)
                    else:
                        P.step("vector",
                               lambda e, oc=oc, bk=bk: e.tensor_tensor(
                                   y2_sb[:, oc, :], PSB[bk][:, 0:T3],
                                   y2_sb[:, oc, :], ALU.add),
                               waits=[(PT, t_pj)], inc=PV)
                    bank_r(bk, PV)
                hsb_guard = (PT, P.now(PT))
            for oc in range(8):
                P.step("vector", lambda e, oc=oc: e.tensor_tensor(
                    y2_sb[:, oc, :], y2_sb[:, oc, :], rsq_rep[:, :], ALU.mult),
                    waits=[(PV, v_rsq)], inc=PV)
            v_y2 = P.now(PV)

            y2f = y2_sb.ap().rearrange("p a b -> p (a b)")
            mf = m_sb.ap().rearrange("p a b -> p (a b)")
            sf = spk_sb.ap().rearrange("p a b -> p (a b)")
            y1f = y1_in.ap()[:, slot, :, :].rearrange("p a b -> p (a b)")
            P.step("vector", lambda e, y2f=y2f, sf=sf: e.tensor_single_scalar(
                sf, y2f, 1.0, ALU.is_gt), waits=[(PV, v_y2)], inc=PV)
            P.step("vector", lambda e, y1f=y1f, sf=sf: e.scalar_tensor_tensor(
                y1f, sf, 0.2, y1f, ALU.mult, ALU.add), inc=PV)
            P.step("vector", lambda e, y2f=y2f, mf=mf: e.scalar_tensor_tensor(
                mf, y2f, 0.9, y2f, ALU.mult, ALU.add), inc=PV)
            P.step("vector", lambda e, mf=mf, sf=sf: e.tensor_sub(
                mf, mf, sf), inc=PV)
            for stp in range(4):
                P.step("vector", lambda e, mf=mf, sf=sf: e.tensor_single_scalar(
                    sf, mf, 1.0, ALU.is_gt), inc=PV)
                P.step("vector",
                       lambda e, y1f=y1f, sf=sf: e.scalar_tensor_tensor(
                           y1f, sf, 0.2, y1f, ALU.mult, ALU.add), inc=PV)
                if stp < 3:
                    P.step("vector",
                           lambda e, mf=mf, y2f=y2f: e.scalar_tensor_tensor(
                               mf, mf, 0.9, y2f, ALU.mult, ALU.add), inc=PV)
                    P.step("vector", lambda e, mf=mf, sf=sf: e.tensor_sub(
                        mf, mf, sf), inc=PV)
            v_out = P.now(PV)

            pending_out[0] = (t, slot, v_out)

        emit_pending_out()
        end_sems = [SST] + SDX + SDQ + SDY + SSO
        P.step("sync", lambda e: e.nop(),
               waits=[(sname, P.now(sname)) for sname in end_sems])

        block = stk.enter_context(nc.Block())
        P.emit(block)
    return nc


_NC_CACHE = None


def kernel(L0_attn_x, L0_attn_w_in, L0_attn_w_out, L0_attn_beta,
           L0_ffn_W_fc, L0_ffn_W_proj):
    global _NC_CACHE
    x = np.asarray(L0_attn_x, dtype=np.float32)
    w_in = np.ascontiguousarray(np.asarray(L0_attn_w_in, np.float32).T)
    w_out = np.ascontiguousarray(np.asarray(L0_attn_w_out, np.float32).T)
    beta = np.ascontiguousarray(np.asarray(L0_attn_beta, np.float32).T)
    W_fc = np.ascontiguousarray(np.asarray(L0_ffn_W_fc, np.float32).T)
    W_proj = np.ascontiguousarray(np.asarray(L0_ffn_W_proj, np.float32).T)
    if _NC_CACHE is None:
        _NC_CACHE = build_nc()
    nc = _NC_CACHE
    in_maps = []
    for c in range(8):
        in_maps.append({
            "xT": np.ascontiguousarray(x[c].T),
            "w_inT": w_in, "w_outT": w_out, "betaT": beta,
            "W_fcT": W_fc, "W_projT": W_proj,
        })
    res = run_bass_kernel_spmd(nc, in_maps, core_ids=list(range(8)))
    out = np.empty((8, S, D), dtype=np.float32)
    for c in range(8):
        out[c] = res.results[c]["outT"].T
    return out
